# version 07
# baseline (speedup 1.0000x reference)
import math
from contextlib import ExitStack

import numpy as np
import ml_dtypes

import concourse.bass as bass
import concourse.mybir as mybir
from concourse.bass_utils import run_bass_kernel_spmd

F32 = mybir.dt.float32
BF16 = mybir.dt.bfloat16
AF = mybir.ActivationFunctionType
ALU = mybir.AluOpType
BF = ml_dtypes.bfloat16

NCORES = 8
T = 2048
DM = 1024
DMC = DM // 128
L_SEQ = 4096
D_STATE = 64
D_HEAD = 64
N_HEADS = 32
D_INNER = 2048
XBC = D_INNER + 2 * D_STATE
PROJ = 2 * D_INNER + 2 * D_STATE + N_HEADS
D_CONV = 4
EPS = 1e-5
CHUNK = 128
NCHUNK = T // CHUNK
AH = 16
AD = 64
NEXP = 8
LNB = 256


def _alibi_slopes(n):
    start = 2.0 ** (-2.0 ** (-(np.log2(n) - 3)))
    return np.array([start * start ** i for i in range(n)], dtype=np.float32)

SLOPES = _alibi_slopes(AH)


ENGS = ("sync", "tensor", "vector", "scalar", "gpsimd")


class RInstr:
    __slots__ = ("incs",)

    def __init__(self):
        self.incs = []

    def then_inc(self, sem, k=1):
        self.incs.append((sem, k))
        sem.n += k
        return self


class Sem:

    def __init__(self, h, name):
        self.h = h
        self.name = name
        self.n = 0


class RQueue:
    def __init__(self, name):
        self.name = name
        self.ops = []

    def __getattr__(self, m):
        if m.startswith("_"):
            raise AttributeError(m)

        def f(*a, **kw):
            r = RInstr()
            self.ops.append((m, a, kw, r))
            return r

        return f

    def replay(self, eng):
        for m, a, kw, r in self.ops:
            ins = getattr(eng, m)(*a, **kw)
            for s, k in r.incs:
                ins.then_inc(s.h, k)


class Prog:

    def __init__(self, nc, es):
        self.nc = nc
        self.es = es
        self.q = {e: RQueue(e) for e in ENGS}
        self.sem = {}
        for name in ("T", "V", "S", "P", "L", "G", "C"):
            self.sem[name] = Sem(es.enter_context(nc.semaphore("sem_" + name)), name)

    @property
    def t(self):
        return self.q["tensor"]

    @property
    def v(self):
        return self.q["vector"]

    @property
    def s(self):
        return self.q["scalar"]

    @property
    def g(self):
        return self.q["gpsimd"]

    @property
    def y(self):
        return self.q["sync"]

    def wait(self, qname, tok):
        if tok is None:
            return
        sem, val = tok
        assert val <= sem.n, f"wait on future value {val} > {sem.n} of {sem.name}"
        if val <= 0:
            return
        self.q[qname].wait_ge(sem.h, val)

    def waits(self, qname, toks):
        best = {}
        for tk in toks:
            if tk is None:
                continue
            s, v = tk
            if v > best.get(id(s), (s, -1))[1]:
                best[id(s)] = (s, v)
        for s, v in best.values():
            self.wait(qname, (s, v))

    def tok(self, name):
        return (self.sem[name], self.sem[name].n)

    def dma(self, qname, out, in_, **kw):
        sem = self.sem["L"] if qname == "sync" else self.sem["G"]
        r = self.q[qname].dma_start(out=out, in_=in_, **kw)
        r.then_inc(sem, 16)
        return (sem, sem.n)

    def mm(self, out, lhsT, rhs, start=True, stop=True, inc=True):
        r = self.t.matmul(out, lhsT, rhs, start=start, stop=stop)
        if inc:
            r.then_inc(self.sem["T"], 1)
        return (self.sem["T"], self.sem["T"].n)

    def trfence(self, scratch_psum, ones):
        return self.mm(scratch_psum[0:1, 0:1], ones[0:1, 0:1], ones[0:1, 0:1])

    def tr(self, out, in_, ident):
        r = self.t.transpose(out, in_, ident)
        r.then_inc(self.sem["T"], 1)
        return (self.sem["T"], self.sem["T"].n)

    def vop(self, method, *a, **kw):
        r = getattr(self.v, method)(*a, **kw)
        r.then_inc(self.sem["V"], 1)
        return (self.sem["V"], self.sem["V"].n)

    def sop(self, method, *a, **kw):
        r = getattr(self.s, method)(*a, **kw)
        r.then_inc(self.sem["S"], 1)
        return (self.sem["S"], self.sem["S"].n)

    def gop(self, method, *a, **kw):
        r = getattr(self.g, method)(*a, **kw)
        r.then_inc(self.sem["P"], 1)
        return (self.sem["P"], self.sem["P"].n)

    def collective(self, kind, op, groups, ins, outs):
        r = self.g.collective_compute(kind, op, replica_groups=groups,
                                      ins=ins, outs=outs)
        r.then_inc(self.sem["C"], 1)
        return (self.sem["C"], self.sem["C"].n)

    def vfence(self):
        self.wait("vector", (self.sem["V"], self.sem["V"].n))

    def barrier(self):
        toks = [(s, s.n) for s in self.sem.values() if s.n > 0]
        for qname in ENGS:
            self.waits(qname, toks)

    def finalize(self):
        with self.nc.Block() as block:
            for name in ENGS:
                q = self.q[name]
                if not q.ops:
                    continue
                getattr(block, name)(q.replay)


_CONST_CACHE = None


def host_constants():
    global _CONST_CACHE
    if _CONST_CACHE is not None:
        return _CONST_CACHE
    c = {}
    c["c_id128f"] = np.eye(128, dtype=np.float32)
    c["c_id128b"] = np.eye(128, dtype=BF)
    c["c_ones_1x128b"] = np.ones((1, 128), dtype=BF)
    c["c_onescol_b"] = np.ones((128, 1), dtype=BF)
    c["c_ones_1x128f"] = np.ones((1, 128), dtype=np.float32)
    c["c_onescol_f"] = np.ones((128, 1), dtype=np.float32)
    o2 = np.zeros((128, 2), dtype=BF)
    o2[0:64, 0] = 1
    o2[64:128, 1] = 1
    c["c_ones2"] = o2
    sel = np.zeros((32, 2048), dtype=BF)
    for h in range(32):
        sel[h, h * 64:(h + 1) * 64] = 1
    c["c_selhd"] = sel
    c["c_lt01"] = np.triu(np.ones((128, 128), dtype=np.float32))
    c["c_ones32x64b"] = np.ones((32, 64), dtype=BF)
    s32 = np.zeros((32, 32 * 128), dtype=BF)
    for h in range(32):
        s32[h, h * 128:(h + 1) * 128] = 1
    c["c_sel32"] = s32
    oh = np.zeros((8, 8 * 128), dtype=BF)
    for e in range(8):
        oh[e, e * 128:(e + 1) * 128] = 1
    c["c_oneh8"] = oh
    c["c_postk"] = (np.arange(32)[None, :] * 128 +
                    np.arange(128)[:, None]).astype(np.float32)
    p = np.arange(128)[:, None]
    b = np.arange(512)[None, :]
    dabs = [np.abs(p - b + d0 * 128).astype(np.float32) for d0 in range(4)]
    c["c_dabs"] = np.concatenate(dabs, axis=1)
    c["c_ones32x128f"] = np.ones((32, 128), dtype=np.float32)
    _CONST_CACHE = c
    return c


def core_constants(core):
    qoff = (core % 2) * T
    cc = {}
    posq = (qoff + np.arange(T)).astype(np.float64)
    qrows = np.zeros((64, T), dtype=BF)
    for h in range(AH):
        a = SLOPES[h].astype(np.float64) * posq
        hi = a.astype(BF)
        lo = (a - hi.astype(np.float64)).astype(BF)
        qrows[4 * h + 0] = hi
        qrows[4 * h + 1] = lo
        nhi = (-a).astype(BF)
        nlo = (-a - nhi.astype(np.float64)).astype(BF)
        qrows[4 * h + 2] = nhi
        qrows[4 * h + 3] = nlo
    cc["c_qrows"] = qrows
    cc["c_mask64"] = np.full((64, 1), 1.0 if core % 2 == 1 else 0.0, np.float32)
    cc["c_mask128"] = np.full((128, 1), 1.0 if core % 2 == 1 else 0.0, np.float32)
    selAB = np.zeros((16, 4096), dtype=BF)
    regsel = np.zeros((1, 128), dtype=np.float32)
    for tqb in range(4):
        tq0 = qoff + tqb * 512
        for tkc in range(32):
            tk0 = tkc * 128
            if tk0 >= tq0 + 512:
                reg = "A"
            elif tk0 <= tq0 - 128:
                reg = "B"
            else:
                reg = "C"
            if reg == "A":
                selAB[tqb * 4 + 0, tk0:tk0 + 128] = 1
                selAB[tqb * 4 + 1, tk0:tk0 + 128] = 1
                regsel[0, tqb * 32 + tkc] = -1.0
            elif reg == "B":
                selAB[tqb * 4 + 2, tk0:tk0 + 128] = 1
                selAB[tqb * 4 + 3, tk0:tk0 + 128] = 1
                regsel[0, tqb * 32 + tkc] = 1.0
    cc["c_selAB"] = selAB
    cc["c_regsel"] = regsel
    dd = np.zeros((128, 8 * 512), dtype=np.float64)
    pidx = np.arange(128)[:, None]
    bidx = np.arange(512)[None, :]
    for di in range(8):
        owner = 0 if di < 4 else 2048
        if owner == qoff:
            dd[:, di * 512:(di + 1) * 512] = np.abs(pidx - bidx + (di % 4) * 128)
    corr = np.zeros((AH, 128, 8 * 512), dtype=BF)
    for h in range(AH):
        corr[h] = np.exp(-np.float64(SLOPES[h]) * dd).astype(BF)
    cc["c_corr"] = corr
    return cc



def ceil_div(a, b):
    return (a + b - 1) // b


class Ctx:

    def __init__(self, nc, p, es):
        self.nc = nc
        self.p = p
        self.es = es
        self.sb = {}
        self.dram = {}
        self._scopes = []
        self._scope_serial = 0

    def push_scope(self):
        self._scope_serial += 1
        self._scopes.append((ExitStack(), []))

    def pop_scope(self):
        st, names = self._scopes.pop()
        st.close()
        for nm in names:
            self.sb.pop(nm, None)

    def sbuf(self, name, shape, dtype):
        import os
        if self._scopes:
            st, names = self._scopes[-1]
            t = st.enter_context(
                self.nc.sbuf_tensor(f"sb{self._scope_serial}_{name}_{len(names)}",
                                    shape, dtype))
            names.append(name)
            if os.environ.get("SBUF_DEBUG"):
                esz = {F32: 4, BF16: 2}.get(dtype, 4)
                print(f"[sbuf-s] {name}: {shape} = {shape[-1] * esz}B/part, "
                      f"remaining {self.nc.sbuf_bytes_remaining}")
            self.sb[name] = t
            return t
        t = self.es.enter_context(self.nc.sbuf_tensor("sb_" + name, shape, dtype))
        if os.environ.get("SBUF_DEBUG"):
            esz = {F32: 4, BF16: 2}.get(dtype, 4)
            print(f"[sbuf] {name}: {shape} = {shape[-1] * esz}B/part, "
                  f"remaining {self.nc.sbuf_bytes_remaining}")
        self.sb[name] = t
        return t

    def sget(self, name, shape, dtype):
        if name not in self.sb:
            return self.sbuf(name, shape, dtype)
        return self.sb[name]

    def psum(self, name, shape, dtype=F32):
        t = self.es.enter_context(self.nc.psum_tensor(name, shape, dtype))
        return t


def load_common_consts(cx, needed=None):
    p = cx.p
    names = [
        ("c_id128f", [128, 128], F32),
        ("c_id128b", [128, 128], BF16),
        ("c_ones_1x128b", [1, 128], BF16),
        ("c_onescol_b", [128, 1], BF16),
        ("c_ones_1x128f", [1, 128], F32),
        ("c_onescol_f", [128, 1], F32),
        ("c_ones2", [128, 2], BF16),
        ("c_selhd", [32, 2048], BF16),
        ("c_lt01", [128, 128], F32),
        ("c_ones32x64b", [32, 64], BF16),
        ("c_oneh8", [8, 8 * 128], BF16),
        ("c_postk", [128, 32], F32),
        ("c_dabs", [128, 4 * 512], F32),
        ("c_ones32x128f", [32, 128], F32),
        ("c_sel32", [32, 32 * 128], BF16),
        ("c_qrows", [64, T], BF16),
        ("c_mask64", [64, 1], F32),
        ("c_mask128", [128, 1], F32),
    ]
    tok = None
    for nm, shp, dt in names:
        if nm not in cx.dram:
            continue
        if needed is not None and nm not in needed:
            continue
        tile = cx.sbuf(nm, shp, dt) if nm not in cx.sb else cx.sb[nm]
        tok = p.dma("sync", tile[:], cx.dram[nm])
    return tok



def emit_load_x_fm(cx, x_tm_dram, xr, pw0, pw1):
    p = cx.p
    stg = [cx.sget(f"xio_stg{i}", [128, DM], F32) for i in range(2)]
    idf = cx.sb["c_id128f"]
    pws = [pw0, pw1]
    ev_tok = [None, None]
    ld_tok = [None, None]
    tr_tok = [None, None]
    vtok = None
    for tc in range(T // 128):
        i = tc % 2
        p.waits("sync", [tr_tok[i]])
        ld_tok[i] = p.dma("sync", stg[i][:], x_tm_dram[tc * 128:(tc + 1) * 128, :])
        p.waits("tensor", [ld_tok[i], ev_tok[i]])
        for c in range(DMC):
            tr_tok[i] = p.tr(pws[i][:, c * 128:(c + 1) * 128],
                             stg[i][:, c * 128:(c + 1) * 128], idf[:])
        tr_tok[i] = p.trfence(cx.sb["c_trf"], cx.sb["c_ones_1x128b"])
        p.waits("vector", [tr_tok[i]])
        dst = xr[:].rearrange("p (c t) -> p c t", c=DMC)[:, :, tc * 128:(tc + 1) * 128]
        src = pws[i][:].rearrange("p (c t) -> p c t", c=DMC)
        ev_tok[i] = vtok = p.vop("tensor_copy", out=dst, in_=src)
    p.vfence()
    return cx.p.tok("V")


def emit_store_x_fm(cx, xr, x_tm_dram, pw0, pw1, pre_toks=()):
    p = cx.p
    stg = [cx.sget(f"xio_stg{i}", [128, DM], F32) for i in range(2)]
    idf = cx.sb["c_id128f"]
    pws = [pw0, pw1]
    ev_tok = [None, None]
    st_tok = [None, None]
    tr_tok = [None, None]
    p.waits("tensor", list(pre_toks))
    gtok = None
    for tc in range(T // 128):
        i = tc % 2
        p.waits("tensor", [ev_tok[i]])
        for c in range(DMC):
            tr_tok[i] = p.tr(pws[i][:, c * 128:(c + 1) * 128],
                             xr[:, c * T + tc * 128: c * T + (tc + 1) * 128],
                             idf[:])
        p.waits("vector", [tr_tok[i], st_tok[i]])
        ev_tok[i] = p.vop("tensor_copy", out=stg[i][:], in_=pws[i][:])
        p.waits("gpsimd", [ev_tok[i]])
        st_tok[i] = gtok = p.dma("gpsimd", x_tm_dram[tc * 128:(tc + 1) * 128, :],
                                 stg[i][:])
    return gtok


def emit_ln(cx, xr, xl, pk0, pk1, pk2, pk3, x_tok=(), rows_out=None):
    p = cx.p
    onesb = cx.sb["c_onescol_b"]
    ones1 = cx.sb["c_ones_1x128b"]
    xbf = cx.sget("ln_xbf", [128, DMC * LNB], BF16)
    x2b = cx.sget("ln_x2b", [128, DMC * LNB], BF16)
    srow = [cx.sget(f"ln_row{i}", [1, LNB], F32) for i in range(3)]
    row4 = cx.sget("ln_row3", [1, LNB], F32)
    mrb = cx.sget("ln_mrb", [1, LNB], BF16)
    rrb = cx.sget("ln_rrb", [1, LNB], BF16)
    tmp = cx.sget("ln_tmp", [128, LNB], F32)
    if rows_out is not None:
        rows_out = (rows_out[0], rows_out[1], row4)
    out_tok = None
    stats_done = None
    stats_read = None
    apply_done = None
    for blk in range(T // LNB):
        p.waits("vector", [stats_done] + list(x_tok))
        ctok = None
        for c in range(DMC):
            ctok = p.vop("tensor_copy", out=xbf[:, c * LNB:(c + 1) * LNB],
                         in_=xr[:, c * T + blk * LNB: c * T + (blk + 1) * LNB])
        p.waits("scalar", [stats_done] + list(x_tok))
        sq_tok = None
        for c in range(DMC):
            sq_tok = p.sop("activation", out=x2b[:, c * LNB:(c + 1) * LNB],
                           in_=xr[:, c * T + blk * LNB: c * T + (blk + 1) * LNB],
                           func=AF.Square)
        p.waits("tensor", [ctok, sq_tok, stats_read])
        for c in range(DMC):
            p.mm(pk0[0:1, 0:LNB], onesb[:], xbf[:, c * LNB:(c + 1) * LNB],
                 start=(c == 0), stop=(c == DMC - 1))
        for c in range(DMC):
            stats_done = p.mm(pk1[0:1, 0:LNB], onesb[:],
                              x2b[:, c * LNB:(c + 1) * LNB],
                              start=(c == 0), stop=(c == DMC - 1))
        p.waits("vector", [stats_done])
        p.vop("tensor_scalar", out=srow[0][:], in0=pk0[0:1, 0:LNB],
              scalar1=1.0 / DM, scalar2=None, op0=ALU.mult)
        stats_read = p.vop("tensor_scalar", out=srow[1][:], in0=pk1[0:1, 0:LNB],
                           scalar1=1.0 / DM, scalar2=EPS, op0=ALU.mult,
                           op1=ALU.add)
        p.vfence()
        p.vop("tensor_tensor", out=srow[2][:], in0=srow[0][:], in1=srow[0][:],
              op=ALU.mult)
        p.vfence()
        p.vop("tensor_tensor", out=srow[1][:], in0=srow[1][:], in1=srow[2][:],
              op=ALU.subtract)
        p.vfence()
        vtok = p.vop("reciprocal", out=srow[2][:], in_=srow[1][:])
        p.vfence()
        p.waits("scalar", [vtok])
        stok = p.sop("activation", out=rrb[:], in_=srow[2][:], func=AF.Sqrt)
        mtok = p.vop("tensor_copy", out=mrb[:], in_=srow[0][:])
        if rows_out is not None:
            dm_m, dm_r, row4 = rows_out
            rtk = p.sop("activation", out=row4[:], in_=srow[2][:], func=AF.Sqrt)
            p.waits("gpsimd", [mtok, rtk])
            dtk1 = p.dma("gpsimd", dm_m[blk * LNB:(blk + 1) * LNB], srow[0][:])
            dtk2 = p.dma("gpsimd", dm_r[blk * LNB:(blk + 1) * LNB], row4[:])
            p.waits("vector", [dtk1, dtk2])
            p.waits("scalar", [dtk2])
        p.waits("tensor", [mtok, stok, apply_done])
        p.mm(pk2[:, 0:LNB], ones1[:], mrb[:])
        bt2 = p.mm(pk3[:, 0:LNB], ones1[:], rrb[:])
        p.waits("vector", [bt2])
        for c in range(DMC):
            p.vop("tensor_tensor", out=tmp[:],
                  in0=xr[:, c * T + blk * LNB: c * T + (blk + 1) * LNB],
                  in1=pk2[:, 0:LNB], op=ALU.subtract)
            p.vfence()
            out_tok = p.vop("tensor_tensor",
                            out=xl[:, c * T + blk * LNB: c * T + (blk + 1) * LNB],
                            in0=tmp[:], in1=pk3[:, 0:LNB], op=ALU.mult)
        apply_done = out_tok
    return out_tok


def emit_cast_weight(cx, src_dram, dst_sb, kchunks, cols, stage, scale=None,
                     pre_toks=(), qname="sync"):
    p = cx.p
    ltoks = []
    p.waits("sync", list(pre_toks))
    for c in range(kchunks):
        ltoks.append(p.dma(qname, stage[:, c * cols:(c + 1) * cols],
                           src_dram[c * 128:(c + 1) * 128, :]))
    p.waits("vector", [ltoks[-1]])
    if scale is None:
        vt = p.vop("tensor_copy", out=dst_sb[:, 0:kchunks * cols],
                   in_=stage[:, 0:kchunks * cols])
    else:
        vt = p.vop("tensor_scalar", out=dst_sb[:, 0:kchunks * cols],
                   in0=stage[:, 0:kchunks * cols], scalar1=float(scale),
                   scalar2=None, op0=ALU.mult)
    return vt


def emit_transpose_weight(cx, src_dram, dst_sb, stage, pw0, pw1, scale=None,
                          pre_toks=()):
    p = cx.p
    idf = cx.sb["c_id128f"]
    ltoks = []
    p.waits("sync", list(pre_toks))
    for a in range(8):
        ltoks.append(p.dma("sync", stage[:, a * DM:(a + 1) * DM],
                           src_dram[a * 128:(a + 1) * 128, :]))
    pws = [pw0, pw1]
    ev = [None, None]
    vt = None
    p.waits("tensor", [ltoks[-1]])
    for b in range(8):
        i = b % 2
        p.waits("tensor", [ev[i]])
        ttok = None
        for a in range(8):
            ttok = p.tr(pws[i][:, a * 128:(a + 1) * 128],
                        stage[:, a * DM + b * 128: a * DM + (b + 1) * 128],
                        idf[:])
        p.waits("vector", [ttok])
        if scale is None:
            ev[i] = vt = p.vop("tensor_copy", out=dst_sb[:, b * DM:(b + 1) * DM],
                               in_=pws[i][:])
        else:
            ev[i] = vt = p.vop("tensor_scalar", out=dst_sb[:, b * DM:(b + 1) * DM],
                               in0=pws[i][:], scalar1=float(scale), scalar2=None,
                               op0=ALU.mult)
    return vt


def emit_weff(cx, weT, stage32, pw0, pw1, pk0):
    p = cx.p
    idf = cx.sb["c_id128f"]
    d = cx.dram
    ldT = cx.sget("we_ldT", [100, 4], F32)
    luT = cx.sget("we_luT", [4, 50], F32)
    da = cx.sget("we_da", [100, 1024], F32)
    upT = cx.sget("we_upT", [50, 1024], F32)
    t1 = cx.sget("we_t1", [4, 1024], F32)
    t2 = cx.sget("we_t2", [50, 1024], F32)
    up_stg = cx.sget("we_upstg", [128, 8 * 50], F32)
    wblk = cx.sget("we_wblk", [128, 2 * 128], F32)
    tk1 = p.dma("sync", ldT[:], d["L1_ffn_lora_down"].rearrange("a b -> b a"),
                allow_slow_non_contiguous=True)
    tk2 = p.dma("sync", luT[:], d["L1_ffn_lora_up"].rearrange("a b -> b a"),
                allow_slow_non_contiguous=True)
    tk3 = p.dma("sync", da[:], d["L1_ffn_down_aux"])
    utoks = []
    for a in range(8):
        utoks.append(p.dma("sync", up_stg[:, a * 50:(a + 1) * 50],
                           d["L1_ffn_up_aux"][a * 128:(a + 1) * 128, :]))
    p.waits("tensor", [utoks[-1]])
    ttok = None
    for a in range(8):
        ttok = p.tr(pw0[0:50, a * 128:(a + 1) * 128],
                    up_stg[:, a * 50:(a + 1) * 50], idf[:])
    p.waits("vector", [ttok])
    uev = p.vop("tensor_copy", out=upT[:], in_=pw0[0:50, 0:1024])
    p.waits("tensor", [tk1, tk3, uev])
    mt = p.mm(pk0[0:4, 0:512], ldT[:], da[:, 0:512])
    mt2 = p.mm(pw1[0:4, 0:512], ldT[:], da[:, 512:1024])
    p.waits("vector", [mt, mt2])
    e1 = p.vop("tensor_copy", out=t1[:, 0:512], in_=pk0[0:4, 0:512])
    e2 = p.vop("tensor_copy", out=t1[:, 512:1024], in_=pw1[0:4, 0:512])
    p.waits("tensor", [tk2, e1, e2])
    mt = p.mm(pk0[0:50, 0:512], luT[:], t1[:, 0:512])
    mt2 = p.mm(pw1[0:50, 0:512], luT[:], t1[:, 512:1024])
    p.waits("vector", [mt, mt2])
    e1 = p.vop("tensor_copy", out=t2[:, 0:512], in_=pk0[0:50, 0:512])
    e2 = p.vop("tensor_copy", out=t2[:, 512:1024], in_=pw1[0:50, 0:512])
    p.waits("tensor", [e1, e2])
    for ic in range(8):
        for half in range(2):
            ps = pk0 if half == 0 else pw1
            dmt = p.mm(ps[0:128, 0:512], t2[:, ic * 128:(ic + 1) * 128],
                       upT[:, half * 512:(half + 1) * 512])
            p.waits("vector", [dmt])
            vt = p.vop("tensor_scalar",
                       out=stage32[:, ic * DM + half * 512: ic * DM + (half + 1) * 512],
                       in0=ps[0:128, 0:512], scalar1=0.5, scalar2=None,
                       op0=ALU.mult)
            p.waits("tensor", [vt])
    ld_tok = [None, None]
    tr_done = [None, None]
    ev = [None, None]
    vt = None
    for b in range(8):
        i = b % 2
        pws = pw0 if i == 0 else pw1
        for a in range(8):
            p.waits("sync", [tr_done[i]])
            ld_tok[i] = p.dma("sync", wblk[:, i * 128:(i + 1) * 128],
                              d["L1_ffn_W"][a * 128:(a + 1) * 128,
                                            b * 128:(b + 1) * 128])
            p.waits("tensor", [ld_tok[i], ev[i]])
            tr_done[i] = p.tr(pws[:, a * 128:(a + 1) * 128],
                              wblk[:, i * 128:(i + 1) * 128], idf[:])
        p.waits("vector", [tr_done[i]])
        ev[i] = vt = p.vop("scalar_tensor_tensor",
                           out=weT[:, b * DM:(b + 1) * DM],
                           in0=pws[:, 0:DM], scalar=0.5,
                           in1=stage32[:, b * DM:(b + 1) * DM],
                           op0=ALU.mult, op1=ALU.add)
    return vt


def emit_transpose_weight_f32(cx, src_dram, dst_f32, pw0, pw1, scale=None):
    p = cx.p
    idf = cx.sb["c_id128f"]
    stage = cx.sget("wtr_stage", [128, 8 * DM], F32)
    ltoks = []
    for a in range(8):
        ltoks.append(p.dma("sync", stage[:, a * DM:(a + 1) * DM],
                           src_dram[a * 128:(a + 1) * 128, :]))
    pws = [pw0, pw1]
    ev = [None, None]
    vt = None
    p.waits("tensor", [ltoks[-1]])
    for b in range(8):
        i = b % 2
        p.waits("tensor", [ev[i]])
        ttok = None
        for a in range(8):
            ttok = p.tr(pws[i][:, a * 128:(a + 1) * 128],
                        stage[:, a * DM + b * 128: a * DM + (b + 1) * 128],
                        idf[:])
        p.waits("vector", [ttok])
        if scale is None:
            ev[i] = vt = p.vop("tensor_copy", out=dst_f32[:, b * DM:(b + 1) * DM],
                               in_=pws[i][:, 0:DM])
        else:
            ev[i] = vt = p.vop("tensor_scalar", out=dst_f32[:, b * DM:(b + 1) * DM],
                               in0=pws[i][:, 0:DM], scalar1=float(scale),
                               scalar2=None, op0=ALU.mult)
    return vt


def emit_l1ffn(cx, xr, xl, weT, halfb, pk0, pk1, tok_xl, tok_w):
    p = cx.p
    pks = [pk0, pk1]
    ev = [None, None]
    vt = None
    p.waits("tensor", [tok_xl, tok_w])
    g = 0
    for oc in range(DMC):
        for tqb in range(T // 512):
            i = g % 2
            p.waits("tensor", [ev[i]])
            mt = None
            for kc in range(DMC):
                mt = p.mm(pks[i][:, 0:512],
                          weT[:, kc * DM + oc * 128: kc * DM + (oc + 1) * 128],
                          xl[:, kc * T + tqb * 512: kc * T + (tqb + 1) * 512],
                          start=(kc == 0), stop=(kc == DMC - 1))
            p.waits("vector", [mt])
            ev[i] = vt = p.vop(
                "scalar_tensor_tensor",
                out=xr[:, oc * T + tqb * 512: oc * T + (tqb + 1) * 512],
                in0=pks[i][:, 0:512], scalar=halfb[:, oc:oc + 1],
                in1=xr[:, oc * T + tqb * 512: oc * T + (tqb + 1) * 512],
                op0=ALU.add, op1=ALU.add)
            g += 1
    return vt



def build_l1ffn_program():
    nc = bass.Bass("TRN2", target_bir_lowering=False)
    es = ExitStack()
    p = Prog(nc, es)
    cx = Ctx(nc, p, es)
    d = cx.dram
    d["x"] = nc.dram_tensor("x", [T, DM], F32, kind="ExternalInput").ap()
    for nm, shp in [("L1_ffn_W", [DM, DM]), ("L1_ffn_b", [DM]),
                    ("L1_ffn_lora_down", [4, 100]), ("L1_ffn_lora_up", [50, 4]),
                    ("L1_ffn_down_aux", [100, DM]), ("L1_ffn_up_aux", [DM, 50])]:
        d[nm] = nc.dram_tensor(nm, shp, F32, kind="ExternalInput").ap()
    for nm, arr in host_constants().items():
        d[nm] = nc.dram_tensor(nm, list(arr.shape),
                               BF16 if arr.dtype == BF else F32,
                               kind="ExternalInput").ap()
    d["x_out"] = nc.dram_tensor("x_out", [T, DM], F32, kind="ExternalOutput").ap()

    with es:
        pk0 = cx.psum("pk0", [128, 512])
        pk1 = cx.psum("pk1", [128, 512])
        pk2 = cx.psum("pk2", [128, 512])
        pk3 = cx.psum("pk3", [128, 512])
        pw0 = cx.psum("pw0", [128, 1024])
        pw1 = cx.psum("pw1", [128, 1024])
        cx.sb["c_trf"] = pk0
        xr = cx.sbuf("xr", [128, DMC * T], F32)
        xl = cx.sbuf("xl", [128, DMC * T], BF16)
        weT = cx.sbuf("weT", [128, 8 * DM], BF16)
        stage32 = cx.sbuf("stage32", [128, 8 * DM], F32)
        halfb = cx.sbuf("halfb", [128, 8], F32)
        ctok = load_common_consts(cx, needed=["c_id128f", "c_ones_1x128b",
                                              "c_onescol_b"])
        btok = p.dma("sync", halfb[:],
                     d["L1_ffn_b"].rearrange("(c p) -> p c", p=128),
                     allow_slow_non_contiguous=True)
        p.waits("vector", [btok])
        hbtok = p.vop("tensor_scalar", out=halfb[:], in0=halfb[:],
                      scalar1=0.5, scalar2=None, op0=ALU.mult)
        p.waits("tensor", [ctok])
        xtok = emit_load_x_fm(cx, d["x"], xr, pw0, pw1)
        p.barrier()
        wtok = emit_weff(cx, weT, stage32, pw0, pw1, pk0)
        p.barrier()
        lntok = emit_ln(cx, xr, xl, pk0, pk1, pk2, pk3)
        p.barrier()
        ftok = emit_l1ffn(cx, xr, xl, weT, halfb, pk0, pk1, lntok, wtok)
        p.barrier()
        stok = emit_store_x_fm(cx, xr, d["x_out"], pw0, pw1, pre_toks=[ftok])
        p.waits("gpsimd", [stok])
        p.finalize()
    return nc


def build_copy_program(stages=("load", "weff", "ln", "ffn")):
    nc = bass.Bass("TRN2", target_bir_lowering=False)
    es = ExitStack()
    p = Prog(nc, es)
    cx = Ctx(nc, p, es)
    d = cx.dram
    d["x"] = nc.dram_tensor("x", [T, DM], F32, kind="ExternalInput").ap()
    for nm, shp in [("L1_ffn_W", [DM, DM]), ("L1_ffn_b", [DM]),
                    ("L1_ffn_lora_down", [4, 100]), ("L1_ffn_lora_up", [50, 4]),
                    ("L1_ffn_down_aux", [100, DM]), ("L1_ffn_up_aux", [DM, 50])]:
        d[nm] = nc.dram_tensor(nm, shp, F32, kind="ExternalInput").ap()
    for nm, arr in host_constants().items():
        d[nm] = nc.dram_tensor(nm, list(arr.shape),
                               BF16 if arr.dtype == BF else F32,
                               kind="ExternalInput").ap()
    d["x_out"] = nc.dram_tensor("x_out", [T, DM], F32, kind="ExternalOutput").ap()
    d["xl_out"] = nc.dram_tensor("xl_out", [T, DM], F32, kind="ExternalOutput").ap()
    d["we_out"] = nc.dram_tensor("we_out", [1024, DM], F32, kind="ExternalOutput").ap()
    with es:
        pk0 = cx.psum("pk0", [128, 512])
        pk1 = cx.psum("pk1", [128, 512])
        pk2 = cx.psum("pk2", [128, 512])
        pk3 = cx.psum("pk3", [128, 512])
        pw0 = cx.psum("pw0", [128, 1024])
        pw1 = cx.psum("pw1", [128, 1024])
        cx.sb["c_trf"] = pk0
        xr = cx.sbuf("xr", [128, DMC * T], F32)
        xl = cx.sbuf("xl", [128, DMC * T], BF16)
        weT = cx.sbuf("weT", [128, 8 * DM], BF16)
        stage32 = cx.sbuf("stage32", [128, 8 * DM], F32)
        halfb = cx.sbuf("halfb", [128, 8], F32)
        ctok = load_common_consts(cx, needed=["c_id128f", "c_ones_1x128b",
                                              "c_onescol_b"])
        btok = p.dma("sync", halfb[:],
                     d["L1_ffn_b"].rearrange("(c p) -> p c", p=128),
                     allow_slow_non_contiguous=True)
        p.waits("vector", [btok])
        p.vop("tensor_scalar", out=halfb[:], in0=halfb[:],
              scalar1=0.5, scalar2=None, op0=ALU.mult)
        p.waits("tensor", [ctok])
        xtok = emit_load_x_fm(cx, d["x"], xr, pw0, pw1)
        p.barrier()
        wtok = None
        if "weff" in stages:
            wtok = emit_weff(cx, weT, stage32, pw0, pw1, pk0)
            p.barrier()
            for kc in range(8):
                vt = p.vop("tensor_copy", out=stage32[:, kc * DM:(kc + 1) * DM],
                           in_=weT[:, kc * DM:(kc + 1) * DM])
            p.waits("gpsimd", [vt])
            for kc in range(8):
                gt = p.dma("gpsimd", d["we_out"][kc * 128:(kc + 1) * 128, :],
                           stage32[:, kc * DM:(kc + 1) * DM])
            p.barrier()
        if "ln" in stages:
            lntok = emit_ln(cx, xr, xl, pk0, pk1, pk2, pk3)
            p.barrier()
            for c in range(DMC):
                vt = p.vop("tensor_copy", out=stage32[:, c * DM: c * DM + 512],
                           in_=xl[:, c * T:c * T + 512])
            p.waits("gpsimd", [vt])
            gt = p.dma("gpsimd",
                       d["xl_out"].rearrange("(a p) c -> p (a c)", p=128)[:, 0:8 * 512][:, 0:0]
                       if False else
                       d["xl_out"][0:128, :],
                       stage32[:, 0:DM])
            p.barrier()
        if "ffn" in stages:
            ftok = emit_l1ffn(cx, xr, xl, weT, halfb, pk0, pk1, None, None)
            p.barrier()
        stok = emit_store_x_fm(cx, xr, d["x_out"], pw0, pw1)
        p.waits("gpsimd", [stok])
        p.finalize()
    return nc


def emit_ln_rows(cx, xr, xl, pk0, pk1, pk2, pk3, negm_col, rcol, d_bounce,
                 x_tok=()):
    p = cx.p
    tok = emit_ln(cx, xr, xl, pk0, pk1, pk2, pk3, x_tok=x_tok,
                  rows_out=(d_bounce[0, :], d_bounce[1, :]))
    p.waits("sync", [p.tok("G")])
    t3 = p.dma("sync", negm_col[:],
               d_bounce[0, :].rearrange("(a p) -> p a", p=128),
               allow_slow_non_contiguous=True)
    t4 = p.dma("sync", rcol[:],
               d_bounce[1, :].rearrange("(a p) -> p a", p=128),
               allow_slow_non_contiguous=True)
    p.waits("vector", [t3, t4])
    ct = p.vop("tensor_scalar", out=negm_col[:], in0=negm_col[:],
               scalar1=-1.0, scalar2=None, op0=ALU.mult)
    return tok, ct


def emit_moe(cx, xr, xl, negm_col, rcol, pk0, pk1, pk2, pk3, pw0, pw1,
             xl_tok, cols_tok, debug=False):
    p = cx.p
    d = cx.dram
    idf = cx.sb["c_id128f"]
    ones1f = cx.sb["c_ones_1x128f"]
    onescf = cx.sb["c_onescol_f"]
    oneh8 = cx.sb["c_oneh8"]
    wg = cx.sget("moe_wg", [128, 64], F32)
    bgrow = cx.sget("moe_bgrow", [1, 8], F32)
    b1c = cx.sget("moe_b1c", [128, 64], F32)

    t_wg = p.dma("sync", wg[:], d["L0_ffn_Wg"].rearrange("(c p) e -> p c e", p=128),
                 allow_slow_non_contiguous=True)
    t_bg = p.dma("sync", bgrow[:], d["L0_ffn_bg"].rearrange("(x e) -> x e", x=1))
    t_b1 = p.dma("sync", b1c[:], d["L0_ffn_b1"].rearrange("e (c p) -> p e c", p=128),
                 allow_slow_non_contiguous=True)
    masks_fm = cx.sget("moe_masksfm", [8, T], BF16)
    cx.push_scope()
    S_sb = cx.sget("moe_S", [8, T], F32)
    csumr = cx.sget("moe_csumr", [1, 8], F32)
    csumb = cx.sget("moe_csumb", [128, 8], F32)
    bgb = cx.sget("moe_bgb", [128, 8], F32)
    lg = cx.sget("moe_lg", [128, 128], F32)
    eq = cx.sget("moe_eq", [128, 128], F32)
    cum = cx.sget("moe_cum", [128, 128], F32)
    cum2 = cx.sget("moe_cum2", [128, 128], F32)
    red = cx.sget("moe_red", [128, 16], F32)
    p.waits("tensor", [t_wg, xl_tok, cols_tok])
    for tqb in range(T // 512):
        mt = None
        for kc in range(DMC):
            mt = p.mm(pk0[0:8, 0:512], wg[:, kc * 8:(kc + 1) * 8],
                      xr[:, kc * T + tqb * 512: kc * T + (tqb + 1) * 512],
                      start=(kc == 0), stop=(kc == DMC - 1))
        p.waits("vector", [mt])
        ev = p.vop("tensor_copy", out=S_sb[:, tqb * 512:(tqb + 1) * 512],
                   in_=pk0[0:8, 0:512])
        p.waits("tensor", [ev])
    for kc in range(DMC):
        mt = p.mm(pk1[0:1, 0:8], onescf[:], wg[:, kc * 8:(kc + 1) * 8],
                  start=(kc == 0), stop=(kc == DMC - 1))
    p.waits("vector", [mt])
    ev = p.vop("tensor_copy", out=csumr[:], in_=pk1[0:1, 0:8])
    p.waits("tensor", [ev, t_bg])
    m1 = p.mm(pk2[:, 0:8], ones1f[:], csumr[:])
    m2 = p.mm(pk3[:, 0:8], ones1f[:], bgrow[:])
    p.waits("vector", [m1, m2])
    p.vop("tensor_copy", out=csumb[:], in_=pk2[:, 0:8])
    ev = p.vop("tensor_copy", out=bgb[:], in_=pk3[:, 0:8])
    p.vfence()
    S_tm = cx.sget("moe_Stm", [128, 128], F32)
    p.waits("tensor", [ev])
    ev_t = [None, None]
    for tc in range(16):
        pkx = pk2 if tc % 2 == 0 else pk3
        p.waits("tensor", [ev_t[tc % 2]])
        ttok = p.tr(pkx[:, 0:8], S_sb[0:8, tc * 128:(tc + 1) * 128],
                    idf[0:8, 0:8])
        p.waits("vector", [ttok])
        ev_t[tc % 2] = p.vop("tensor_copy", out=S_tm[:, tc * 8:(tc + 1) * 8],
                             in_=pkx[:, 0:8])
    p.vfence()
    for tc in range(16):
        sl = slice(tc * 8, (tc + 1) * 8)
        p.vop("scalar_tensor_tensor", out=lg[:, sl], in0=csumb[:],
              scalar=negm_col[:, tc:tc + 1], in1=S_tm[:, sl],
              op0=ALU.mult, op1=ALU.add)
    p.vfence()
    for tc in range(16):
        sl = slice(tc * 8, (tc + 1) * 8)
        p.vop("tensor_scalar", out=lg[:, sl], in0=lg[:, sl],
              scalar1=rcol[:, tc:tc + 1], scalar2=None, op0=ALU.mult)
    p.vfence()
    for tc in range(16):
        sl = slice(tc * 8, (tc + 1) * 8)
        p.vop("tensor_tensor", out=lg[:, sl], in0=lg[:, sl], in1=bgb[:],
              op=ALU.add)
    p.vfence()
    for tc in range(16):
        p.vop("tensor_reduce", out=red[:, tc:tc + 1], in_=lg[:, tc * 8:(tc + 1) * 8],
              axis=mybir.AxisListType.X, op=ALU.max)
    p.vfence()
    for tc in range(16):
        p.vop("tensor_tensor", out=eq[:, tc * 8:(tc + 1) * 8],
              in0=lg[:, tc * 8:(tc + 1) * 8],
              in1=red[:, tc:tc + 1].broadcast_to([128, 8]), op=ALU.is_equal)
    p.vfence()
    eq3 = eq[:].rearrange("p (a b) -> p a b", b=8)
    cum3 = cum[:].rearrange("p (a b) -> p a b", b=8)
    c23 = cum2[:].rearrange("p (a b) -> p a b", b=8)
    p.vop("tensor_copy", out=cum3[:, :, 0:1], in_=eq3[:, :, 0:1])
    p.vop("tensor_tensor", out=cum3[:, :, 1:8], in0=eq3[:, :, 1:8],
          in1=eq3[:, :, 0:7], op=ALU.add)
    p.vfence()
    p.vop("tensor_copy", out=c23[:, :, 0:2], in_=cum3[:, :, 0:2])
    p.vop("tensor_tensor", out=c23[:, :, 2:8], in0=cum3[:, :, 2:8],
          in1=cum3[:, :, 0:6], op=ALU.add)
    p.vfence()
    p.vop("tensor_copy", out=cum3[:, :, 0:4], in_=c23[:, :, 0:4])
    p.vop("tensor_tensor", out=cum3[:, :, 4:8], in0=c23[:, :, 4:8],
          in1=c23[:, :, 0:4], op=ALU.add)
    p.vfence()
    p.vop("tensor_scalar", out=cum[:], in0=cum[:], scalar1=1.0, scalar2=None,
          op0=ALU.is_equal)
    p.vfence()
    mk = p.vop("tensor_tensor", out=eq[:], in0=eq[:], in1=cum[:], op=ALU.mult)
    if debug:
        p.waits("gpsimd", [mk])
        p.dma("gpsimd", cx.dram["dbg_lg"], lg[:])
        p.dma("gpsimd", cx.dram["dbg_eq"], eq[:])
        p.dma("gpsimd", cx.dram["dbg_S"], S_sb[:])
        p.dma("gpsimd", cx.dram["dbg_red"], red[:])
        p.dma("gpsimd", cx.dram["dbg_cum"], cum[:])
        p.dma("gpsimd", cx.dram["dbg_mr"][0:128, :], negm_col[:])
        p.dma("gpsimd", cx.dram["dbg_mr"][128:256, :], rcol[:])
    p.waits("tensor", [mk])
    ev_t = [None, None]
    for tc in range(16):
        pkx = pk2 if tc % 2 == 0 else pk3
        p.waits("tensor", [ev_t[tc % 2]])
        ttok = p.tr(pkx[0:8, 0:128], eq[:, tc * 8:(tc + 1) * 8], idf[:])
        p.waits("vector", [ttok])
        ev_t[tc % 2] = p.vop("tensor_copy",
                             out=masks_fm[:, tc * 128:(tc + 1) * 128],
                             in_=pkx[0:8, 0:128])
    p.waits("tensor", [ev_t[0], ev_t[1]])
    cx.pop_scope()
    stg_sl = [cx.sget("xio_stg0", [128, DM], F32),
              cx.sget("xio_stg1", [128, DM], F32)]
    wb1 = cx.sget("moe_wb1", [128, 2 * 8 * DM], BF16)
    wb2 = cx.sget("moe_wb2", [128, 2 * 8 * DM], BF16)
    h = cx.sget("moe_h", [128, 8 * 512], BF16)
    mbc1 = cx.sget("moe_mbc", [128, T], BF16)
    b2r = cx.sget("moe_b2r", [1, 2 * DM], BF16)
    w_rd = [[None, None], [None, None]]
    b2_rd = [None, None]
    mbc_rd_t = None
    mbc_rd_v = None
    cast_done = [[None, None], [None, None]]
    b2c_done = [None, None]
    mbc_tok = None
    slot_cast = [None, None]

    def pf_make(e):
        buf = e % 2
        pend = []

        def dma_op(j):
            def f():
                s = j % 2
                p.waits("sync", [slot_cast[s]])
                if j < 16:
                    w_i, kc = divmod(j, 8)
                    lt = p.dma("sync", stg_sl[s][:],
                               d["L0_ffn_W1" if w_i == 0 else "L0_ffn_W2"]
                               [e, kc * 128:(kc + 1) * 128, :])
                    pend.append((w_i, kc, s, lt))
                else:
                    lt = p.dma("sync", stg_sl[s][0:1, :],
                               d["L0_ffn_b2"][e:e + 1, :])
                    pend.append((2, 0, s, lt))
            return f

        def cast_op():
            def f():
                w_i, kc, s, lt = pend.pop(0)
                if w_i == 0:
                    p.waits("vector", [lt, w_rd[0][buf]])
                    tk = p.vop("tensor_copy",
                               out=wb1[:, (buf * 8 + kc) * DM:
                                       (buf * 8 + kc + 1) * DM],
                               in_=stg_sl[s][:])
                    cast_done[0][buf] = tk
                elif w_i == 1:
                    p.waits("vector", [lt, w_rd[1][buf]])
                    tk = p.vop("tensor_scalar",
                               out=wb2[:, (buf * 8 + kc) * DM:
                                       (buf * 8 + kc + 1) * DM],
                               in0=stg_sl[s][:], scalar1=0.5,
                               scalar2=None, op0=ALU.mult)
                    cast_done[1][buf] = tk
                else:
                    p.waits("vector", [lt, b2_rd[buf]])
                    tk = p.vop("tensor_scalar",
                               out=b2r[0:1, buf * DM:(buf + 1) * DM],
                               in0=stg_sl[s][0:1, :],
                               scalar1=0.5, scalar2=None, op0=ALU.mult)
                    b2c_done[buf] = tk
                slot_cast[s] = tk
            return f

        ops = [dma_op(0), dma_op(1)]
        for j in range(2, 17):
            ops.append(cast_op())
            ops.append(dma_op(j))
        ops.append(cast_op())
        ops.append(cast_op())
        return ops

    for f in pf_make(0):
        f()

    out_evac = None
    h_mask_tok = None
    gelu_tok = [None, None]
    w2_done = None
    final_tok = None
    mbc_cp = [None, None]
    for e in range(NEXP):
        buf = e % 2
        for tqb in range(T // 512):
            pkx = pk2 if tqb % 2 == 0 else pk3
            p.waits("tensor", [mbc_rd_t, mbc_cp[tqb % 2]])
            mt = p.mm(pkx[:, 0:512], oneh8[:, e * 128:(e + 1) * 128],
                      masks_fm[:, tqb * 512:(tqb + 1) * 512])
            p.waits("vector", [mt, mbc_rd_v])
            mbc_cp[tqb % 2] = p.vop(
                "tensor_copy",
                out=mbc1[:, tqb * 512:(tqb + 1) * 512],
                in_=pkx[:, 0:512])
        mbc_tok = mbc_cp[1]
        pf = pf_make(e + 1) if e + 1 < NEXP else []

        for tqb in range(T // 512):
            a = (len(pf) * tqb) // 4
            b = (len(pf) * (tqb + 1)) // 4
            for f in pf[a:b]:
                f()
            tsl = slice(tqb * 512, (tqb + 1) * 512)
            for ic in range(DMC):
                pk = pk0 if ic % 2 == 0 else pk1
                p.waits("tensor", [cast_done[0][buf], gelu_tok[ic % 2],
                                   xl_tok])
                mt = None
                for kc in range(DMC):
                    mt = p.mm(pk[:, 0:512],
                              wb1[:, (buf * 8 + kc) * DM + ic * 128:
                                  (buf * 8 + kc) * DM + (ic + 1) * 128],
                              xl[:, kc * T + tqb * 512: kc * T + (tqb + 1) * 512],
                              start=(kc == 0), stop=(kc == DMC - 1))
                w_rd[0][buf] = mt
                p.waits("scalar", [mt, w2_done, t_b1])
                gelu_tok[ic % 2] = p.sop(
                    "activation", out=h[:, ic * 512:(ic + 1) * 512],
                    in_=pk[:, 0:512], func=AF.Gelu,
                    bias=b1c[:, e * 8 + ic: e * 8 + ic + 1])
                p.waits("vector", [gelu_tok[ic % 2], mbc_tok])
                h_mask_tok = p.vop(
                    "tensor_tensor", out=h[:, ic * 512:(ic + 1) * 512],
                    in0=h[:, ic * 512:(ic + 1) * 512],
                    in1=mbc1[:, tsl], op=ALU.mult)
                mbc_rd_v = h_mask_tok
            for ohalf in range(2):
                p.waits("tensor", [h_mask_tok, out_evac, cast_done[1][buf],
                                   b2c_done[buf]])
                for o4 in range(4):
                    o = ohalf * 4 + o4
                    pw = pw0 if o4 < 2 else pw1
                    psl = pw[:, (o4 % 2) * 512:(o4 % 2) * 512 + 512]
                    for ic in range(DMC):
                        mt = p.mm(psl,
                                  wb2[:, (buf * 8 + ic) * DM + o * 128:
                                      (buf * 8 + ic) * DM + (o + 1) * 128],
                                  h[:, ic * 512:(ic + 1) * 512],
                                  start=(ic == 0), stop=False)
                    mt = p.mm(psl, b2r[0:1, buf * DM + o * 128:
                                       buf * DM + (o + 1) * 128],
                              mbc1[0:1, tsl], start=False, stop=True)
                    w2_done = mt
                    w_rd[1][buf] = mt
                    b2_rd[buf] = mt
                    mbc_rd_t = mt
                p.waits("vector", [mt])
                for o4 in range(4):
                    o = ohalf * 4 + o4
                    pw = pw0 if o4 < 2 else pw1
                    psl = pw[:, (o4 % 2) * 512:(o4 % 2) * 512 + 512]
                    out_evac = p.vop(
                        "tensor_tensor",
                        out=xr[:, o * T + tqb * 512: o * T + (tqb + 1) * 512],
                        in0=xr[:, o * T + tqb * 512: o * T + (tqb + 1) * 512],
                        in1=psl, op=ALU.add)
            final_tok = out_evac
    return final_tok


def moe_dram_decls(nc, d):
    for nm, shp in [("L0_ffn_Wg", [DM, 8]), ("L0_ffn_bg", [8]),
                    ("L0_ffn_W1", [8, DM, DM]), ("L0_ffn_b1", [8, DM]),
                    ("L0_ffn_W2", [8, DM, DM]), ("L0_ffn_b2", [8, DM])]:
        d[nm] = nc.dram_tensor(nm, shp, F32, kind="ExternalInput").ap()


def build_moe_program(debug=False):
    nc = bass.Bass("TRN2", target_bir_lowering=False)
    es = ExitStack()
    p = Prog(nc, es)
    cx = Ctx(nc, p, es)
    d = cx.dram
    d["x"] = nc.dram_tensor("x", [T, DM], F32, kind="ExternalInput").ap()
    moe_dram_decls(nc, d)
    for nm, arr in host_constants().items():
        d[nm] = nc.dram_tensor(nm, list(arr.shape),
                               BF16 if arr.dtype == BF else F32,
                               kind="ExternalInput").ap()
    d["x_out"] = nc.dram_tensor("x_out", [T, DM], F32, kind="ExternalOutput").ap()
    d["d_rows"] = nc.dram_tensor("d_rows", [2, T], F32).ap()
    if debug:
        d["dbg_lg"] = nc.dram_tensor("dbg_lg", [128, 128], F32, kind="ExternalOutput").ap()
        d["dbg_eq"] = nc.dram_tensor("dbg_eq", [128, 128], F32, kind="ExternalOutput").ap()
        d["dbg_S"] = nc.dram_tensor("dbg_S", [8, T], F32, kind="ExternalOutput").ap()
        d["dbg_red"] = nc.dram_tensor("dbg_red", [128, 16], F32, kind="ExternalOutput").ap()
        d["dbg_mr"] = nc.dram_tensor("dbg_mr", [256, 16], F32, kind="ExternalOutput").ap()
        d["dbg_cum"] = nc.dram_tensor("dbg_cum", [128, 128], F32, kind="ExternalOutput").ap()
    with es:
        pk0 = cx.psum("pk0", [128, 512])
        pk1 = cx.psum("pk1", [128, 512])
        pk2 = cx.psum("pk2", [128, 512])
        pk3 = cx.psum("pk3", [128, 512])
        pw0 = cx.psum("pw0", [128, 1024])
        pw1 = cx.psum("pw1", [128, 1024])
        cx.sb["c_trf"] = pk0
        xr = cx.sbuf("xr", [128, DMC * T], F32)
        xl = cx.sbuf("xl", [128, DMC * T], BF16)
        negm = cx.sbuf("negm", [128, 16], F32)
        rcol = cx.sbuf("rcol", [128, 16], F32)
        ctok = load_common_consts(cx, needed=[
            "c_id128f", "c_ones_1x128b", "c_onescol_b", "c_ones_1x128f",
            "c_onescol_f", "c_oneh8"])
        p.waits("tensor", [ctok])
        xtok = emit_load_x_fm(cx, d["x"], xr, pw0, pw1)
        p.barrier()
        lntok, coltok = emit_ln_rows(cx, xr, xl, pk0, pk1, pk2, pk3,
                                     negm, rcol, d["d_rows"])
        p.barrier()
        emit_moe(cx, xr, xl, negm, rcol, pk0, pk1, pk2, pk3, pw0, pw1,
                 lntok, coltok, debug=debug)
        p.barrier()
        stok = emit_store_x_fm(cx, xr, d["x_out"], pw0, pw1)
        p.waits("gpsimd", [stok])
        p.finalize()
    return nc


def make_b2sel(b2):
    out = np.zeros((8, 4, DM), dtype=np.float32)
    for e in range(8):
        out[e, e // 2] = b2[e]
    return out.reshape(8, 4 * DM)


def attn_dram_decls(nc, d):
    for nm, shp in [("L1_attn_Wq", [DM, DM]), ("L1_attn_Wk", [DM, DM]),
                    ("L1_attn_Wv", [DM, DM]), ("L1_attn_Wo", [DM, DM]),
                    ("L1_attn_bo", [DM])]:
        d[nm] = nc.dram_tensor(nm, shp, F32, kind="ExternalInput").ap()
    d["c_qrows"] = nc.dram_tensor("c_qrows", [64, T], BF16,
                                  kind="ExternalInput").ap()
    d["d_kv_in"] = nc.dram_tensor("d_kv_in", [4097, DM], BF16).ap()
    d["d_kv_out"] = nc.dram_tensor("d_kv_out", [2 * 4097, DM], BF16).ap()
    d["c_selAB"] = nc.dram_tensor("c_selAB", [16, 4096], BF16,
                                  kind="ExternalInput").ap()
    d["c_regsel"] = nc.dram_tensor("c_regsel", [1, 128], F32,
                                   kind="ExternalInput").ap()
    d["c_corr"] = nc.dram_tensor("c_corr", [AH, 128, 8 * 512], BF16,
                                 kind="ExternalInput").ap()


def emit_attn(cx, xr, xl, qoff, pk0, pk1, pk2, pk3, pw0, pw1, xl_tok,
              debug=False):
    p = cx.p
    d = cx.dram
    idf = cx.sb["c_id128f"]
    ones1f = cx.sb["c_ones_1x128f"]
    ones2 = cx.sb["c_ones2"]
    postk = cx.sb["c_postk"]
    Qfm = cx.sget("at_q", [128, DMC * T], BF16)
    ctxU = xl
    bocol = cx.sget("at_bocol", [128, 8], F32)
    q2m = cx.sget("at_q2m", [2, 8], F32)
    k2m = cx.sget("at_k2m", [2, 8], F32)
    k2m2 = cx.sget("at_k2m2", [2, 16], BF16)
    mrow = cx.sget("at_mrow", [1, 16], F32)
    negM = cx.sget("at_negM", [128, 16], F32)
    red2 = cx.sget("at_red2", [2, 1], F32)
    sq = cx.sget("at_sq", [128, 512], BF16)
    kvstg = [cx.sget(f"xio_stg{i}", [128, DM], F32).bitcast(BF16)
             for i in range(2)]
    rl = cx.sget("at_rl", [1, 512], F32)
    rlb = cx.sget("at_rlb", [64, 512], BF16)

    stage = cx.sget("at_wstg", [128, 256], F32)
    wT = cx.sget("at_wT", [128, 8 * DM], BF16)

    botok = p.dma("sync", bocol[:], d["L1_attn_bo"].rearrange("(c p) -> p c", p=128),
                  allow_slow_non_contiguous=True)
    z1 = p.vop("memset", q2m[:], -1e30)
    z2 = p.vop("memset", k2m[:], -1e30)
    p.vfence()
    q2g = None

    def transpose_weight_into(dst, src_dram, scale=None):
        ld_tok = [None, None]
        tr_done = [None, None]
        ev = [None, None]
        vt = None
        for b in range(8):
            i = b % 2
            pws = pw0 if i == 0 else pw1
            for a in range(8):
                p.waits("sync", [tr_done[i]])
                ld_tok[i] = p.dma("sync", stage[:, i * 128:(i + 1) * 128],
                                  src_dram[a * 128:(a + 1) * 128,
                                           b * 128:(b + 1) * 128])
                p.waits("tensor", [ld_tok[i], ev[i]])
                tr_done[i] = p.tr(pws[:, a * 128:(a + 1) * 128],
                                  stage[:, i * 128:(i + 1) * 128], idf[:])
            p.waits("vector", [tr_done[i]])
            if scale is None:
                ev[i] = vt = p.vop("tensor_copy", out=dst[:, b * DM:(b + 1) * DM],
                                   in_=pws[:, 0:DM])
            else:
                ev[i] = vt = p.vop("tensor_scalar", out=dst[:, b * DM:(b + 1) * DM],
                                   in0=pws[:, 0:DM], scalar1=float(scale),
                                   scalar2=None, op0=ALU.mult)
        return vt

    wtok = transpose_weight_into(wT, d["L1_attn_Wq"], scale=0.125)
    p.barrier()
    ev_rot = [None, None]
    sq_done = None
    mm_done = None
    for oc in range(DMC):
        for tqb in range(T // 512):
            i = (oc * 4 + tqb) % 2
            pkx = pk0 if i == 0 else pk1
            p.waits("tensor", [ev_rot[i], sq_done])
            mt = None
            for kc in range(DMC):
                mt = p.mm(pkx[:, 0:512],
                          wT[:, kc * DM + oc * 128: kc * DM + (oc + 1) * 128],
                          xl[:, kc * T + tqb * 512: kc * T + (tqb + 1) * 512],
                          start=(kc == 0), stop=(kc == DMC - 1))
            p.waits("vector", [mt])
            ev_rot[i] = p.vop("tensor_copy",
                              out=Qfm[:, oc * T + tqb * 512: oc * T + (tqb + 1) * 512],
                              in_=pkx[:, 0:512])
            p.waits("scalar", [ev_rot[i]])
            sq_done = p.sop("activation", out=sq[:], in_=pkx[:, 0:512],
                            func=AF.Square)
            p.waits("tensor", [sq_done, q2g])
            m2 = p.mm(pk2[0:2, 0:512], ones2[:], sq[:])
            p.waits("vector", [m2])
            p.vop("tensor_reduce", out=red2[:], in_=pk2[0:2, 0:512],
                  axis=mybir.AxisListType.X, op=ALU.max)
            p.vfence()
            q2g = p.vop("tensor_tensor", out=q2m[:, oc:oc + 1],
                        in0=q2m[:, oc:oc + 1], in1=red2[:], op=ALU.max)
            p.vfence()
    wtok = transpose_weight_into(wT, d["L1_attn_Wk"])
    p.barrier()
    kv_view = d["d_kv_in"][0:2048, :].rearrange("(f two) c -> f two c", two=2)
    st_done = [None, None]
    for oc in range(DMC):
        for tqb in range(T // 512):
            i = (oc * 4 + tqb) % 2
            pkx = pk0 if i == 0 else pk1
            p.waits("tensor", [ev_rot[i], sq_done])
            mt = None
            for kc in range(DMC):
                mt = p.mm(pkx[:, 0:512],
                          wT[:, kc * DM + oc * 128: kc * DM + (oc + 1) * 128],
                          xl[:, kc * T + tqb * 512: kc * T + (tqb + 1) * 512],
                          start=(kc == 0), stop=(kc == DMC - 1))
            p.waits("vector", [mt, st_done[i]])
            ev_rot[i] = p.vop("tensor_copy", out=kvstg[i][:, 0:512],
                              in_=pkx[:, 0:512])
            p.waits("gpsimd", [ev_rot[i]])
            tv = tqb * 512
            st_done[i] = p.dma(
                "gpsimd",
                kv_view[oc * 128:(oc + 1) * 128,
                        tv // 1024, tv % 1024: tv % 1024 + 512],
                kvstg[i][:, 0:512])
            p.waits("scalar", [ev_rot[i]])
            sq_done = p.sop("activation", out=sq[:], in_=pkx[:, 0:512],
                            func=AF.Square)
            p.waits("tensor", [sq_done, q2g])
            m2 = p.mm(pk2[0:2, 0:512], ones2[:], sq[:])
            p.waits("vector", [m2])
            p.vop("tensor_reduce", out=red2[:], in_=pk2[0:2, 0:512],
                  axis=mybir.AxisListType.X, op=ALU.max)
            p.vfence()
            q2g = p.vop("tensor_tensor", out=k2m[:, oc:oc + 1],
                        in0=k2m[:, oc:oc + 1], in1=red2[:], op=ALU.max)
            p.vfence()
    wtok = transpose_weight_into(wT, d["L1_attn_Wv"])
    p.barrier()
    for tc in range(T // 128):
        i = tc % 2
        p.waits("tensor", [st_done[i]])
        for ohalf in range(2):
            pwx = pw0 if ohalf == 0 else pw1
            mt = None
            for kc in range(DMC):
                mt = p.mm(pwx[:, 0:512],
                          xl[:, kc * T + tc * 128: kc * T + (tc + 1) * 128],
                          wT[:, kc * DM + ohalf * 512: kc * DM + (ohalf + 1) * 512],
                          start=(kc == 0), stop=(kc == DMC - 1))
            p.waits("vector", [mt])
            ev = p.vop("tensor_copy", out=kvstg[i][:, ohalf * 512:(ohalf + 1) * 512],
                       in_=pwx[:, 0:512])
        p.waits("gpsimd", [ev])
        st_done[i] = p.dma("gpsimd", d["d_kv_in"][2048 + tc * 128:2048 + (tc + 1) * 128, :],
                           kvstg[i][:, 0:1024])
    k2mb = cx.sget("at_k2mb", [2, 8], BF16)
    cv = p.vop("tensor_copy", out=k2mb[:], in_=k2m[:])
    p.waits("gpsimd", [cv])
    k2tok = p.dma("gpsimd", d["d_kv_in"][4096:4097, 0:16].rearrange(
        "x (a b) -> (x a) b", a=2), k2mb[:])
    p.waits("gpsimd", [st_done[0], st_done[1], k2tok])
    cctok = p.collective("AllGather", ALU.bypass,
                         [[0, 1], [2, 3], [4, 5], [6, 7]],
                         [d["d_kv_in"].opt()], [d["d_kv_out"].opt()])
    p.waits("sync", [cctok])
    t1 = p.dma("sync", k2m2[:, 0:8], d["d_kv_out"][4096:4097, 0:16].rearrange(
        "x (a b) -> (x a) b", a=2))
    t2 = p.dma("sync", k2m2[:, 8:16], d["d_kv_out"][8193:8194, 0:16].rearrange(
        "x (a b) -> (x a) b", a=2))
    p.waits("vector", [t1, t2])
    p.vop("tensor_tensor", out=k2m[:], in0=k2m2[:, 0:8], in1=k2m2[:, 8:16],
          op=ALU.max)
    p.vfence()
    p.vop("tensor_scalar", out=q2m[:], in0=q2m[:], scalar1=64.0, scalar2=None,
          op0=ALU.mult)
    p.vfence()
    mtk = p.vop("tensor_tensor", out=k2m[:], in0=k2m[:], in1=q2m[:], op=ALU.add)
    p.vfence()
    mtk2 = p.vop("tensor_scalar", out=k2m[:], in0=k2m[:], scalar1=1.0 / 16,
                 scalar2=None, op0=ALU.mult)
    p.waits("gpsimd", [mtk2])
    bt = p.dma("gpsimd", d["d_rows"][0:1, 0:16].rearrange("x (a b) -> (x a) b", a=2),
               k2m[:])
    p.waits("sync", [bt])
    bt2 = p.dma("sync", mrow[:], d["d_rows"][0:1, 0:16])
    p.waits("tensor", [bt2])
    mmt = p.mm(pk2[:, 0:16], ones1f[:], mrow[:])
    p.waits("vector", [mmt])
    negtok = p.vop("tensor_scalar", out=negM[:], in0=pk2[:, 0:16], scalar1=-1.0,
                   scalar2=None, op0=ALU.mult)
    p.vfence()
    if debug:
        p.waits("gpsimd", [negtok])
        p.dma("gpsimd", cx.dram["dbg_negM"], negM[:])
    kaux = cx.sget("at_kaux", [68, 2 * T], BF16)
    qt = cx.sget("at_qt", [68, T], BF16)
    vt_t = cx.sget("at_vt", [128, 32 * 65], BF16)
    ptile = [cx.sget(f"at_pt{i}", [128, 512], BF16) for i in range(2)]
    corr = cx.sget("at_corr", [128, 8 * 512], BF16)
    btab = cx.sget("at_btab", [128, 4 * 32], F32)
    regb = cx.sget("at_regb", [128, 128], F32)
    postk4 = cx.sget("at_postk4", [128, 128], F32)
    pp = None
    for q4 in range(4):
        pp = p.vop("tensor_copy", out=postk4[:, q4 * 32:(q4 + 1) * 32], in_=postk[:])
    rtk = p.dma("sync", rl[0:1, 0:128], d["c_regsel"])
    p.waits("tensor", [rtk])
    rbm = p.mm(pw0[:, 0:128], ones1f[:], rl[0:1, 0:128])
    p.waits("vector", [rbm])
    regtok = p.vop("tensor_copy", out=regb[:], in_=pw0[:, 0:128])
    def c_union(tqb):
        pos = []
        for qo in (0, 2048):
            for k2_ in range(4):
                tk0 = qo + tqb * 512 + k2_ * 128
                if 0 <= tk0 < 4096:
                    pos.append(tk0 // 128)
        return pos
    kv_blk = [d["d_kv_out"][r * 4097:r * 4097 + 2048, :]
              .rearrange("(f two) c -> f two c", two=2) for r in range(2)]
    vv_blk = [d["d_kv_out"][r * 4097 + 2048: r * 4097 + 4096, :] for r in range(2)]
    ctx_ev = [None, None]
    pS_ev = [None, None]
    kld = [None, None]
    for h in range(AH):
        colh = (h % 2) * 8 + h // 2
        sl_h = float(SLOPES[h])
        p.barrier()
        mz2 = p.vop("memset", vt_t[:], 1.0)
        p.waits("sync", [mz2])
        for r in range(2):
            kld[r] = p.dma("sync",
                           kaux[0:64, r * T:(r + 1) * T]
                           .rearrange("p (a b) -> p a b", b=1024),
                           kv_blk[r][h * 64:(h + 1) * 64, :, :])
        vld = p.dma("sync",
                    vt_t[:].rearrange("p (c e) -> p c e", e=65)[:, 0:16, 0:64],
                    vv_blk[0][:, h * 64:(h + 1) * 64]
                    .rearrange("(c pp2) dd -> pp2 c dd", pp2=128))
        vld = p.dma("sync",
                    vt_t[:].rearrange("p (c e) -> p c e", e=65)[:, 16:32, 0:64],
                    vv_blk[1][:, h * 64:(h + 1) * 64]
                    .rearrange("(c pp2) dd -> pp2 c dd", pp2=128))
        qcp = p.vop("tensor_copy", out=qt[0:64, :],
                    in_=Qfm[(h % 2) * 64:(h % 2) * 64 + 64,
                            (h // 2) * T:(h // 2 + 1) * T])
        qra = p.dma("sync", qt[64:68, :], d["c_qrows"][4 * h:4 * h + 4, :])
        p.vop("tensor_tensor", out=btab[:], in0=postk4[:], in1=regb[:],
              op=ALU.mult)
        p.vfence()
        bA = p.vop("tensor_scalar", out=btab[:], in0=btab[:],
                   scalar1=sl_h, scalar2=negM[:, colh:colh + 1],
                   op0=ALU.mult, op1=ALU.add)
        p.vfence()

        ctok = p.dma("sync", corr[:], d["c_corr"][h])
        seltok = None
        for tqb in range(T // 512):
            p.waits("sync", [p.tok("T")])
            seltok = p.dma("sync", kaux[64:68, :],
                           d["c_selAB"][tqb * 4:(tqb + 1) * 4, :])
            cpos = c_union(tqb)
            ic = tqb % 2
            pctx = pk2 if ic == 0 else pk3
            first = True
            amt = None
            for tkc in range(32):
                ip = tkc % 2
                pS = pk0 if ip == 0 else pk1
                ipt = tkc % 2
                p.waits("tensor", [pS_ev[ip], kld[0], kld[1], qra, seltok,
                                   qcp, ctx_ev[ic]])
                smt = p.mm(pS[:, 0:512], kaux[:, tkc * 128:(tkc + 1) * 128],
                           qt[:, tqb * 512:(tqb + 1) * 512])
                bias = btab[:, tqb * 32 + tkc: tqb * 32 + tkc + 1]
                p.waits("scalar", [smt, bA])
                pS_ev[ip] = p.sop("activation", out=ptile[ipt][:],
                                  in_=pS[:, 0:512], func=AF.Exp, bias=bias)
                ptok = pS_ev[ip]
                if tkc in cpos:
                    di = cpos.index(tkc)
                    p.waits("vector", [ptok, ctok])
                    ptok = p.vop("tensor_tensor", out=ptile[ipt][:],
                                 in0=ptile[ipt][:],
                                 in1=corr[:, di * 512:(di + 1) * 512],
                                 op=ALU.mult)
                if debug and h == 0 and tqb == 0 and tkc in (0, 3, 10, 31):
                    didx = {0: 0, 3: 1, 10: 2, 31: 3}[tkc]
                    p.waits("gpsimd", [ptok])
                    gdt = p.dma("gpsimd", cx.dram["dbg_p"][didx], ptile[ipt][:])
                    p.waits("tensor", [gdt])
                p.waits("tensor", [ptok, vld])
                amt = p.mm(pctx[0:65, 0:512], vt_t[:, tkc * 65:(tkc + 1) * 65],
                           ptile[ipt][:], start=first, stop=(tkc == 31))
                first = False
            p.waits("vector", [amt])
            if debug:
                dl = p.vop("tensor_copy", out=rlb[0:1, 0:512],
                           in_=pctx[64:65, 0:512])
                p.waits("gpsimd", [dl])
                p.dma("gpsimd", cx.dram["dbg_l"][h * 4 + tqb: h * 4 + tqb + 1, :],
                      rlb[0:1, 0:512])
            p.vop("reciprocal", out=rl[:], in_=pctx[64:65, 0:512])
            p.vfence()
            rc = p.vop("tensor_copy", out=sq[0:1, 0:512], in_=rl[:])
            p.waits("tensor", [rc])
            bmt = p.mm(pw0[0:64, 0:512], cx.sb["c_ones_1x128b"][0:1, 0:64],
                       sq[0:1, 0:512])
            p.waits("vector", [bmt])
            cpt = p.vop("tensor_copy", out=rlb[:], in_=pw0[0:64, 0:512])
            p.vfence()
            ctx_ev[ic] = p.vop(
                "tensor_tensor",
                out=ctxU[(h % 2) * 64:(h % 2) * 64 + 64,
                         (h // 2) * T + tqb * 512: (h // 2) * T + (tqb + 1) * 512],
                in0=pctx[0:64, 0:512], in1=rlb[:], op=ALU.mult)
    p.barrier()
    wotok = transpose_weight_into(wT, d["L1_attn_Wo"])
    woT = wT
    p.barrier()
    ev_rot = [None, None]
    for oc in range(DMC):
        for tqb in range(T // 512):
            i = (oc * 4 + tqb) % 2
            pkx = pk0 if i == 0 else pk1
            p.waits("tensor", [ev_rot[i]])
            mt = None
            for kc in range(DMC):
                mt = p.mm(pkx[:, 0:512],
                          woT[:, kc * DM + oc * 128: kc * DM + (oc + 1) * 128],
                          ctxU[:, kc * T + tqb * 512: kc * T + (tqb + 1) * 512],
                          start=(kc == 0), stop=(kc == DMC - 1))
            p.waits("vector", [mt, botok])
            ev_rot[i] = p.vop(
                "scalar_tensor_tensor",
                out=xr[:, oc * T + tqb * 512: oc * T + (tqb + 1) * 512],
                in0=pkx[:, 0:512], scalar=bocol[:, oc:oc + 1],
                in1=xr[:, oc * T + tqb * 512: oc * T + (tqb + 1) * 512],
                op0=ALU.add, op1=ALU.add)
    return ev_rot[i]


def build_attn_program(debug=False):
    nc = bass.Bass("TRN2", target_bir_lowering=False)
    es = ExitStack()
    p = Prog(nc, es)
    cx = Ctx(nc, p, es)
    d = cx.dram
    d["x"] = nc.dram_tensor("x", [T, DM], F32, kind="ExternalInput").ap()
    attn_dram_decls(nc, d)
    for nm, arr in host_constants().items():
        d[nm] = nc.dram_tensor(nm, list(arr.shape),
                               BF16 if arr.dtype == BF else F32,
                               kind="ExternalInput").ap()
    d["x_out"] = nc.dram_tensor("x_out", [T, DM], F32, kind="ExternalOutput").ap()
    d["d_rows"] = nc.dram_tensor("d_rows", [2, T], F32).ap()
    if debug:
        d["dbg_negM"] = nc.dram_tensor("dbg_negM", [128, 16], F32,
                                       kind="ExternalOutput").ap()
        d["dbg_l"] = nc.dram_tensor("dbg_l", [64, 512], F32,
                                    kind="ExternalOutput").ap()
        d["dbg_p"] = nc.dram_tensor("dbg_p", [4, 128, 512], BF16,
                                    kind="ExternalOutput").ap()
    with es:
        pk0 = cx.psum("pk0", [128, 512])
        pk1 = cx.psum("pk1", [128, 512])
        pk2 = cx.psum("pk2", [128, 512])
        pk3 = cx.psum("pk3", [128, 512])
        pw0 = cx.psum("pw0", [128, 1024])
        pw1 = cx.psum("pw1", [128, 1024])
        cx.sb["c_trf"] = pk0
        xr = cx.sbuf("xr", [128, DMC * T], F32)
        xl = cx.sbuf("xl", [128, DMC * T], BF16)
        ctok = load_common_consts(cx, needed=[
            "c_id128f", "c_ones_1x128b", "c_onescol_b", "c_ones_1x128f",
            "c_onescol_f", "c_ones2", "c_postk"])
        p.waits("tensor", [ctok])
        xtok = emit_load_x_fm(cx, d["x"], xr, pw0, pw1)
        p.barrier()
        lntok = emit_ln(cx, xr, xl, pk0, pk1, pk2, pk3)
        p.barrier()
        emit_attn(cx, xr, xl, 0, pk0, pk1, pk2, pk3, pw0, pw1, lntok,
                  debug=debug)
        p.barrier()
        stok = emit_store_x_fm(cx, xr, d["x_out"], pw0, pw1)
        p.waits("gpsimd", [stok])
        p.finalize()
    return nc


def mamba_dram_decls(nc, d):
    for nm, shp in [("L0_attn_W_in", [DM, PROJ]), ("L0_attn_conv_w", [XBC, D_CONV]),
                    ("L0_attn_conv_b", [XBC]), ("L0_attn_A_log", [N_HEADS]),
                    ("L0_attn_D_param", [N_HEADS]), ("L0_attn_dt_bias", [N_HEADS]),
                    ("L0_attn_norm_w", [D_INNER]), ("L0_attn_W_out", [D_INNER, DM])]:
        d[nm] = nc.dram_tensor(nm, shp, F32, kind="ExternalInput").ap()
    d["d_z"] = nc.dram_tensor("d_z", [NCHUNK, 128, D_INNER], BF16).ap()
    d["d_y"] = nc.dram_tensor("d_y", [NCHUNK, 128, D_INNER], BF16).ap()
    d["d_misc"] = nc.dram_tensor("d_misc", [4, 32, 128], BF16).ap()
    d["d_halo_in"] = nc.dram_tensor("d_halo_in", [128, 51], F32).ap()
    d["d_halo_out"] = nc.dram_tensor("d_halo_out", [256, 51], F32).ap()
    d["d_state_in"] = nc.dram_tensor("d_state_in", [64, D_INNER], F32).ap()
    d["d_state_out"] = nc.dram_tensor("d_state_out", [128, D_INNER], F32).ap()
    d["d_xfm"] = nc.dram_tensor("d_xfm", [T, DM], F32).ap()


def emit_mamba(cx, pk0, pk1, pk2, pk3, pw0, pw1):
    p = cx.p
    d = cx.dram
    idf = cx.sb["c_id128f"]
    idb = cx.sb["c_id128b"]
    ones1b = cx.sb["c_ones_1x128b"]
    onescb = cx.sb["c_onescol_b"]
    selhd = cx.sb["c_selhd"]
    lt01 = cx.sb["c_lt01"]
    o3264 = cx.sb["c_ones32x64b"]
    ones32f = cx.sb["c_ones32x128f"]
    mask64 = cx.sb["c_mask64"]
    mask128 = cx.sb["c_mask128"]
    winb = cx.sget("mb_win", [128, 8 * PROJ], BF16)
    C_sb = cx.sget("mb_C", [64, T], BF16)
    cumg = cx.sget("mb_cumg", [32, T], F32)
    state = cx.sget("mb_state", [64, D_INNER], F32)
    state_b = cx.sget("mb_stateb", [64, D_INNER], BF16)
    lamr = cx.sget("mb_lamr", [32, D_INNER], BF16)
    Acol = cx.sget("mb_Acol", [32, 1], F32)
    Dcol = cx.sget("mb_Dcol", [32, 1], F32)
    dtbc = cx.sget("mb_dtbc", [32, 1], F32)
    normw = cx.sget("mb_normw", [128, 16], F32)
    wcol = cx.sget("mb_wcol", [128, 68], F32)
    bcol = cx.sget("mb_bcol", [128, 17], F32)
    halo = cx.sget("mb_halo", [128, 51], F32)
    hin = cx.sget("mb_hin", [64, D_INNER], F32)
    stage = cx.sget("mb_stage", [128, 2128], F32)
    xbc = cx.sget("mb_xbc", [128, 17 * 131], F32)
    cnv = cx.sget("mb_cnv", [128, 17 * 128], F32)
    xcv = cx.sget("mb_xcv", [128, 16 * 128], BF16)
    xcvT = cx.sget("mb_xcvT", [128, D_INNER], BF16)
    xtid = cx.sget("mb_xtid", [128, D_INNER], BF16)
    zstg = cx.sget("mb_zstg", [128, 16 * 128], BF16)
    ystg = [cx.sget(f"mb_ystg{i}", [128, 512], BF16) for i in range(2)]
    dt_c = cx.sget("mb_dt", [32, 128], F32)
    dtr = cx.sget("mb_dtr", [32, 128], F32)
    acl = cx.sget("mb_acl", [32, 128], F32)
    asc = [cx.sget(f"mb_asc{i}", [32, 128], F32) for i in range(3)]
    ahl = [cx.sget(f"mb_ahl{i}", [32, 128], BF16) for i in range(3)]
    sel32 = cx.sb["c_sel32"]
    colT = cx.sget("mb_colT", [128, 128], F32)
    B_c = cx.sget("mb_Bc", [64, 128], BF16)
    BT = cx.sget("mb_BT", [128, 64], BF16)
    M0LT = cx.sget("mb_M0LT", [128, 128], BF16)
    Dtmp = [cx.sget(f"mb_Dtmp{i}", [128, 128], F32) for i in range(2)]
    Mh = [cx.sget(f"mb_Mh{i}", [128, 128], BF16) for i in range(2)]
    Cdd = [cx.sget(f"mb_Cdd{i}", [64, 128], BF16) for i in range(2)]
    Bpt = [cx.sget(f"mb_Bpt{i}", [128, 64], BF16) for i in range(2)]
    sttmp = cx.sget("mb_sttmp", [64, 512], F32)
    ln3 = cx.sget("mb_ln3", [128, 8 * 128], BF16)
    x2c = cx.sget("mb_x2c", [128, 8 * 128], BF16)
    xc = cx.sget("mb_xc", [128, 8 * 128], F32)
    xtm_stg = cx.sget("mb_xtm", [128, DM], F32)
    cx.sb["c_trf2"] = pk3
    xfm = d["d_xfm"]
    srow = cx.sget("mb_srow", [1, 128], F32)
    srow2 = cx.sget("mb_srow2", [1, 128], F32)
    mrb = cx.sget("mb_mrb", [1, 128], BF16)
    rrb = cx.sget("mb_rrb", [1, 128], BF16)
    ytot = cx.sget("mb_ytot", [128, D_INNER], BF16)
    yfm = cx.sget("mb_yfm", [128, D_INNER], BF16)
    gf = cx.sget("mb_gf", [128, D_INNER], BF16)

    t1 = p.dma("sync", Acol[:], d["L0_attn_A_log"].rearrange("(a b) -> a b", b=1))
    t2 = p.dma("sync", Dcol[:], d["L0_attn_D_param"].rearrange("(a b) -> a b", b=1))
    t3 = p.dma("sync", dtbc[:], d["L0_attn_dt_bias"].rearrange("(a b) -> a b", b=1))
    t4 = p.dma("sync", normw[:], d["L0_attn_norm_w"].rearrange("(c p) -> p c", p=128),
               allow_slow_non_contiguous=True)
    t5 = p.dma("sync", wcol[:].rearrange("p (c k) -> p c k", k=4),
               d["L0_attn_conv_w"].rearrange("(c p) k -> p c k", p=128),
               allow_slow_non_contiguous=True)
    t6 = p.dma("sync", bcol[:], d["L0_attn_conv_b"].rearrange("(c p) -> p c", p=128),
               allow_slow_non_contiguous=True)
    p.waits("scalar", [t1])
    st = p.sop("activation", out=Acol[:], in_=Acol[:], func=AF.Exp)
    p.waits("vector", [st])
    atok = p.vop("tensor_scalar", out=Acol[:], in0=Acol[:], scalar1=-1.0,
                 scalar2=None, op0=ALU.mult)
    wtoks = []
    cast_tok = None
    for kc in range(DMC):
        for hh in range(2):
            lt = p.dma("sync", stage[:, 0:2128] if hh == 0 else stage[:, 0:2128],
                       d["L0_attn_W_in"][kc * 128:(kc + 1) * 128,
                                         hh * 2128:(hh + 1) * 2128])
            p.waits("vector", [lt, cast_tok])
            cast_tok = p.vop("tensor_copy",
                             out=winb[:, kc * PROJ + hh * 2128: kc * PROJ + (hh + 1) * 2128],
                             in_=stage[:, 0:2128])
            p.waits("sync", [cast_tok])
    win_tok = cast_tok

    def load_xc_fm(tc):
        p.waits("gpsimd", [p.tok("T"), p.tok("V"), p.tok("S")])
        ldx = p.dma("gpsimd", xtm_stg[:], xfm[tc * 128:(tc + 1) * 128, :])
        p.waits("tensor", [ldx])
        for c in range(DMC):
            p.tr(pk0[:, 0:128] if c % 2 == 0 else pk1[:, 0:128],
                 xtm_stg[:, c * 128:(c + 1) * 128], idf[:])
            tt = p.trfence(cx.sb["c_trf2"], cx.sb["c_ones_1x128b"])
            p.waits("vector", [tt])
            ev = p.vop("tensor_copy", out=xc[:, c * 128:(c + 1) * 128],
                       in_=(pk0 if c % 2 == 0 else pk1)[:, 0:128])
            p.waits("tensor", [ev])
        p.vfence()
        return p.tok("V")

    def ln_chunk(tc, out_tile, ncols=128, col0=0):
        xsl = xc[:, 0:DMC * 128].rearrange(
            "p (c t) -> p c t", t=128)[:, :, col0:col0 + ncols]
        ot = out_tile[:, 0:DMC * ncols].rearrange("p (c t) -> p c t", t=ncols)
        x2 = x2c[:, 0:DMC * ncols].rearrange("p (c t) -> p c t", t=ncols)
        p.waits("scalar", [p.tok("V")])
        ct = p.vop("tensor_copy", out=ot, in_=xsl)
        st = p.sop("activation", out=x2, in_=xsl, func=AF.Square)
        p.waits("tensor", [ct, st])
        for c in range(DMC):
            p.mm(pk0[0:1, 0:ncols], onescb[:],
                 out_tile[:, c * ncols:(c + 1) * ncols],
                 start=(c == 0), stop=(c == DMC - 1))
        for c in range(DMC):
            mt = p.mm(pk1[0:1, 0:ncols], onescb[:],
                      x2c[:, c * ncols:(c + 1) * ncols],
                      start=(c == 0), stop=(c == DMC - 1))
        p.waits("vector", [mt])
        p.vop("tensor_scalar", out=srow[0:1, 0:ncols], in0=pk0[0:1, 0:ncols],
              scalar1=1.0 / DM, scalar2=None, op0=ALU.mult)
        v1 = p.vop("tensor_scalar", out=srow2[0:1, 0:ncols], in0=pk1[0:1, 0:ncols],
                   scalar1=1.0 / DM, scalar2=EPS, op0=ALU.mult, op1=ALU.add)
        p.vfence()
        p.vop("tensor_copy", out=mrb[0:1, 0:ncols], in_=srow[0:1, 0:ncols])
        p.vop("tensor_tensor", out=srow[0:1, 0:ncols], in0=srow[0:1, 0:ncols],
              in1=srow[0:1, 0:ncols], op=ALU.mult)
        p.vfence()
        p.vop("tensor_tensor", out=srow2[0:1, 0:ncols], in0=srow2[0:1, 0:ncols],
              in1=srow[0:1, 0:ncols], op=ALU.subtract)
        p.vfence()
        v2 = p.vop("reciprocal", out=srow2[0:1, 0:ncols], in_=srow2[0:1, 0:ncols])
        p.vfence()
        p.waits("scalar", [v2])
        s2 = p.sop("activation", out=rrb[0:1, 0:ncols], in_=srow2[0:1, 0:ncols],
                   func=AF.Sqrt)
        p.waits("tensor", [s2, p.tok("V")])
        p.mm(pk0[:, 0:ncols], ones1b[:], mrb[0:1, 0:ncols])
        bm = p.mm(pk1[:, 0:ncols], ones1b[:], rrb[0:1, 0:ncols])
        p.waits("vector", [bm])
        mb = pk0[:, 0:ncols].rearrange("p (o t) -> p o t", o=1).broadcast_to(
            [128, DMC, ncols])
        rb = pk1[:, 0:ncols].rearrange("p (o t) -> p o t", o=1).broadcast_to(
            [128, DMC, ncols])
        p.vop("tensor_tensor", out=ot, in0=xsl, in1=mb, op=ALU.subtract)
        p.vfence()
        vt = p.vop("tensor_tensor", out=ot, in0=ot, in1=rb, op=ALU.mult)
        p.vfence()
        return vt

    load_xc_fm(15)
    lt3 = ln_chunk(15, ln3, ncols=3, col0=125)
    p.waits("tensor", [lt3, win_tok])
    hm = None
    for s in range(17):
        for kc in range(DMC):
            hm = p.mm(pk2[:, 0:3],
                      winb[:, kc * PROJ + D_INNER + s * 128:
                           kc * PROJ + D_INNER + (s + 1) * 128],
                      ln3[:, kc * 3:(kc + 1) * 3],
                      start=(kc == 0), stop=(kc == DMC - 1))
        p.waits("vector", [hm])
        hv = p.vop("tensor_copy", out=halo[:, s * 3:(s + 1) * 3], in_=pk2[:, 0:3])
        p.waits("tensor", [hv])
    p.waits("gpsimd", [hv])
    hst = p.dma("gpsimd", d["d_halo_in"], halo[:])
    p.waits("gpsimd", [hst])
    cc1 = p.collective("AllGather", ALU.bypass, [[0, 1], [2, 3], [4, 5], [6, 7]],
                       [d["d_halo_in"].opt()], [d["d_halo_out"].opt()])
    p.waits("sync", [cc1])
    hld = p.dma("sync", halo[:], d["d_halo_out"][0:128, :])
    p.waits("vector", [hld])
    htok = p.vop("tensor_scalar", out=halo[:], in0=halo[:], scalar1=mask128[:],
                 scalar2=None, op0=ALU.mult)
    p.vop("memset", state[:], 0.0)
    sz = p.vop("memset", state_b[:], 0.0)
    p.vfence()

    prev_z_st = [None, None]
    y_st = [None, None]
    for k in range(NCHUNK):
        p.barrier()
        load_xc_fm(k)
        lnt = ln_chunk(k, ln3, ncols=128)
        p.waits("tensor", [lnt])
        for oz in range(16):
            pkx = pk0 if oz % 2 == 0 else pk1
            mt = None
            for kc in range(DMC):
                mt = p.mm(pkx[:, 0:128],
                          winb[:, kc * PROJ + oz * 128: kc * PROJ + (oz + 1) * 128],
                          ln3[:, kc * 128:(kc + 1) * 128],
                          start=(kc == 0), stop=(kc == DMC - 1))
            p.waits("vector", [mt] + ([prev_z_st[0]] if oz == 0 else []))
            zv = p.vop("tensor_copy", out=zstg[:, oz * 128:(oz + 1) * 128],
                       in_=pkx[:, 0:128])
            p.waits("tensor", [zv])
        p.waits("gpsimd", [zv])
        zst = p.dma("gpsimd", d["d_z"][k], zstg[:])
        prev_z_st[0] = zst
        xv = p.vop("tensor_copy",
                   out=xbc[:].rearrange("p (s w) -> p s w", w=131)[:, :, 0:3],
                   in_=(halo[:].rearrange("p (s w) -> p s w", w=3) if k == 0 else
                        xbc[:].rearrange("p (s w) -> p s w", w=131)[:, :, 128:131]))
        p.vfence()
        p.waits("tensor", [xv, htok])
        for s in range(17):
            pkx = pk0 if s % 2 == 0 else pk1
            mt = None
            for kc in range(DMC):
                mt = p.mm(pkx[:, 0:128],
                          winb[:, kc * PROJ + D_INNER + s * 128:
                               kc * PROJ + D_INNER + (s + 1) * 128],
                          ln3[:, kc * 128:(kc + 1) * 128],
                          start=(kc == 0), stop=(kc == DMC - 1))
            p.waits("vector", [mt])
            xv = p.vop("tensor_copy", out=xbc[:, s * 131 + 3:(s + 1) * 131],
                       in_=pkx[:, 0:128])
            p.waits("tensor", [xv])
        mt = None
        for kc in range(DMC):
            mt = p.mm(pk2[0:32, 0:128],
                      winb[:, kc * PROJ + D_INNER + XBC: kc * PROJ + PROJ],
                      ln3[:, kc * 128:(kc + 1) * 128],
                      start=(kc == 0), stop=(kc == DMC - 1))
        p.waits("vector", [xv, mt])
        for s in range(17):
            p.vop("tensor_scalar", out=cnv[:, s * 128:(s + 1) * 128],
                  in0=xbc[:, s * 131 + 0: s * 131 + 128],
                  scalar1=wcol[:, s * 4: s * 4 + 1], scalar2=None, op0=ALU.mult)
        p.vfence()
        for tap in range(1, 4):
            for s in range(17):
                p.vop("scalar_tensor_tensor", out=cnv[:, s * 128:(s + 1) * 128],
                      in0=xbc[:, s * 131 + tap: s * 131 + tap + 128],
                      scalar=wcol[:, s * 4 + tap: s * 4 + tap + 1],
                      in1=cnv[:, s * 128:(s + 1) * 128],
                      op0=ALU.mult, op1=ALU.add)
            p.vfence()
        cvt = p.tok("V")
        dv = p.vop("tensor_scalar", out=dtr[:], in0=pk2[0:32, 0:128],
                   scalar1=dtbc[:], scalar2=None, op0=ALU.add)
        p.vfence()
        p.waits("scalar", [p.tok("V")])
        p.sop("activation", out=dt_c[:], in_=dtr[:], func=AF.Abs)
        p.sop("activation", out=acl[:], in_=dt_c[:], func=AF.Exp, scale=-1.0)
        sv = p.sop("activation", out=acl[:], in_=acl[:], func=AF.Ln, bias=1.0)
        p.waits("vector", [sv])
        p.vop("tensor_scalar_max", out=dtr[:], in0=dtr[:], scalar1=0.0)
        p.vfence()
        dtk = p.vop("tensor_tensor", out=dt_c[:], in0=dtr[:], in1=acl[:],
                    op=ALU.add)
        p.vfence()
        sc = p.vop("tensor_tensor_scan", out=cumg[:, k * 128:(k + 1) * 128],
                   data0=ones32f[:], data1=dt_c[:],
                   initial=(0.0 if k == 0 else cumg[:, k * 128 - 1:k * 128]),
                   op0=ALU.mult, op1=ALU.add)
        p.vfence()
        if k == 0:
            av = p.vop("tensor_scalar", out=acl[:], in0=cumg[:, 0:128],
                       scalar1=Acol[:], scalar2=None, op0=ALU.mult)
        else:
            av = p.vop("tensor_scalar", out=acl[:], in0=cumg[:, k * 128:(k + 1) * 128],
                       scalar1=cumg[:, k * 128 - 1:k * 128], scalar2=Acol[:],
                       op0=ALU.subtract, op1=ALU.mult)
        p.vfence()
        p.waits("scalar", [cvt, p.tok("V")])
        svt = None
        for s in range(17):
            if s < 16:
                svt = p.sop("activation", out=xcv[:, s * 128:(s + 1) * 128],
                            in_=cnv[:, s * 128:(s + 1) * 128], func=AF.Silu,
                            bias=bcol[:, s:s + 1])
            else:
                svt = p.sop("activation", out=cnv[:, s * 128:(s + 1) * 128],
                            in_=cnv[:, s * 128:(s + 1) * 128], func=AF.Silu,
                            bias=bcol[:, s:s + 1])
        p.waits("vector", [svt])
        bv = p.vop("tensor_copy", out=B_c[:], in_=cnv[0:64, 16 * 128:17 * 128])
        cv = p.vop("tensor_copy", out=C_sb[:, k * 128:(k + 1) * 128],
                   in_=cnv[64:128, 16 * 128:17 * 128])
        p.vop("tensor_copy", out=ahl[0][:], in_=acl[:])
        p.vfence()
        p.vop("tensor_tensor", out=asc[0][:], in0=acl[:], in1=ahl[0][:],
              op=ALU.subtract)
        p.vfence()
        p.vop("tensor_copy", out=ahl[1][:], in_=asc[0][:])
        p.vop("tensor_scalar", out=asc[1][:], in0=acl[:],
              scalar1=acl[:, 127:128], scalar2=-1.0,
              op0=ALU.subtract, op1=ALU.mult)
        p.vfence()
        p.waits("scalar", [p.tok("V")])
        se = p.sop("activation", out=asc[1][:], in_=asc[1][:], func=AF.Exp)
        se2 = p.sop("activation", out=asc[2][:], in_=acl[:], func=AF.Exp)
        p.waits("vector", [se, se2])
        fcb = p.vop("tensor_copy", out=ahl[2][:], in_=asc[2][:])
        lv = p.vop("tensor_scalar", out=lamr[:], in0=selhd[:],
                   scalar1=asc[2][:, 127:128], scalar2=None, op0=ALU.mult)
        rv = p.vop("reciprocal", out=dtr[:], in_=dt_c[:])
        p.vfence()
        rv2 = p.vop("tensor_scalar", out=dtr[:], in0=dtr[:], scalar1=Dcol[:],
                    scalar2=None, op0=ALU.mult)
        p.vfence()
        p.waits("tensor", [p.tok("V"), p.tok("S")])
        tt1 = p.tr(pk0[:, 0:32], acl[:], idf[0:32, 0:32])
        tt2 = p.tr(pk1[:, 0:32], dtr[:], idf[0:32, 0:32])
        p.waits("vector", [tt2])
        p.vop("tensor_copy", out=colT[:, 0:32], in_=pk0[:, 0:32])
        e1 = p.vop("tensor_copy", out=colT[:, 32:64], in_=pk1[:, 0:32])
        p.waits("tensor", [e1])
        tt3 = p.tr(pk0[:, 0:32], asc[1][:], idf[0:32, 0:32])
        tt4 = p.tr(pk1[:].bitcast(BF16)[:, 0:64], B_c[:], idb[0:64, 0:64])
        p.waits("vector", [tt4])
        p.vop("tensor_copy", out=colT[:, 64:96], in_=pk0[:, 0:32])
        e2 = p.vop("tensor_copy", out=BT[:], in_=pk1[:].bitcast(BF16)[:, 0:64])
        p.waits("tensor", [e2])
        tt5 = p.tr(pk0[:, 0:32], dt_c[:], idf[0:32, 0:32])
        p.waits("vector", [tt5])
        e2b = p.vop("tensor_copy", out=colT[:, 96:128], in_=pk0[:, 0:32])
        p.waits("tensor", [e2b])
        for bb in range(2):
            pwx = pw0 if bb == 0 else pw1
            pwb = pwx[:].bitcast(BF16)
            ttl = None
            for cc2 in range(8):
                ttl = p.tr(pwb[:, cc2 * 128:(cc2 + 1) * 128],
                           xcv[:, (bb * 8 + cc2) * 128:(bb * 8 + cc2 + 1) * 128],
                           idb[:])
            p.waits("vector", [ttl])
            e3 = p.vop("tensor_copy", out=xcvT[:, bb * 1024:(bb + 1) * 1024],
                       in_=pwb[:, 0:1024])
        p.vfence()
        xt = p.vop("tensor_tensor",
                   out=xtid[:].rearrange("p (h dd) -> p h dd", dd=64),
                   in0=xcvT[:].rearrange("p (h dd) -> p h dd", dd=64),
                   in1=colT[:, 96:128]
                   .rearrange("p (h o) -> p h o", o=1).broadcast_to([128, 32, 64]),
                   op=ALU.mult)
        p.vfence()
        p.waits("tensor", [bv, cv, p.tok("V")])
        m0 = p.mm(pk2[:, 0:128], B_c[:], C_sb[:, k * 128:(k + 1) * 128])
        p.waits("vector", [m0])
        m0e = p.vop("tensor_tensor", out=M0LT[:], in0=pk2[:, 0:128], in1=lt01[:],
                    op=ALU.mult)
        p.vfence()
        f1 = f2 = f3 = p.tok("V")
        ps_d = [pk0, pk1]
        dex_tok = [None, None]
        mh_tok = [None, None]
        cdd_tok = [None, None]
        bpt_tok = [None, None]
        yev = None
        dlast = None
        for hg in range(4):
            for hh in range(8):
                h = hg * 8 + hh
                i2 = h % 2
                pD = ps_d[i2]
                p.waits("tensor", [f1, f2, dex_tok[i2]])
                p.mm(pD[:, 0:128], sel32[:, h * 128:(h + 1) * 128], ahl[0][:],
                     start=True, stop=False)
                bm2 = p.mm(pD[:, 0:128], sel32[:, h * 128:(h + 1) * 128],
                           ahl[1][:], start=False, stop=True)
                p.waits("vector", [bm2, mh_tok[i2]])
                p.vop("tensor_scalar", out=Dtmp[i2][:], in0=pD[:, 0:128],
                      scalar1=colT[:, h:h + 1], scalar2=None, op0=ALU.subtract)
                p.vfence()
                dv2 = p.vop("tensor_scalar_min", out=Dtmp[i2][:],
                            in0=Dtmp[i2][:], scalar1=0.0)
                p.vfence()
                dex_tok[i2] = dv2
                p.waits("scalar", [dv2])
                sx = p.sop("activation", out=Dtmp[i2][:], in_=Dtmp[i2][:],
                           func=AF.Exp)
                p.waits("vector", [sx, m0e])
                p.vop("tensor_tensor", out=Mh[i2][:], in0=Dtmp[i2][:],
                      in1=M0LT[:], op=ALU.mult)
                p.vfence()
                mh_tok[i2] = p.vop("scalar_tensor_tensor", out=Mh[i2][:],
                                   in0=idb[:], scalar=colT[:, 32 + h:33 + h],
                                   in1=Mh[i2][:], op0=ALU.mult, op1=ALU.add)
                p.vfence()
                p.waits("tensor", [mh_tok[i2], xt, yev])
                p.mm(pw0[:, hh * 64:(hh + 1) * 64], Mh[i2][:],
                     xtid[:, h * 64:(h + 1) * 64], start=True, stop=False)
                p.waits("tensor", [f3, cdd_tok[0], cdd_tok[1], p.tok("V")])
                cb = p.mm(pk3[0:64, 0:128], sel32[:, h * 128:h * 128 + 64],
                          ahl[2][:])
                p.waits("vector", [cb])
                cdd_tok[i2] = p.vop("tensor_tensor", out=Cdd[i2][:],
                                    in0=pk3[0:64, 0:128],
                                    in1=C_sb[:, k * 128:(k + 1) * 128],
                                    op=ALU.mult)
                p.vfence()
                p.waits("tensor", [cdd_tok[i2], sz])
                ym = p.mm(pw0[:, hh * 64:(hh + 1) * 64], Cdd[i2][:],
                          state_b[:, h * 64:(h + 1) * 64], start=False, stop=True)
                p.waits("vector", [e2, bpt_tok[i2]])
                bp = p.vop("tensor_scalar", out=Bpt[i2][:], in0=BT[:],
                           scalar1=colT[:, 64 + h:65 + h], scalar2=None,
                           op0=ALU.mult)
                p.vfence()
                p.waits("tensor", [bp, dlast])
                bpt_tok[i2] = dm = p.mm(pw1[0:64, hh * 64:(hh + 1) * 64],
                                        Bpt[i2][:], xtid[:, h * 64:(h + 1) * 64])
            p.waits("vector", [ym, y_st[hg % 2]])
            yev = p.vop("tensor_copy", out=ystg[hg % 2][:], in_=pw0[:, 0:512])
            p.waits("gpsimd", [yev])
            y_st[hg % 2] = p.dma("gpsimd", d["d_y"][k][:, hg * 512:(hg + 1) * 512],
                                 ystg[hg % 2][:])
            p.waits("tensor", [yev, lv])
            lb = p.mm(pk3[0:64, 0:512], o3264[:],
                      lamr[:, hg * 512:(hg + 1) * 512])
            p.waits("vector", [lb, dm])
            p.vop("tensor_tensor", out=sttmp[:], in0=state[:, hg * 512:(hg + 1) * 512],
                  in1=pk3[0:64, 0:512], op=ALU.mult)
            p.vfence()
            dlast = p.vop("tensor_tensor", out=state[:, hg * 512:(hg + 1) * 512],
                          in0=sttmp[:], in1=pw1[0:64, 0:512], op=ALU.add)
            p.vfence()
            sz = p.vop("tensor_copy", out=state_b[:, hg * 512:(hg + 1) * 512],
                       in_=state[:, hg * 512:(hg + 1) * 512])
            p.vfence()
    p.barrier()
    sst = p.dma("gpsimd", d["d_state_in"], state[:])
    p.waits("gpsimd", [sst])
    cc2 = p.collective("AllGather", ALU.bypass, [[0, 1], [2, 3], [4, 5], [6, 7]],
                       [d["d_state_in"].opt()], [d["d_state_out"].opt()])
    p.waits("sync", [cc2])
    hld2 = p.dma("sync", hin[:], d["d_state_out"][0:64, :])
    p.waits("vector", [hld2])
    p.vop("tensor_scalar", out=hin[:], in0=hin[:], scalar1=mask64[:],
          scalar2=None, op0=ALU.mult)
    p.vfence()
    hintok = p.vop("tensor_copy", out=state_b[:], in_=hin[:])
    p.vfence()
    cast_tok = None
    for kc in range(16):
        lt = p.dma("sync", stage[:, 0:1024],
                   d["L0_attn_W_out"][kc * 128:(kc + 1) * 128, :])
        p.waits("vector", [lt, cast_tok])
        cast_tok = p.vop("tensor_copy", out=winb[:, kc * 1024:(kc + 1) * 1024],
                         in_=stage[:, 0:1024])
        p.waits("sync", [cast_tok])
    wout_tok = cast_tok
    p.barrier()
    for k in range(NCHUNK):
        p.barrier()
        av = p.vop("tensor_scalar", out=acl[:], in0=cumg[:, k * 128:(k + 1) * 128],
                   scalar1=Acol[:], scalar2=None, op0=ALU.mult)
        p.vfence()
        p.waits("scalar", [av])
        se = p.sop("activation", out=asc[2][:], in_=acl[:], func=AF.Exp)
        p.waits("vector", [se])
        f3 = fcb = p.vop("tensor_copy", out=ahl[2][:], in_=asc[2][:])
        yld = p.dma("gpsimd", ytot[:], d["d_y"][k])
        zld = p.dma("gpsimd", zstg[:], d["d_z"][k])
        cdd_tok = [None, None]
        for h in range(N_HEADS):
            i2 = h % 2
            p.waits("tensor", [f3, cdd_tok[0], cdd_tok[1], p.tok("V")])
            cb = p.mm(pk3[0:64, 0:128], sel32[:, h * 128:h * 128 + 64],
                      ahl[2][:])
            p.waits("vector", [cb])
            cdd_tok[i2] = p.vop("tensor_tensor", out=Cdd[i2][:],
                                in0=pk3[0:64, 0:128],
                                in1=C_sb[:, k * 128:(k + 1) * 128], op=ALU.mult)
            p.vfence()
            p.waits("tensor", [cdd_tok[i2], hintok])
            pwx = pw0 if h < 16 else pw1
            cm = p.mm(pwx[:, (h % 16) * 64:((h % 16) + 1) * 64], Cdd[i2][:],
                      state_b[:, h * 64:(h + 1) * 64])
        p.waits("vector", [cm, yld])
        for half in range(2):
            pwx = pw0 if half == 0 else pw1
            p.vop("tensor_tensor", out=ytot[:, half * 1024:(half + 1) * 1024],
                  in0=ytot[:, half * 1024:(half + 1) * 1024],
                  in1=pwx[:, 0:1024], op=ALU.add)
        p.vfence()
        yt2 = p.tok("V")
        p.waits("tensor", [yt2])
        for bb in range(2):
            pwx = pw0 if bb == 0 else pw1
            pwb = pwx[:].bitcast(BF16)
            ttl = None
            for cc3 in range(8):
                ttl = p.tr(pwb[:, cc3 * 128:(cc3 + 1) * 128],
                           ytot[:, (bb * 8 + cc3) * 128:(bb * 8 + cc3 + 1) * 128],
                           idb[:])
            p.waits("vector", [ttl])
            p.vop("tensor_copy", out=yfm[:, bb * 1024:(bb + 1) * 1024],
                  in_=pwb[:, 0:1024])
        p.vfence()
        p.waits("scalar", [zld, p.tok("V")])
        szt = p.sop("activation", out=zstg[:], in_=zstg[:], func=AF.Silu)
        p.waits("vector", [szt])
        gt = p.vop("tensor_tensor", out=gf[:], in0=yfm[:], in1=zstg[:],
                   op=ALU.mult)
        p.vfence()
        p.waits("scalar", [gt])
        g2t = p.sop("activation", out=ytot[:], in_=gf[:], func=AF.Square)
        p.waits("tensor", [g2t])
        mt = None
        for c in range(16):
            mt = p.mm(pk2[0:1, 0:128], onescb[:], ytot[:, c * 128:(c + 1) * 128],
                      start=(c == 0), stop=(c == 15))
        p.waits("vector", [mt])
        p.vop("tensor_scalar", out=srow[:], in0=pk2[0:1, 0:128],
              scalar1=1.0 / D_INNER, scalar2=EPS, op0=ALU.mult, op1=ALU.add)
        p.vfence()
        rv = p.vop("reciprocal", out=srow2[:], in_=srow[:])
        p.vfence()
        p.waits("scalar", [rv])
        s2 = p.sop("activation", out=rrb[:], in_=srow2[:], func=AF.Sqrt)
        p.waits("tensor", [s2])
        bm = p.mm(pk3[:, 0:128], ones1b[:], rrb[:])
        p.waits("vector", [bm])
        yn = None
        for c in range(16):
            yn = p.vop("scalar_tensor_tensor", out=ytot[:, c * 128:(c + 1) * 128],
                       in0=gf[:, c * 128:(c + 1) * 128],
                       scalar=normw[:, c:c + 1], in1=pk3[:, 0:128],
                       op0=ALU.mult, op1=ALU.mult)
        p.vfence()
        xctok = load_xc_fm(k)
        p.waits("tensor", [yn, wout_tok])
        ev_rot = [None, None]
        for oc in range(DMC):
            i2 = oc % 2
            pkx = pk0 if i2 == 0 else pk1
            p.waits("tensor", [ev_rot[i2], xctok])
            mt = None
            for kc in range(16):
                mt = p.mm(pkx[:, 0:128],
                          winb[:, kc * 1024 + oc * 128: kc * 1024 + (oc + 1) * 128],
                          ytot[:, kc * 128:(kc + 1) * 128],
                          start=(kc == 0), stop=(kc == 15))
            p.waits("vector", [mt])
            ev_rot[i2] = p.vop("tensor_tensor",
                               out=xc[:, oc * 128:(oc + 1) * 128],
                               in0=xc[:, oc * 128:(oc + 1) * 128],
                               in1=pkx[:, 0:128], op=ALU.add)
        p.waits("tensor", [ev_rot[0], ev_rot[1]])
        for c in range(DMC):
            p.tr(pk0[:, 0:128] if c % 2 == 0 else pk1[:, 0:128],
                 xc[:, c * 128:(c + 1) * 128], idf[:])
            tt = p.trfence(cx.sb["c_trf2"], cx.sb["c_ones_1x128b"])
            p.waits("vector", [tt])
            ev = p.vop("tensor_copy", out=xtm_stg[:, c * 128:(c + 1) * 128],
                       in_=(pk0 if c % 2 == 0 else pk1)[:, 0:128])
            p.waits("tensor", [ev])
        nv = p.vop("engine_nop")
        p.waits("gpsimd", [nv])
        xst = p.dma("gpsimd", xfm[k * 128:(k + 1) * 128, :], xtm_stg[:])
    p.waits("gpsimd", [xst])
    return xst


def build_mamba_program():
    nc = bass.Bass("TRN2", target_bir_lowering=False)
    es = ExitStack()
    p = Prog(nc, es)
    cx = Ctx(nc, p, es)
    d = cx.dram
    d["x"] = nc.dram_tensor("x", [T, DM], F32, kind="ExternalInput").ap()
    mamba_dram_decls(nc, d)
    for nm, arr in host_constants().items():
        d[nm] = nc.dram_tensor(nm, list(arr.shape),
                               BF16 if arr.dtype == BF else F32,
                               kind="ExternalInput").ap()
    d["c_mask64"] = nc.dram_tensor("c_mask64", [64, 1], F32,
                                   kind="ExternalInput").ap()
    d["c_mask128"] = nc.dram_tensor("c_mask128", [128, 1], F32,
                                    kind="ExternalInput").ap()
    d["x_out"] = nc.dram_tensor("x_out", [T, DM], F32, kind="ExternalOutput").ap()
    with es:
        pk0 = cx.psum("pk0", [128, 512])
        pk1 = cx.psum("pk1", [128, 512])
        pk2 = cx.psum("pk2", [128, 512])
        pk3 = cx.psum("pk3", [128, 512])
        pw0 = cx.psum("pw0", [128, 1024])
        pw1 = cx.psum("pw1", [128, 1024])
        cx.sb["c_trf"] = pk0
        ctok = load_common_consts(cx, needed=[
            "c_id128f", "c_id128b", "c_ones_1x128b", "c_onescol_b",
            "c_selhd", "c_lt01", "c_ones32x64b", "c_ones32x128f",
            "c_sel32", "c_mask64", "c_mask128"])
        p.waits("tensor", [ctok])
        cx.sb["c_trf"] = pk0
        emit_tm_copy(cx, d["x"], d["d_xfm"])
        p.barrier()
        emit_mamba(cx, pk0, pk1, pk2, pk3, pw0, pw1)
        p.barrier()
        stok = emit_tm_copy(cx, d["d_xfm"], d["x_out"])
        p.waits("gpsimd", [stok])
        p.finalize()
    return nc


def emit_tm_copy(cx, src_tm, dst_tm, qname="gpsimd"):
    p = cx.p
    stg = [cx.sget(f"xio_stg{i}", [128, DM], F32) for i in range(2)]
    ld = [None, None]
    st = [None, None]
    gtok = None
    for tc in range(T // 128):
        i = tc % 2
        p.waits(qname, [st[i]])
        ld[i] = p.dma(qname, stg[i][:], src_tm[tc * 128:(tc + 1) * 128, :])
        p.waits(qname, [ld[i]])
        st[i] = gtok = p.dma(qname, dst_tm[tc * 128:(tc + 1) * 128, :], stg[i][:])
    return gtok


def emit_x_tm_to_dram_fm(cx, x_tm_dram, d_xfm, pw0, pw1):
    p = cx.p
    stg = [cx.sget(f"xio_stg{i}", [128, DM], F32) for i in range(2)]
    stg2 = [cx.sget(f"xio2_stg{i}", [128, DM], F32) for i in range(2)]
    idf = cx.sb["c_id128f"]
    pws = [pw0, pw1]
    ev_tok = [None, None]
    ld_tok = [None, None]
    tr_tok = [None, None]
    st_tok = [None, None]
    gtok = None
    for tc in range(T // 128):
        i = tc % 2
        p.waits("sync", [tr_tok[i]])
        ld_tok[i] = p.dma("sync", stg[i][:], x_tm_dram[tc * 128:(tc + 1) * 128, :])
        p.waits("tensor", [ld_tok[i], ev_tok[i]])
        for c in range(DMC):
            tr_tok[i] = p.tr(pws[i][:, c * 128:(c + 1) * 128],
                             stg[i][:, c * 128:(c + 1) * 128], idf[:])
        tr_tok[i] = p.trfence(cx.sb["c_trf"], cx.sb["c_ones_1x128b"])
        p.waits("vector", [tr_tok[i], st_tok[i]])
        p.vop("tensor_copy", out=stg2[i][:], in_=pws[i][:])
        ev_tok[i] = p.vop("engine_nop")
        p.waits("gpsimd", [ev_tok[i]])
        st_tok[i] = gtok = p.dma(
            "gpsimd", d_xfm[:, :, tc * 128:(tc + 1) * 128],
            stg2[i][:].rearrange("p (c t) -> p c t", t=128))
    return gtok


def emit_dram_fm_to_x_tm(cx, d_xfm, x_tm_dram, pw0, pw1, paranoid=False):
    p = cx.p
    stg = [cx.sget(f"xio_stg{i}", [128, DM], F32) for i in range(2)]
    stg2 = [cx.sget(f"xio2_stg{i}", [128, DM], F32) for i in range(2)]
    idf = cx.sb["c_id128f"]
    pws = [pw0, pw1]
    ev_tok = [None, None]
    ld_tok = [None, None]
    tr_tok = [None, None]
    st_tok = [None, None]
    gtok = None
    for tc in range(T // 128):
        i = tc % 2
        if paranoid:
            p.barrier()
        p.waits("gpsimd", [tr_tok[i]])
        ld_tok[i] = p.dma("gpsimd",
                          stg[i][:].rearrange("p (c t) -> p c t", t=128),
                          d_xfm[:, :, tc * 128:(tc + 1) * 128])
        p.waits("tensor", [ld_tok[i], ev_tok[i]])
        for c in range(DMC):
            tr_tok[i] = p.tr(pws[i][:, c * 128:(c + 1) * 128],
                             stg[i][:, c * 128:(c + 1) * 128], idf[:])
        tr_tok[i] = p.trfence(cx.sb["c_trf"], cx.sb["c_ones_1x128b"])
        p.waits("vector", [tr_tok[i], st_tok[i]])
        p.vop("tensor_copy", out=stg2[i][:], in_=pws[i][:])
        ev_tok[i] = p.vop("engine_nop")
        p.waits("gpsimd", [ev_tok[i]])
        st_tok[i] = gtok = p.dma("gpsimd", x_tm_dram[tc * 128:(tc + 1) * 128, :],
                                 stg2[i][:])
    return gtok



_PROGRAMS = {}


def _program_input_names(nc):
    names = set()
    for alloc in nc.m.functions[0].allocations:
        if (isinstance(alloc, mybir.MemoryLocationSet)
                and alloc.kind == "ExternalInput"):
            nm = alloc.name
            if nm.endswith("_set"):
                nm = nm[:-4]
            names.add(nm)
    return names


def _get_program(name):
    if name not in _PROGRAMS:
        nc = {
            "moe": build_moe_program,
            "mamba": build_mamba_program,
            "l1ffn": build_l1ffn_program,
            "attn": build_attn_program,
            "fused": build_fused_program,
        }[name]()
        _PROGRAMS[name] = (nc, _program_input_names(nc))
    return _PROGRAMS[name]


FUSED = True


def kernel(**inputs):
    consts = host_constants()
    core_c = [core_constants(c) for c in range(NCORES)]
    x = np.ascontiguousarray(inputs["L0_attn_x"].astype(np.float32, copy=False))
    shards = [np.ascontiguousarray(x[c // 2, (c % 2) * T:(c % 2 + 1) * T])
              for c in range(NCORES)]

    moe_w = {k: np.ascontiguousarray(np.asarray(inputs[k], dtype=np.float32))
             for k in ("L0_ffn_Wg", "L0_ffn_bg", "L0_ffn_W1", "L0_ffn_b1",
                       "L0_ffn_W2", "L0_ffn_b2")}
    mamba_w = {k: np.ascontiguousarray(np.asarray(inputs[k], dtype=np.float32))
               for k in ("L0_attn_W_in", "L0_attn_conv_b", "L0_attn_A_log",
                         "L0_attn_D_param", "L0_attn_dt_bias", "L0_attn_norm_w",
                         "L0_attn_W_out")}
    mamba_w["L0_attn_conv_w"] = np.ascontiguousarray(
        np.asarray(inputs["L0_attn_conv_w"], dtype=np.float32).reshape(XBC, 4))
    attn_w = {k: np.ascontiguousarray(np.asarray(inputs[k], dtype=np.float32))
              for k in ("L1_attn_Wq", "L1_attn_Wk", "L1_attn_Wv", "L1_attn_Wo",
                        "L1_attn_bo")}
    l1_w = {k: np.ascontiguousarray(np.asarray(inputs[k], dtype=np.float32))
            for k in ("L1_ffn_W", "L1_ffn_b", "L1_ffn_lora_down",
                      "L1_ffn_lora_up", "L1_ffn_down_aux", "L1_ffn_up_aux")}

    def launch(name, wdict):
        nonlocal shards
        nc, innames = _get_program(name)
        in_maps = []
        for c in range(NCORES):
            m = {"x": shards[c]}
            m.update(wdict)
            m.update(consts)
            m.update(core_c[c])
            in_maps.append({k: v for k, v in m.items() if k in innames})
        res = run_bass_kernel_spmd(nc, in_maps, core_ids=list(range(NCORES)))
        shards = [np.asarray(res.results[c]["x_out"]) for c in range(NCORES)]

    if FUSED:
        allw = {}
        for wd in (moe_w, mamba_w, attn_w, l1_w):
            allw.update(wd)
        launch("fused", allw)
    else:
        launch("moe", moe_w)
        launch("mamba", mamba_w)
        launch("moe", moe_w)
        launch("l1ffn", l1_w)
        launch("attn", attn_w)
        launch("l1ffn", l1_w)

    out = np.empty((4, L_SEQ, DM), dtype=np.float32)
    for c in range(NCORES):
        out[c // 2, (c % 2) * T:(c % 2 + 1) * T] = shards[c]
    return out


def build_fused_program(phases=("moe", "mamba", "moe2", "l1ffn", "attn",
                                "l1ffn2"), debug_xfm=False):
    nc = bass.Bass("TRN2", target_bir_lowering=False)
    es = ExitStack()
    p = Prog(nc, es)
    cx = Ctx(nc, p, es)
    d = cx.dram
    d["x"] = nc.dram_tensor("x", [T, DM], F32, kind="ExternalInput").ap()
    moe_dram_decls(nc, d)
    mamba_dram_decls(nc, d)
    attn_dram_decls(nc, d)
    for nm, shp in [("L1_ffn_W", [DM, DM]), ("L1_ffn_b", [DM]),
                    ("L1_ffn_lora_down", [4, 100]), ("L1_ffn_lora_up", [50, 4]),
                    ("L1_ffn_down_aux", [100, DM]), ("L1_ffn_up_aux", [DM, 50])]:
        d[nm] = nc.dram_tensor(nm, shp, F32, kind="ExternalInput").ap()
    for nm, arr in host_constants().items():
        d[nm] = nc.dram_tensor(nm, list(arr.shape),
                               BF16 if arr.dtype == BF else F32,
                               kind="ExternalInput").ap()
    d["c_mask64"] = nc.dram_tensor("c_mask64", [64, 1], F32,
                                   kind="ExternalInput").ap()
    d["c_mask128"] = nc.dram_tensor("c_mask128", [128, 1], F32,
                                    kind="ExternalInput").ap()
    d["c_regsel2"] = d["c_regsel"] if "c_regsel" in d else None
    d["x_out"] = nc.dram_tensor("x_out", [T, DM], F32, kind="ExternalOutput").ap()
    if debug_xfm:
        d["x_fm_out"] = nc.dram_tensor("x_fm_out", [DMC, 128, T], F32,
                                       kind="ExternalOutput").ap()
        d["x_dbg1"] = nc.dram_tensor("x_dbg1", [DMC, 128, T], F32,
                                     kind="ExternalOutput").ap()
    d["d_rows"] = nc.dram_tensor("d_rows", [2, T], F32).ap()
    xfm = d["d_xfm"]
    with es:
        pk0 = cx.psum("pk0", [128, 512])
        pk1 = cx.psum("pk1", [128, 512])
        pk2 = cx.psum("pk2", [128, 512])
        pk3 = cx.psum("pk3", [128, 512])
        pw0 = cx.psum("pw0", [128, 1024])
        pw1 = cx.psum("pw1", [128, 1024])
        cx.sb["c_trf"] = pk0
        ctok = load_common_consts(cx, needed=[
            "c_id128f", "c_id128b", "c_ones_1x128b", "c_onescol_b",
            "c_ones_1x128f", "c_onescol_f"])
        p.waits("tensor", [ctok])
        p.waits("vector", [ctok])
        p.waits("scalar", [ctok])
        p.waits("gpsimd", [ctok])
        cx.push_scope()
        emit_tm_copy(cx, d["x"], xfm)
        p.barrier()
        cx.pop_scope()

        def load_xr(xr):
            return emit_load_x_fm(cx, xfm, xr, pw0, pw1)

        def store_xr(xr, pre):
            p.waits("tensor", list(pre))
            p.waits("vector", list(pre))
            return emit_store_x_fm(cx, xr, xfm, pw0, pw1)

        def phase_moe():
            cx.push_scope()
            ct = load_common_consts(cx, needed=["c_oneh8"])
            p.waits("tensor", [ct])
            xr = cx.sget("ph_xr", [128, DMC * T], F32)
            xl = cx.sget("ph_xl", [128, DMC * T], BF16)
            negm = cx.sget("ph_negm", [128, 16], F32)
            rcol = cx.sget("ph_rcol", [128, 16], F32)
            ltok = load_xr(xr)
            p.barrier()
            lntok, coltok = emit_ln_rows(cx, xr, xl, pk0, pk1, pk2, pk3,
                                         negm, rcol, d["d_rows"])
            p.barrier()
            ftok = emit_moe(cx, xr, xl, negm, rcol, pk0, pk1, pk2, pk3,
                            pw0, pw1, lntok, coltok)
            p.barrier()
            store_xr(xr, [ftok])
            p.barrier()
            cx.pop_scope()

        def phase_mamba():
            cx.push_scope()
            ct = load_common_consts(cx, needed=[
                "c_selhd", "c_lt01", "c_ones32x64b", "c_ones32x128f",
                "c_sel32", "c_mask64", "c_mask128"])
            p.waits("tensor", [ct])
            p.waits("vector", [ct])
            p.waits("scalar", [ct])
            emit_mamba(cx, pk0, pk1, pk2, pk3, pw0, pw1)
            p.barrier()
            cx.pop_scope()

        def phase_l1ffn():
            cx.push_scope()
            xr = cx.sget("ph_xr", [128, DMC * T], F32)
            xl = cx.sget("ph_xl", [128, DMC * T], BF16)
            weT = cx.sget("ph_weT", [128, 8 * DM], BF16)
            stage32 = cx.sget("ph_stage32", [128, 8 * DM], F32)
            halfb = cx.sget("ph_halfb", [128, 8], F32)
            btok = p.dma("sync", halfb[:],
                         d["L1_ffn_b"].rearrange("(c p) -> p c", p=128),
                         allow_slow_non_contiguous=True)
            p.waits("vector", [btok])
            p.vop("tensor_scalar", out=halfb[:], in0=halfb[:],
                  scalar1=0.5, scalar2=None, op0=ALU.mult)
            ltok = load_xr(xr)
            p.barrier()
            wtok = emit_weff(cx, weT, stage32, pw0, pw1, pk0)
            p.barrier()
            lntok = emit_ln(cx, xr, xl, pk0, pk1, pk2, pk3)
            p.barrier()
            ftok = emit_l1ffn(cx, xr, xl, weT, halfb, pk0, pk1, lntok, wtok)
            p.barrier()
            store_xr(xr, [ftok])
            p.barrier()
            cx.pop_scope()

        def phase_attn():
            cx.push_scope()
            ct = load_common_consts(cx, needed=["c_ones2", "c_postk"])
            p.waits("tensor", [ct])
            p.waits("vector", [ct])
            xr = cx.sget("ph_xr", [128, DMC * T], F32)
            xl = cx.sget("ph_xl", [128, DMC * T], BF16)
            ltok = load_xr(xr)
            p.barrier()
            lntok = emit_ln(cx, xr, xl, pk0, pk1, pk2, pk3)
            p.barrier()
            ftok = emit_attn(cx, xr, xl, 0, pk0, pk1, pk2, pk3, pw0, pw1,
                             lntok)
            p.barrier()
            store_xr(xr, [ftok])
            p.barrier()
            cx.pop_scope()

        def phase_xronly():
            cx.push_scope()
            xr = cx.sget("ph_xr", [128, DMC * T], F32)
            load_xr(xr)
            p.barrier()
            if "x_dbg1" in d:
                for c in range(DMC):
                    p.dma("gpsimd", d["x_dbg1"][c], xr[:, c * T:(c + 1) * T])
                p.barrier()
            store_xr(xr, [])
            p.barrier()
            cx.pop_scope()

        def phase_lnonly():
            cx.push_scope()
            xr = cx.sget("ph_xr", [128, DMC * T], F32)
            xl = cx.sget("ph_xl", [128, DMC * T], BF16)
            negm = cx.sget("ph_negm", [128, 16], F32)
            rcol = cx.sget("ph_rcol", [128, 16], F32)
            load_xr(xr)
            p.barrier()
            emit_ln_rows(cx, xr, xl, pk0, pk1, pk2, pk3, negm, rcol,
                         d["d_rows"])
            p.barrier()
            store_xr(xr, [])
            p.barrier()
            cx.pop_scope()

        fns = {"moe": phase_moe, "moe2": phase_moe, "mamba": phase_mamba,
               "l1ffn": phase_l1ffn, "l1ffn2": phase_l1ffn, "attn": phase_attn,
               "xronly": phase_xronly, "lnonly": phase_lnonly}
        for ph in phases:
            fns[ph]()
            p.barrier()
        cx.push_scope()
        stok = emit_tm_copy(cx, xfm, d["x_out"])
        if debug_xfm:
            dstg = cx.sget("dbg_stg", [128, T], F32)
            stok2 = None
            for c in range(DMC):
                t1_ = p.dma("sync", dstg[:], xfm[c])
                p.waits("gpsimd", [t1_] + ([stok2] if stok2 else []))
                stok2 = p.dma("gpsimd", d["x_fm_out"][c], dstg[:])
                p.waits("sync", [stok2])
            p.waits("gpsimd", [stok, stok2])
        p.waits("gpsimd", [stok])
        cx.pop_scope()
        p.finalize()
    return nc


# version 09
# speedup vs baseline: 1.2820x; 1.2820x over previous
import math
from contextlib import ExitStack

import numpy as np
import ml_dtypes

import concourse.bass as bass
import concourse.mybir as mybir
from concourse.bass_utils import run_bass_kernel_spmd

F32 = mybir.dt.float32
BF16 = mybir.dt.bfloat16
AF = mybir.ActivationFunctionType
ALU = mybir.AluOpType
BF = ml_dtypes.bfloat16

NCORES = 8
T = 2048
DM = 1024
DMC = DM // 128
L_SEQ = 4096
D_STATE = 64
D_HEAD = 64
N_HEADS = 32
D_INNER = 2048
XBC = D_INNER + 2 * D_STATE
PROJ = 2 * D_INNER + 2 * D_STATE + N_HEADS
D_CONV = 4
EPS = 1e-5
CHUNK = 128
NCHUNK = T // CHUNK
AH = 16
AD = 64
NEXP = 8
LNB = 256
ATTN_HEAD_BARRIER = False
ATTN_SERIAL_CC = False


def _alibi_slopes(n):
    start = 2.0 ** (-2.0 ** (-(np.log2(n) - 3)))
    return np.array([start * start ** i for i in range(n)], dtype=np.float32)

SLOPES = _alibi_slopes(AH)


ENGS = ("sync", "tensor", "vector", "scalar", "gpsimd")


class RInstr:
    __slots__ = ("incs",)

    def __init__(self):
        self.incs = []

    def then_inc(self, sem, k=1):
        self.incs.append((sem, k))
        sem.n += k
        return self


class Sem:

    def __init__(self, h, name):
        self.h = h
        self.name = name
        self.n = 0


class RQueue:
    def __init__(self, name):
        self.name = name
        self.ops = []

    def __getattr__(self, m):
        if m.startswith("_"):
            raise AttributeError(m)

        def f(*a, **kw):
            r = RInstr()
            self.ops.append((m, a, kw, r))
            return r

        return f

    def replay(self, eng):
        for m, a, kw, r in self.ops:
            ins = getattr(eng, m)(*a, **kw)
            for s, k in r.incs:
                ins.then_inc(s.h, k)


class Prog:

    def __init__(self, nc, es):
        self.nc = nc
        self.es = es
        self.q = {e: RQueue(e) for e in ENGS}
        self.sem = {}
        for name in ("T", "V", "S", "P", "L", "G", "C"):
            self.sem[name] = Sem(es.enter_context(nc.semaphore("sem_" + name)), name)

    @property
    def t(self):
        return self.q["tensor"]

    @property
    def v(self):
        return self.q["vector"]

    @property
    def s(self):
        return self.q["scalar"]

    @property
    def g(self):
        return self.q["gpsimd"]

    @property
    def y(self):
        return self.q["sync"]

    def wait(self, qname, tok):
        if tok is None:
            return
        sem, val = tok
        assert val <= sem.n, f"wait on future value {val} > {sem.n} of {sem.name}"
        if val <= 0:
            return
        self.q[qname].wait_ge(sem.h, val)

    def waits(self, qname, toks):
        best = {}
        for tk in toks:
            if tk is None:
                continue
            s, v = tk
            if v > best.get(id(s), (s, -1))[1]:
                best[id(s)] = (s, v)
        for s, v in best.values():
            self.wait(qname, (s, v))

    def tok(self, name):
        return (self.sem[name], self.sem[name].n)

    def dma(self, qname, out, in_, **kw):
        sem = self.sem["L"] if qname == "sync" else self.sem["G"]
        r = self.q[qname].dma_start(out=out, in_=in_, **kw)
        r.then_inc(sem, 16)
        return (sem, sem.n)

    def mm(self, out, lhsT, rhs, start=True, stop=True, inc=True):
        r = self.t.matmul(out, lhsT, rhs, start=start, stop=stop)
        if inc:
            r.then_inc(self.sem["T"], 1)
        return (self.sem["T"], self.sem["T"].n)

    def trfence(self, scratch_psum, ones):
        return self.mm(scratch_psum[0:1, 0:1], ones[0:1, 0:1], ones[0:1, 0:1])

    def tr(self, out, in_, ident):
        r = self.t.transpose(out, in_, ident)
        r.then_inc(self.sem["T"], 1)
        return (self.sem["T"], self.sem["T"].n)

    def vop(self, method, *a, **kw):
        r = getattr(self.v, method)(*a, **kw)
        r.then_inc(self.sem["V"], 1)
        return (self.sem["V"], self.sem["V"].n)

    def sop(self, method, *a, **kw):
        r = getattr(self.s, method)(*a, **kw)
        r.then_inc(self.sem["S"], 1)
        return (self.sem["S"], self.sem["S"].n)

    def gop(self, method, *a, **kw):
        r = getattr(self.g, method)(*a, **kw)
        r.then_inc(self.sem["P"], 1)
        return (self.sem["P"], self.sem["P"].n)

    def collective(self, kind, op, groups, ins, outs):
        r = self.g.collective_compute(kind, op, replica_groups=groups,
                                      ins=ins, outs=outs)
        r.then_inc(self.sem["C"], 1)
        return (self.sem["C"], self.sem["C"].n)

    def vfence(self):
        self.wait("vector", (self.sem["V"], self.sem["V"].n))

    def barrier(self):
        toks = [(s, s.n) for s in self.sem.values() if s.n > 0]
        for qname in ENGS:
            self.waits(qname, toks)

    def finalize(self):
        with self.nc.Block() as block:
            for name in ENGS:
                q = self.q[name]
                if not q.ops:
                    continue
                getattr(block, name)(q.replay)


_CONST_CACHE = None


def host_constants():
    global _CONST_CACHE
    if _CONST_CACHE is not None:
        return _CONST_CACHE
    c = {}
    c["c_id128f"] = np.eye(128, dtype=np.float32)
    c["c_id128b"] = np.eye(128, dtype=BF)
    c["c_ones_1x128b"] = np.ones((1, 128), dtype=BF)
    c["c_onescol_b"] = np.ones((128, 1), dtype=BF)
    c["c_ones_1x128f"] = np.ones((1, 128), dtype=np.float32)
    c["c_onescol_f"] = np.ones((128, 1), dtype=np.float32)
    o2 = np.zeros((128, 2), dtype=BF)
    o2[0:64, 0] = 1
    o2[64:128, 1] = 1
    c["c_ones2"] = o2
    sel = np.zeros((32, 2048), dtype=BF)
    for h in range(32):
        sel[h, h * 64:(h + 1) * 64] = 1
    c["c_selhd"] = sel
    c["c_lt01"] = np.triu(np.ones((128, 128), dtype=np.float32))
    c["c_ones32x64b"] = np.ones((32, 64), dtype=BF)
    s32 = np.zeros((32, 32 * 128), dtype=BF)
    for h in range(32):
        s32[h, h * 128:(h + 1) * 128] = 1
    c["c_sel32"] = s32
    oh = np.zeros((8, 8 * 128), dtype=BF)
    for e in range(8):
        oh[e, e * 128:(e + 1) * 128] = 1
    c["c_oneh8"] = oh
    c["c_postk"] = (np.arange(32)[None, :] * 128 +
                    np.arange(128)[:, None]).astype(np.float32)
    p = np.arange(128)[:, None]
    b = np.arange(512)[None, :]
    dabs = [np.abs(p - b + d0 * 128).astype(np.float32) for d0 in range(4)]
    c["c_dabs"] = np.concatenate(dabs, axis=1)
    c["c_ones32x128f"] = np.ones((32, 128), dtype=np.float32)
    oq = np.zeros((12, T), dtype=BF)
    for g in range(4):
        for j in range(3):
            oq[3 * g + j, g * 512:(g + 1) * 512] = 1
    c["c_onesq"] = oq
    _CONST_CACHE = c
    return c


def core_constants(core):
    qoff = (core % 2) * T
    cc = {}
    posq = (qoff + np.arange(T)).astype(np.float64)
    qrows = np.zeros((64, T), dtype=BF)
    for h in range(AH):
        a = SLOPES[h].astype(np.float64) * posq
        hi = a.astype(BF)
        lo = (a - hi.astype(np.float64)).astype(BF)
        qrows[4 * h + 0] = hi
        qrows[4 * h + 1] = lo
        nhi = (-a).astype(BF)
        nlo = (-a - nhi.astype(np.float64)).astype(BF)
        qrows[4 * h + 2] = nhi
        qrows[4 * h + 3] = nlo
    cc["c_qrows"] = qrows
    qm = np.zeros((16 * AH, T), dtype=BF)
    for h in range(AH):
        for g in range(4):
            for j in range(4):
                qm[16 * h + 4 * g + j, g * 512:(g + 1) * 512] = \
                    qrows[4 * h + j, g * 512:(g + 1) * 512]
    cc["c_qrowsM"] = qm
    cc["c_mask64"] = np.full((64, 1), 1.0 if core % 2 == 1 else 0.0, np.float32)
    cc["c_mask128"] = np.full((128, 1), 1.0 if core % 2 == 1 else 0.0, np.float32)
    selAB = np.zeros((16, 4096), dtype=BF)
    regsel = np.zeros((1, 128), dtype=np.float32)
    for tqb in range(4):
        tq0 = qoff + tqb * 512
        for tkc in range(32):
            tk0 = tkc * 128
            if tk0 >= tq0 + 512:
                reg = "A"
            elif tk0 <= tq0 - 128:
                reg = "B"
            else:
                reg = "C"
            if reg == "A":
                selAB[tqb * 4 + 0, tk0:tk0 + 128] = 1
                selAB[tqb * 4 + 1, tk0:tk0 + 128] = 1
                regsel[0, tqb * 32 + tkc] = -1.0
            elif reg == "B":
                selAB[tqb * 4 + 2, tk0:tk0 + 128] = 1
                selAB[tqb * 4 + 3, tk0:tk0 + 128] = 1
                regsel[0, tqb * 32 + tkc] = 1.0
    cc["c_selAB"] = selAB
    cc["c_regsel"] = regsel
    br = np.zeros((AH, 12, 4096), dtype=BF)
    kpos = np.arange(4096).astype(np.float64)
    for h in range(AH):
        for g in range(4):
            v = (np.float64(SLOPES[h]) *
                 np.repeat(regsel[0, g * 32:(g + 1) * 32], 128) * kpos)
            for j in range(3):
                hj = v.astype(BF)
                br[h, 3 * g + j] = hj
                v = v - hj.astype(np.float64)
    cc["c_biasrows"] = br
    dd = np.zeros((128, 8 * 512), dtype=np.float64)
    pidx = np.arange(128)[:, None]
    bidx = np.arange(512)[None, :]
    for di in range(8):
        owner = 0 if di < 4 else 2048
        if owner == qoff:
            dd[:, di * 512:(di + 1) * 512] = np.abs(pidx - bidx + (di % 4) * 128)
    corr = np.zeros((AH, 128, 8 * 512), dtype=BF)
    for h in range(AH):
        corr[h] = np.exp(-np.float64(SLOPES[h]) * dd).astype(BF)
    cc["c_corr"] = corr
    return cc



def ceil_div(a, b):
    return (a + b - 1) // b


class Ctx:

    def __init__(self, nc, p, es):
        self.nc = nc
        self.p = p
        self.es = es
        self.sb = {}
        self.dram = {}
        self._scopes = []
        self._scope_serial = 0

    def push_scope(self):
        self._scope_serial += 1
        self._scopes.append((ExitStack(), []))

    def pop_scope(self):
        st, names = self._scopes.pop()
        st.close()
        for nm in names:
            self.sb.pop(nm, None)

    def sbuf(self, name, shape, dtype):
        import os
        if self._scopes:
            st, names = self._scopes[-1]
            t = st.enter_context(
                self.nc.sbuf_tensor(f"sb{self._scope_serial}_{name}_{len(names)}",
                                    shape, dtype))
            names.append(name)
            if os.environ.get("SBUF_DEBUG"):
                esz = {F32: 4, BF16: 2}.get(dtype, 4)
                print(f"[sbuf-s] {name}: {shape} = {shape[-1] * esz}B/part, "
                      f"remaining {self.nc.sbuf_bytes_remaining}")
            self.sb[name] = t
            return t
        t = self.es.enter_context(self.nc.sbuf_tensor("sb_" + name, shape, dtype))
        if os.environ.get("SBUF_DEBUG"):
            esz = {F32: 4, BF16: 2}.get(dtype, 4)
            print(f"[sbuf] {name}: {shape} = {shape[-1] * esz}B/part, "
                  f"remaining {self.nc.sbuf_bytes_remaining}")
        self.sb[name] = t
        return t

    def sget(self, name, shape, dtype):
        if name not in self.sb:
            return self.sbuf(name, shape, dtype)
        return self.sb[name]

    def psum(self, name, shape, dtype=F32):
        t = self.es.enter_context(self.nc.psum_tensor(name, shape, dtype))
        return t


def load_common_consts(cx, needed=None):
    p = cx.p
    names = [
        ("c_id128f", [128, 128], F32),
        ("c_id128b", [128, 128], BF16),
        ("c_ones_1x128b", [1, 128], BF16),
        ("c_onescol_b", [128, 1], BF16),
        ("c_ones_1x128f", [1, 128], F32),
        ("c_onescol_f", [128, 1], F32),
        ("c_ones2", [128, 2], BF16),
        ("c_selhd", [32, 2048], BF16),
        ("c_lt01", [128, 128], F32),
        ("c_ones32x64b", [32, 64], BF16),
        ("c_oneh8", [8, 8 * 128], BF16),
        ("c_postk", [128, 32], F32),
        ("c_dabs", [128, 4 * 512], F32),
        ("c_ones32x128f", [32, 128], F32),
        ("c_sel32", [32, 32 * 128], BF16),
        ("c_qrows", [64, T], BF16),
        ("c_mask64", [64, 1], F32),
        ("c_mask128", [128, 1], F32),
    ]
    tok = None
    for nm, shp, dt in names:
        if nm not in cx.dram:
            continue
        if needed is not None and nm not in needed:
            continue
        tile = cx.sbuf(nm, shp, dt) if nm not in cx.sb else cx.sb[nm]
        tok = p.dma("sync", tile[:], cx.dram[nm])
    return tok



def emit_load_x_fm(cx, x_tm_dram, xr, pw0, pw1):
    p = cx.p
    stg = [cx.sget(f"xio_stg{i}", [128, DM], F32) for i in range(2)]
    idf = cx.sb["c_id128f"]
    pws = [pw0, pw1]
    ev_tok = [None, None]
    ld_tok = [None, None]
    tr_tok = [None, None]
    vtok = None
    for tc in range(T // 128):
        i = tc % 2
        p.waits("sync", [tr_tok[i]])
        ld_tok[i] = p.dma("sync", stg[i][:], x_tm_dram[tc * 128:(tc + 1) * 128, :])
        p.waits("tensor", [ld_tok[i], ev_tok[i]])
        for c in range(DMC):
            tr_tok[i] = p.tr(pws[i][:, c * 128:(c + 1) * 128],
                             stg[i][:, c * 128:(c + 1) * 128], idf[:])
        tr_tok[i] = p.trfence(cx.sb["c_trf"], cx.sb["c_ones_1x128b"])
        p.waits("vector", [tr_tok[i]])
        dst = xr[:].rearrange("p (c t) -> p c t", c=DMC)[:, :, tc * 128:(tc + 1) * 128]
        src = pws[i][:].rearrange("p (c t) -> p c t", c=DMC)
        ev_tok[i] = vtok = p.vop("tensor_copy", out=dst, in_=src)
    p.vfence()
    return cx.p.tok("V")


def emit_store_x_fm(cx, xr, x_tm_dram, pw0, pw1, pre_toks=()):
    p = cx.p
    stg = [cx.sget(f"xio_stg{i}", [128, DM], F32) for i in range(2)]
    idf = cx.sb["c_id128f"]
    pws = [pw0, pw1]
    ev_tok = [None, None]
    st_tok = [None, None]
    tr_tok = [None, None]
    p.waits("tensor", list(pre_toks))
    gtok = None
    for tc in range(T // 128):
        i = tc % 2
        p.waits("tensor", [ev_tok[i]])
        for c in range(DMC):
            tr_tok[i] = p.tr(pws[i][:, c * 128:(c + 1) * 128],
                             xr[:, c * T + tc * 128: c * T + (tc + 1) * 128],
                             idf[:])
        p.waits("vector", [tr_tok[i], st_tok[0], st_tok[1]])
        ev_tok[i] = p.vop("tensor_copy", out=stg[i][:], in_=pws[i][:])
        p.waits("gpsimd", [ev_tok[i]])
        st_tok[i] = gtok = p.dma("gpsimd", x_tm_dram[tc * 128:(tc + 1) * 128, :],
                                 stg[i][:])
    return gtok


def emit_ln(cx, xr, xl, pk0, pk1, pk2, pk3, x_tok=(), rows_out=None):
    p = cx.p
    onesb = cx.sb["c_onescol_b"]
    ones1 = cx.sb["c_ones_1x128b"]
    xbf = cx.sget("ln_xbf", [128, DMC * LNB], BF16)
    x2b = cx.sget("ln_x2b", [128, DMC * LNB], BF16)
    srow = [cx.sget(f"ln_row{i}", [1, LNB], F32) for i in range(3)]
    row4 = cx.sget("ln_row3", [1, LNB], F32)
    mrb = cx.sget("ln_mrb", [1, LNB], BF16)
    rrb = cx.sget("ln_rrb", [1, LNB], BF16)
    tmp = cx.sget("ln_tmp", [128, LNB], F32)
    if rows_out is not None:
        rows_out = (rows_out[0], rows_out[1], row4)
    out_tok = None
    stats_done = None
    stats_read = None
    apply_done = None
    for blk in range(T // LNB):
        p.waits("vector", [stats_done] + list(x_tok))
        ctok = None
        for c in range(DMC):
            ctok = p.vop("tensor_copy", out=xbf[:, c * LNB:(c + 1) * LNB],
                         in_=xr[:, c * T + blk * LNB: c * T + (blk + 1) * LNB])
        p.waits("scalar", [stats_done] + list(x_tok))
        sq_tok = None
        for c in range(DMC):
            sq_tok = p.sop("activation", out=x2b[:, c * LNB:(c + 1) * LNB],
                           in_=xr[:, c * T + blk * LNB: c * T + (blk + 1) * LNB],
                           func=AF.Square)
        p.waits("tensor", [ctok, sq_tok, stats_read])
        for c in range(DMC):
            p.mm(pk0[0:1, 0:LNB], onesb[:], xbf[:, c * LNB:(c + 1) * LNB],
                 start=(c == 0), stop=(c == DMC - 1))
        for c in range(DMC):
            stats_done = p.mm(pk1[0:1, 0:LNB], onesb[:],
                              x2b[:, c * LNB:(c + 1) * LNB],
                              start=(c == 0), stop=(c == DMC - 1))
        p.waits("vector", [stats_done])
        p.vop("tensor_scalar", out=srow[0][:], in0=pk0[0:1, 0:LNB],
              scalar1=1.0 / DM, scalar2=None, op0=ALU.mult)
        stats_read = p.vop("tensor_scalar", out=srow[1][:], in0=pk1[0:1, 0:LNB],
                           scalar1=1.0 / DM, scalar2=EPS, op0=ALU.mult,
                           op1=ALU.add)
        p.vfence()
        p.vop("tensor_tensor", out=srow[2][:], in0=srow[0][:], in1=srow[0][:],
              op=ALU.mult)
        p.vfence()
        p.vop("tensor_tensor", out=srow[1][:], in0=srow[1][:], in1=srow[2][:],
              op=ALU.subtract)
        p.vfence()
        vtok = p.vop("reciprocal", out=srow[2][:], in_=srow[1][:])
        p.vfence()
        p.waits("scalar", [vtok])
        stok = p.sop("activation", out=rrb[:], in_=srow[2][:], func=AF.Sqrt)
        mtok = p.vop("tensor_copy", out=mrb[:], in_=srow[0][:])
        if rows_out is not None:
            dm_m, dm_r, row4 = rows_out
            rtk = p.sop("activation", out=row4[:], in_=srow[2][:], func=AF.Sqrt)
            p.waits("gpsimd", [mtok, rtk])
            dtk1 = p.dma("gpsimd", dm_m[blk * LNB:(blk + 1) * LNB], srow[0][:])
            dtk2 = p.dma("gpsimd", dm_r[blk * LNB:(blk + 1) * LNB], row4[:])
            p.waits("vector", [dtk1, dtk2])
            p.waits("scalar", [dtk2])
        p.waits("tensor", [mtok, stok, apply_done])
        p.mm(pk2[:, 0:LNB], ones1[:], mrb[:])
        bt2 = p.mm(pk3[:, 0:LNB], ones1[:], rrb[:])
        p.waits("vector", [bt2])
        for c in range(DMC):
            p.vop("tensor_tensor", out=tmp[:],
                  in0=xr[:, c * T + blk * LNB: c * T + (blk + 1) * LNB],
                  in1=pk2[:, 0:LNB], op=ALU.subtract)
            p.vfence()
            out_tok = p.vop("tensor_tensor",
                            out=xl[:, c * T + blk * LNB: c * T + (blk + 1) * LNB],
                            in0=tmp[:], in1=pk3[:, 0:LNB], op=ALU.mult)
        apply_done = out_tok
    return out_tok


def emit_cast_weight(cx, src_dram, dst_sb, kchunks, cols, stage, scale=None,
                     pre_toks=(), qname="sync"):
    p = cx.p
    ltoks = []
    p.waits("sync", list(pre_toks))
    for c in range(kchunks):
        ltoks.append(p.dma(qname, stage[:, c * cols:(c + 1) * cols],
                           src_dram[c * 128:(c + 1) * 128, :]))
    p.waits("vector", [ltoks[-1]])
    if scale is None:
        vt = p.vop("tensor_copy", out=dst_sb[:, 0:kchunks * cols],
                   in_=stage[:, 0:kchunks * cols])
    else:
        vt = p.vop("tensor_scalar", out=dst_sb[:, 0:kchunks * cols],
                   in0=stage[:, 0:kchunks * cols], scalar1=float(scale),
                   scalar2=None, op0=ALU.mult)
    return vt


def emit_transpose_weight(cx, src_dram, dst_sb, stage, pw0, pw1, scale=None,
                          pre_toks=()):
    p = cx.p
    idf = cx.sb["c_id128f"]
    ltoks = []
    p.waits("sync", list(pre_toks))
    for a in range(8):
        ltoks.append(p.dma("sync", stage[:, a * DM:(a + 1) * DM],
                           src_dram[a * 128:(a + 1) * 128, :]))
    pws = [pw0, pw1]
    ev = [None, None]
    vt = None
    p.waits("tensor", [ltoks[-1]])
    for b in range(8):
        i = b % 2
        p.waits("tensor", [ev[i]])
        ttok = None
        for a in range(8):
            ttok = p.tr(pws[i][:, a * 128:(a + 1) * 128],
                        stage[:, a * DM + b * 128: a * DM + (b + 1) * 128],
                        idf[:])
        p.waits("vector", [ttok])
        if scale is None:
            ev[i] = vt = p.vop("tensor_copy", out=dst_sb[:, b * DM:(b + 1) * DM],
                               in_=pws[i][:])
        else:
            ev[i] = vt = p.vop("tensor_scalar", out=dst_sb[:, b * DM:(b + 1) * DM],
                               in0=pws[i][:], scalar1=float(scale), scalar2=None,
                               op0=ALU.mult)
    return vt


def emit_weff(cx, weT, stage32, pw0, pw1, pk0):
    p = cx.p
    idf = cx.sb["c_id128f"]
    d = cx.dram
    ldT = cx.sget("we_ldT", [100, 4], F32)
    luT = cx.sget("we_luT", [4, 50], F32)
    da = cx.sget("we_da", [100, 1024], F32)
    upT = cx.sget("we_upT", [50, 1024], F32)
    t1 = cx.sget("we_t1", [4, 1024], F32)
    t2 = cx.sget("we_t2", [50, 1024], F32)
    up_stg = cx.sget("we_upstg", [128, 8 * 50], F32)
    wblk = cx.sget("we_wblk", [128, 2 * 128], F32)
    tk1 = p.dma("sync", ldT[:], d["L1_ffn_lora_down"].rearrange("a b -> b a"),
                allow_slow_non_contiguous=True)
    tk2 = p.dma("sync", luT[:], d["L1_ffn_lora_up"].rearrange("a b -> b a"),
                allow_slow_non_contiguous=True)
    tk3 = p.dma("sync", da[:], d["L1_ffn_down_aux"])
    utoks = []
    for a in range(8):
        utoks.append(p.dma("sync", up_stg[:, a * 50:(a + 1) * 50],
                           d["L1_ffn_up_aux"][a * 128:(a + 1) * 128, :]))
    p.waits("tensor", [utoks[-1]])
    ttok = None
    for a in range(8):
        ttok = p.tr(pw0[0:50, a * 128:(a + 1) * 128],
                    up_stg[:, a * 50:(a + 1) * 50], idf[:])
    p.waits("vector", [ttok])
    uev = p.vop("tensor_copy", out=upT[:], in_=pw0[0:50, 0:1024])
    p.waits("tensor", [tk1, tk3, uev])
    mt = p.mm(pk0[0:4, 0:512], ldT[:], da[:, 0:512])
    mt2 = p.mm(pw1[0:4, 0:512], ldT[:], da[:, 512:1024])
    p.waits("vector", [mt, mt2])
    e1 = p.vop("tensor_copy", out=t1[:, 0:512], in_=pk0[0:4, 0:512])
    e2 = p.vop("tensor_copy", out=t1[:, 512:1024], in_=pw1[0:4, 0:512])
    p.waits("tensor", [tk2, e1, e2])
    mt = p.mm(pk0[0:50, 0:512], luT[:], t1[:, 0:512])
    mt2 = p.mm(pw1[0:50, 0:512], luT[:], t1[:, 512:1024])
    p.waits("vector", [mt, mt2])
    e1 = p.vop("tensor_copy", out=t2[:, 0:512], in_=pk0[0:50, 0:512])
    e2 = p.vop("tensor_copy", out=t2[:, 512:1024], in_=pw1[0:50, 0:512])
    p.waits("tensor", [e1, e2])
    for ic in range(8):
        for half in range(2):
            ps = pk0 if half == 0 else pw1
            dmt = p.mm(ps[0:128, 0:512], t2[:, ic * 128:(ic + 1) * 128],
                       upT[:, half * 512:(half + 1) * 512])
            p.waits("vector", [dmt])
            vt = p.vop("tensor_scalar",
                       out=stage32[:, ic * DM + half * 512: ic * DM + (half + 1) * 512],
                       in0=ps[0:128, 0:512], scalar1=0.5, scalar2=None,
                       op0=ALU.mult)
            p.waits("tensor", [vt])
    ld_tok = [None, None]
    tr_done = [None, None]
    ev = [None, None]
    vt = None
    for b in range(8):
        i = b % 2
        pws = pw0 if i == 0 else pw1
        for a in range(8):
            p.waits("sync", [tr_done[i]])
            ld_tok[i] = p.dma("sync", wblk[:, i * 128:(i + 1) * 128],
                              d["L1_ffn_W"][a * 128:(a + 1) * 128,
                                            b * 128:(b + 1) * 128])
            p.waits("tensor", [ld_tok[i], ev[i]])
            tr_done[i] = p.tr(pws[:, a * 128:(a + 1) * 128],
                              wblk[:, i * 128:(i + 1) * 128], idf[:])
        p.waits("vector", [tr_done[i]])
        ev[i] = vt = p.vop("scalar_tensor_tensor",
                           out=weT[:, b * DM:(b + 1) * DM],
                           in0=pws[:, 0:DM], scalar=0.5,
                           in1=stage32[:, b * DM:(b + 1) * DM],
                           op0=ALU.mult, op1=ALU.add)
    return vt


def emit_transpose_weight_f32(cx, src_dram, dst_f32, pw0, pw1, scale=None):
    p = cx.p
    idf = cx.sb["c_id128f"]
    stage = cx.sget("wtr_stage", [128, 8 * DM], F32)
    ltoks = []
    for a in range(8):
        ltoks.append(p.dma("sync", stage[:, a * DM:(a + 1) * DM],
                           src_dram[a * 128:(a + 1) * 128, :]))
    pws = [pw0, pw1]
    ev = [None, None]
    vt = None
    p.waits("tensor", [ltoks[-1]])
    for b in range(8):
        i = b % 2
        p.waits("tensor", [ev[i]])
        ttok = None
        for a in range(8):
            ttok = p.tr(pws[i][:, a * 128:(a + 1) * 128],
                        stage[:, a * DM + b * 128: a * DM + (b + 1) * 128],
                        idf[:])
        p.waits("vector", [ttok])
        if scale is None:
            ev[i] = vt = p.vop("tensor_copy", out=dst_f32[:, b * DM:(b + 1) * DM],
                               in_=pws[i][:, 0:DM])
        else:
            ev[i] = vt = p.vop("tensor_scalar", out=dst_f32[:, b * DM:(b + 1) * DM],
                               in0=pws[i][:, 0:DM], scalar1=float(scale),
                               scalar2=None, op0=ALU.mult)
    return vt


def emit_l1ffn(cx, xr, xl, weT, halfb, pk0, pk1, tok_xl, tok_w):
    p = cx.p
    pks = [pk0, pk1]
    ev = [None, None]
    vt = None
    p.waits("tensor", [tok_xl, tok_w])
    g = 0
    for oc in range(DMC):
        for tqb in range(T // 512):
            i = g % 2
            p.waits("tensor", [ev[i]])
            mt = None
            for kc in range(DMC):
                mt = p.mm(pks[i][:, 0:512],
                          weT[:, kc * DM + oc * 128: kc * DM + (oc + 1) * 128],
                          xl[:, kc * T + tqb * 512: kc * T + (tqb + 1) * 512],
                          start=(kc == 0), stop=(kc == DMC - 1))
            p.waits("vector", [mt])
            ev[i] = vt = p.vop(
                "scalar_tensor_tensor",
                out=xr[:, oc * T + tqb * 512: oc * T + (tqb + 1) * 512],
                in0=pks[i][:, 0:512], scalar=halfb[:, oc:oc + 1],
                in1=xr[:, oc * T + tqb * 512: oc * T + (tqb + 1) * 512],
                op0=ALU.add, op1=ALU.add)
            g += 1
    return vt



def build_l1ffn_program():
    nc = bass.Bass("TRN2", target_bir_lowering=False)
    es = ExitStack()
    p = Prog(nc, es)
    cx = Ctx(nc, p, es)
    d = cx.dram
    d["x"] = nc.dram_tensor("x", [T, DM], F32, kind="ExternalInput").ap()
    for nm, shp in [("L1_ffn_W", [DM, DM]), ("L1_ffn_b", [DM]),
                    ("L1_ffn_lora_down", [4, 100]), ("L1_ffn_lora_up", [50, 4]),
                    ("L1_ffn_down_aux", [100, DM]), ("L1_ffn_up_aux", [DM, 50])]:
        d[nm] = nc.dram_tensor(nm, shp, F32, kind="ExternalInput").ap()
    for nm, arr in host_constants().items():
        d[nm] = nc.dram_tensor(nm, list(arr.shape),
                               BF16 if arr.dtype == BF else F32,
                               kind="ExternalInput").ap()
    d["x_out"] = nc.dram_tensor("x_out", [T, DM], F32, kind="ExternalOutput").ap()

    with es:
        pk0 = cx.psum("pk0", [128, 512])
        pk1 = cx.psum("pk1", [128, 512])
        pk2 = cx.psum("pk2", [128, 512])
        pk3 = cx.psum("pk3", [128, 512])
        pw0 = cx.psum("pw0", [128, 1024])
        pw1 = cx.psum("pw1", [128, 1024])
        cx.sb["c_trf"] = pk0
        xr = cx.sbuf("xr", [128, DMC * T], F32)
        xl = cx.sbuf("xl", [128, DMC * T], BF16)
        weT = cx.sbuf("weT", [128, 8 * DM], BF16)
        stage32 = cx.sbuf("stage32", [128, 8 * DM], F32)
        halfb = cx.sbuf("halfb", [128, 8], F32)
        ctok = load_common_consts(cx, needed=["c_id128f", "c_ones_1x128b",
                                              "c_onescol_b"])
        btok = p.dma("sync", halfb[:],
                     d["L1_ffn_b"].rearrange("(c p) -> p c", p=128),
                     allow_slow_non_contiguous=True)
        p.waits("vector", [btok])
        hbtok = p.vop("tensor_scalar", out=halfb[:], in0=halfb[:],
                      scalar1=0.5, scalar2=None, op0=ALU.mult)
        p.waits("tensor", [ctok])
        xtok = emit_load_x_fm(cx, d["x"], xr, pw0, pw1)
        p.barrier()
        wtok = emit_weff(cx, weT, stage32, pw0, pw1, pk0)
        p.barrier()
        lntok = emit_ln(cx, xr, xl, pk0, pk1, pk2, pk3)
        p.barrier()
        ftok = emit_l1ffn(cx, xr, xl, weT, halfb, pk0, pk1, lntok, wtok)
        p.barrier()
        stok = emit_store_x_fm(cx, xr, d["x_out"], pw0, pw1, pre_toks=[ftok])
        p.waits("gpsimd", [stok])
        p.finalize()
    return nc


def build_copy_program(stages=("load", "weff", "ln", "ffn")):
    nc = bass.Bass("TRN2", target_bir_lowering=False)
    es = ExitStack()
    p = Prog(nc, es)
    cx = Ctx(nc, p, es)
    d = cx.dram
    d["x"] = nc.dram_tensor("x", [T, DM], F32, kind="ExternalInput").ap()
    for nm, shp in [("L1_ffn_W", [DM, DM]), ("L1_ffn_b", [DM]),
                    ("L1_ffn_lora_down", [4, 100]), ("L1_ffn_lora_up", [50, 4]),
                    ("L1_ffn_down_aux", [100, DM]), ("L1_ffn_up_aux", [DM, 50])]:
        d[nm] = nc.dram_tensor(nm, shp, F32, kind="ExternalInput").ap()
    for nm, arr in host_constants().items():
        d[nm] = nc.dram_tensor(nm, list(arr.shape),
                               BF16 if arr.dtype == BF else F32,
                               kind="ExternalInput").ap()
    d["x_out"] = nc.dram_tensor("x_out", [T, DM], F32, kind="ExternalOutput").ap()
    d["xl_out"] = nc.dram_tensor("xl_out", [T, DM], F32, kind="ExternalOutput").ap()
    d["we_out"] = nc.dram_tensor("we_out", [1024, DM], F32, kind="ExternalOutput").ap()
    with es:
        pk0 = cx.psum("pk0", [128, 512])
        pk1 = cx.psum("pk1", [128, 512])
        pk2 = cx.psum("pk2", [128, 512])
        pk3 = cx.psum("pk3", [128, 512])
        pw0 = cx.psum("pw0", [128, 1024])
        pw1 = cx.psum("pw1", [128, 1024])
        cx.sb["c_trf"] = pk0
        xr = cx.sbuf("xr", [128, DMC * T], F32)
        xl = cx.sbuf("xl", [128, DMC * T], BF16)
        weT = cx.sbuf("weT", [128, 8 * DM], BF16)
        stage32 = cx.sbuf("stage32", [128, 8 * DM], F32)
        halfb = cx.sbuf("halfb", [128, 8], F32)
        ctok = load_common_consts(cx, needed=["c_id128f", "c_ones_1x128b",
                                              "c_onescol_b"])
        btok = p.dma("sync", halfb[:],
                     d["L1_ffn_b"].rearrange("(c p) -> p c", p=128),
                     allow_slow_non_contiguous=True)
        p.waits("vector", [btok])
        p.vop("tensor_scalar", out=halfb[:], in0=halfb[:],
              scalar1=0.5, scalar2=None, op0=ALU.mult)
        p.waits("tensor", [ctok])
        xtok = emit_load_x_fm(cx, d["x"], xr, pw0, pw1)
        p.barrier()
        wtok = None
        if "weff" in stages:
            wtok = emit_weff(cx, weT, stage32, pw0, pw1, pk0)
            p.barrier()
            for kc in range(8):
                vt = p.vop("tensor_copy", out=stage32[:, kc * DM:(kc + 1) * DM],
                           in_=weT[:, kc * DM:(kc + 1) * DM])
            p.waits("gpsimd", [vt])
            for kc in range(8):
                gt = p.dma("gpsimd", d["we_out"][kc * 128:(kc + 1) * 128, :],
                           stage32[:, kc * DM:(kc + 1) * DM])
            p.barrier()
        if "ln" in stages:
            lntok = emit_ln(cx, xr, xl, pk0, pk1, pk2, pk3)
            p.barrier()
            for c in range(DMC):
                vt = p.vop("tensor_copy", out=stage32[:, c * DM: c * DM + 512],
                           in_=xl[:, c * T:c * T + 512])
            p.waits("gpsimd", [vt])
            gt = p.dma("gpsimd",
                       d["xl_out"].rearrange("(a p) c -> p (a c)", p=128)[:, 0:8 * 512][:, 0:0]
                       if False else
                       d["xl_out"][0:128, :],
                       stage32[:, 0:DM])
            p.barrier()
        if "ffn" in stages:
            ftok = emit_l1ffn(cx, xr, xl, weT, halfb, pk0, pk1, None, None)
            p.barrier()
        stok = emit_store_x_fm(cx, xr, d["x_out"], pw0, pw1)
        p.waits("gpsimd", [stok])
        p.finalize()
    return nc


def emit_ln_rows(cx, xr, xl, pk0, pk1, pk2, pk3, negm_col, rcol, d_bounce,
                 x_tok=()):
    p = cx.p
    tok = emit_ln(cx, xr, xl, pk0, pk1, pk2, pk3, x_tok=x_tok,
                  rows_out=(d_bounce[0, :], d_bounce[1, :]))
    p.waits("sync", [p.tok("G")])
    t3 = p.dma("sync", negm_col[:],
               d_bounce[0, :].rearrange("(a p) -> p a", p=128),
               allow_slow_non_contiguous=True)
    t4 = p.dma("sync", rcol[:],
               d_bounce[1, :].rearrange("(a p) -> p a", p=128),
               allow_slow_non_contiguous=True)
    p.waits("vector", [t3, t4])
    ct = p.vop("tensor_scalar", out=negm_col[:], in0=negm_col[:],
               scalar1=-1.0, scalar2=None, op0=ALU.mult)
    return tok, ct


def emit_moe(cx, xr, xl, negm_col, rcol, pk0, pk1, pk2, pk3, pw0, pw1,
             xl_tok, cols_tok, debug=False):
    p = cx.p
    d = cx.dram
    idf = cx.sb["c_id128f"]
    ones1f = cx.sb["c_ones_1x128f"]
    onescf = cx.sb["c_onescol_f"]
    oneh8 = cx.sb["c_oneh8"]
    wg = cx.sget("moe_wg", [128, 64], F32)
    bgrow = cx.sget("moe_bgrow", [1, 8], F32)
    b1c = cx.sget("moe_b1c", [128, 64], F32)

    t_wg = p.dma("sync", wg[:], d["L0_ffn_Wg"].rearrange("(c p) e -> p c e", p=128),
                 allow_slow_non_contiguous=True)
    t_bg = p.dma("sync", bgrow[:], d["L0_ffn_bg"].rearrange("(x e) -> x e", x=1))
    t_b1 = p.dma("sync", b1c[:], d["L0_ffn_b1"].rearrange("e (c p) -> p e c", p=128),
                 allow_slow_non_contiguous=True)
    masks_fm = cx.sget("moe_masksfm", [8, T], BF16)
    cx.push_scope()
    S_sb = cx.sget("moe_S", [8, T], F32)
    csumr = cx.sget("moe_csumr", [1, 8], F32)
    csumb = cx.sget("moe_csumb", [128, 8], F32)
    bgb = cx.sget("moe_bgb", [128, 8], F32)
    lg = cx.sget("moe_lg", [128, 128], F32)
    eq = cx.sget("moe_eq", [128, 128], F32)
    cum = cx.sget("moe_cum", [128, 128], F32)
    cum2 = cx.sget("moe_cum2", [128, 128], F32)
    red = cx.sget("moe_red", [128, 16], F32)
    p.waits("tensor", [t_wg, xl_tok, cols_tok])
    for tqb in range(T // 512):
        mt = None
        for kc in range(DMC):
            mt = p.mm(pk0[0:8, 0:512], wg[:, kc * 8:(kc + 1) * 8],
                      xr[:, kc * T + tqb * 512: kc * T + (tqb + 1) * 512],
                      start=(kc == 0), stop=(kc == DMC - 1))
        p.waits("vector", [mt])
        ev = p.vop("tensor_copy", out=S_sb[:, tqb * 512:(tqb + 1) * 512],
                   in_=pk0[0:8, 0:512])
        p.waits("tensor", [ev])
    for kc in range(DMC):
        mt = p.mm(pk1[0:1, 0:8], onescf[:], wg[:, kc * 8:(kc + 1) * 8],
                  start=(kc == 0), stop=(kc == DMC - 1))
    p.waits("vector", [mt])
    ev = p.vop("tensor_copy", out=csumr[:], in_=pk1[0:1, 0:8])
    p.waits("tensor", [ev, t_bg])
    m1 = p.mm(pk2[:, 0:8], ones1f[:], csumr[:])
    m2 = p.mm(pk3[:, 0:8], ones1f[:], bgrow[:])
    p.waits("vector", [m1, m2])
    p.vop("tensor_copy", out=csumb[:], in_=pk2[:, 0:8])
    ev = p.vop("tensor_copy", out=bgb[:], in_=pk3[:, 0:8])
    p.vfence()
    S_tm = cx.sget("moe_Stm", [128, 128], F32)
    p.waits("tensor", [ev])
    ev_t = [None, None]
    for tc in range(16):
        pkx = pk2 if tc % 2 == 0 else pk3
        p.waits("tensor", [ev_t[tc % 2]])
        ttok = p.tr(pkx[:, 0:8], S_sb[0:8, tc * 128:(tc + 1) * 128],
                    idf[0:8, 0:8])
        p.waits("vector", [ttok])
        ev_t[tc % 2] = p.vop("tensor_copy", out=S_tm[:, tc * 8:(tc + 1) * 8],
                             in_=pkx[:, 0:8])
    p.vfence()
    for tc in range(16):
        sl = slice(tc * 8, (tc + 1) * 8)
        p.vop("scalar_tensor_tensor", out=lg[:, sl], in0=csumb[:],
              scalar=negm_col[:, tc:tc + 1], in1=S_tm[:, sl],
              op0=ALU.mult, op1=ALU.add)
    p.vfence()
    for tc in range(16):
        sl = slice(tc * 8, (tc + 1) * 8)
        p.vop("tensor_scalar", out=lg[:, sl], in0=lg[:, sl],
              scalar1=rcol[:, tc:tc + 1], scalar2=None, op0=ALU.mult)
    p.vfence()
    for tc in range(16):
        sl = slice(tc * 8, (tc + 1) * 8)
        p.vop("tensor_tensor", out=lg[:, sl], in0=lg[:, sl], in1=bgb[:],
              op=ALU.add)
    p.vfence()
    for tc in range(16):
        p.vop("tensor_reduce", out=red[:, tc:tc + 1], in_=lg[:, tc * 8:(tc + 1) * 8],
              axis=mybir.AxisListType.X, op=ALU.max)
    p.vfence()
    for tc in range(16):
        p.vop("tensor_tensor", out=eq[:, tc * 8:(tc + 1) * 8],
              in0=lg[:, tc * 8:(tc + 1) * 8],
              in1=red[:, tc:tc + 1].broadcast_to([128, 8]), op=ALU.is_equal)
    p.vfence()
    eq3 = eq[:].rearrange("p (a b) -> p a b", b=8)
    cum3 = cum[:].rearrange("p (a b) -> p a b", b=8)
    c23 = cum2[:].rearrange("p (a b) -> p a b", b=8)
    p.vop("tensor_copy", out=cum3[:, :, 0:1], in_=eq3[:, :, 0:1])
    p.vop("tensor_tensor", out=cum3[:, :, 1:8], in0=eq3[:, :, 1:8],
          in1=eq3[:, :, 0:7], op=ALU.add)
    p.vfence()
    p.vop("tensor_copy", out=c23[:, :, 0:2], in_=cum3[:, :, 0:2])
    p.vop("tensor_tensor", out=c23[:, :, 2:8], in0=cum3[:, :, 2:8],
          in1=cum3[:, :, 0:6], op=ALU.add)
    p.vfence()
    p.vop("tensor_copy", out=cum3[:, :, 0:4], in_=c23[:, :, 0:4])
    p.vop("tensor_tensor", out=cum3[:, :, 4:8], in0=c23[:, :, 4:8],
          in1=c23[:, :, 0:4], op=ALU.add)
    p.vfence()
    p.vop("tensor_scalar", out=cum[:], in0=cum[:], scalar1=1.0, scalar2=None,
          op0=ALU.is_equal)
    p.vfence()
    mk = p.vop("tensor_tensor", out=eq[:], in0=eq[:], in1=cum[:], op=ALU.mult)
    if debug:
        p.waits("gpsimd", [mk])
        p.dma("gpsimd", cx.dram["dbg_lg"], lg[:])
        p.dma("gpsimd", cx.dram["dbg_eq"], eq[:])
        p.dma("gpsimd", cx.dram["dbg_S"], S_sb[:])
        p.dma("gpsimd", cx.dram["dbg_red"], red[:])
        p.dma("gpsimd", cx.dram["dbg_cum"], cum[:])
        p.dma("gpsimd", cx.dram["dbg_mr"][0:128, :], negm_col[:])
        p.dma("gpsimd", cx.dram["dbg_mr"][128:256, :], rcol[:])
    p.waits("tensor", [mk])
    ev_t = [None, None]
    for tc in range(16):
        pkx = pk2 if tc % 2 == 0 else pk3
        p.waits("tensor", [ev_t[tc % 2]])
        ttok = p.tr(pkx[0:8, 0:128], eq[:, tc * 8:(tc + 1) * 8], idf[:])
        p.waits("vector", [ttok])
        ev_t[tc % 2] = p.vop("tensor_copy",
                             out=masks_fm[:, tc * 128:(tc + 1) * 128],
                             in_=pkx[0:8, 0:128])
    p.waits("tensor", [ev_t[0], ev_t[1]])
    cx.pop_scope()
    stg_sl = [cx.sget("xio_stg0", [128, DM], F32),
              cx.sget("xio_stg1", [128, DM], F32)]
    wb1 = cx.sget("moe_wb1", [128, 2 * 8 * DM], BF16)
    wb2 = cx.sget("moe_wb2", [128, 2 * 8 * DM], BF16)
    h = cx.sget("moe_h", [128, 8 * 512], BF16)
    mbc1 = cx.sget("moe_mbc", [128, T], BF16)
    b2r = cx.sget("moe_b2r", [1, 2 * DM], BF16)
    w_rd = [[None, None], [None, None]]
    b2_rd = [None, None]
    mbc_rd_t = None
    mbc_rd_v = None
    cast_done = [[None, None], [None, None]]
    b2c_done = [None, None]
    mbc_tok = None
    slot_cast = [None, None]

    def pf_make(e):
        buf = e % 2
        pend = []

        def dma_op(j):
            def f():
                s = j % 2
                p.waits("sync", [slot_cast[s]])
                if j < 16:
                    w_i, kc = divmod(j, 8)
                    lt = p.dma("sync", stg_sl[s][:],
                               d["L0_ffn_W1" if w_i == 0 else "L0_ffn_W2"]
                               [e, kc * 128:(kc + 1) * 128, :])
                    pend.append((w_i, kc, s, lt))
                else:
                    lt = p.dma("sync", stg_sl[s][0:1, :],
                               d["L0_ffn_b2"][e:e + 1, :])
                    pend.append((2, 0, s, lt))
            return f

        def cast_op():
            def f():
                w_i, kc, s, lt = pend.pop(0)
                lt = p.tok("L")
                if w_i == 0:
                    p.waits("vector", [lt, w_rd[0][buf]])
                    tk = p.vop("tensor_copy",
                               out=wb1[:, (buf * 8 + kc) * DM:
                                       (buf * 8 + kc + 1) * DM],
                               in_=stg_sl[s][:])
                    cast_done[0][buf] = tk
                elif w_i == 1:
                    p.waits("vector", [lt, w_rd[1][buf]])
                    tk = p.vop("tensor_scalar",
                               out=wb2[:, (buf * 8 + kc) * DM:
                                       (buf * 8 + kc + 1) * DM],
                               in0=stg_sl[s][:], scalar1=0.5,
                               scalar2=None, op0=ALU.mult)
                    cast_done[1][buf] = tk
                else:
                    p.waits("vector", [lt, b2_rd[buf]])
                    tk = p.vop("tensor_scalar",
                               out=b2r[0:1, buf * DM:(buf + 1) * DM],
                               in0=stg_sl[s][0:1, :],
                               scalar1=0.5, scalar2=None, op0=ALU.mult)
                    b2c_done[buf] = tk
                slot_cast[s] = tk
            return f

        ops = [dma_op(0), dma_op(1)]
        for j in range(2, 17):
            ops.append(cast_op())
            ops.append(dma_op(j))
        ops.append(cast_op())
        ops.append(cast_op())
        return ops

    for f in pf_make(0):
        f()

    out_evac = None
    h_mask_tok = None
    gelu_tok = [None, None]
    w2_done = None
    final_tok = None
    mbc_cp = [None, None]
    for e in range(NEXP):
        buf = e % 2
        for tqb in range(T // 512):
            pkx = pk2 if tqb % 2 == 0 else pk3
            p.waits("tensor", [mbc_rd_t, mbc_cp[tqb % 2]])
            mt = p.mm(pkx[:, 0:512], oneh8[:, e * 128:(e + 1) * 128],
                      masks_fm[:, tqb * 512:(tqb + 1) * 512])
            p.waits("vector", [mt, mbc_rd_v])
            mbc_cp[tqb % 2] = p.vop(
                "tensor_copy",
                out=mbc1[:, tqb * 512:(tqb + 1) * 512],
                in_=pkx[:, 0:512])
        mbc_tok = mbc_cp[1]
        pf = pf_make(e + 1) if e + 1 < NEXP else []

        for tqb in range(T // 512):
            a = (len(pf) * tqb) // 4
            b = (len(pf) * (tqb + 1)) // 4
            for f in pf[a:b]:
                f()
            tsl = slice(tqb * 512, (tqb + 1) * 512)
            for ic in range(DMC):
                pk = pk0 if ic % 2 == 0 else pk1
                p.waits("tensor", [cast_done[0][buf], gelu_tok[ic % 2],
                                   xl_tok])
                mt = None
                for kc in range(DMC):
                    mt = p.mm(pk[:, 0:512],
                              wb1[:, (buf * 8 + kc) * DM + ic * 128:
                                  (buf * 8 + kc) * DM + (ic + 1) * 128],
                              xl[:, kc * T + tqb * 512: kc * T + (tqb + 1) * 512],
                              start=(kc == 0), stop=(kc == DMC - 1))
                w_rd[0][buf] = mt
                p.waits("scalar", [mt, w2_done, t_b1])
                gelu_tok[ic % 2] = p.sop(
                    "activation", out=h[:, ic * 512:(ic + 1) * 512],
                    in_=pk[:, 0:512], func=AF.Gelu,
                    bias=b1c[:, e * 8 + ic: e * 8 + ic + 1])
                p.waits("vector", [gelu_tok[ic % 2], mbc_tok])
                h_mask_tok = p.vop(
                    "tensor_tensor", out=h[:, ic * 512:(ic + 1) * 512],
                    in0=h[:, ic * 512:(ic + 1) * 512],
                    in1=mbc1[:, tsl], op=ALU.mult)
                mbc_rd_v = h_mask_tok
            for ohalf in range(2):
                p.waits("tensor", [h_mask_tok, out_evac, cast_done[1][buf],
                                   b2c_done[buf]])
                for o4 in range(4):
                    o = ohalf * 4 + o4
                    pw = pw0 if o4 < 2 else pw1
                    psl = pw[:, (o4 % 2) * 512:(o4 % 2) * 512 + 512]
                    for ic in range(DMC):
                        mt = p.mm(psl,
                                  wb2[:, (buf * 8 + ic) * DM + o * 128:
                                      (buf * 8 + ic) * DM + (o + 1) * 128],
                                  h[:, ic * 512:(ic + 1) * 512],
                                  start=(ic == 0), stop=False)
                    mt = p.mm(psl, b2r[0:1, buf * DM + o * 128:
                                       buf * DM + (o + 1) * 128],
                              mbc1[0:1, tsl], start=False, stop=True)
                    w2_done = mt
                    w_rd[1][buf] = mt
                    b2_rd[buf] = mt
                    mbc_rd_t = mt
                p.waits("vector", [mt])
                for o4 in range(4):
                    o = ohalf * 4 + o4
                    pw = pw0 if o4 < 2 else pw1
                    psl = pw[:, (o4 % 2) * 512:(o4 % 2) * 512 + 512]
                    out_evac = p.vop(
                        "tensor_tensor",
                        out=xr[:, o * T + tqb * 512: o * T + (tqb + 1) * 512],
                        in0=xr[:, o * T + tqb * 512: o * T + (tqb + 1) * 512],
                        in1=psl, op=ALU.add)
            final_tok = out_evac
    return final_tok


def moe_dram_decls(nc, d):
    for nm, shp in [("L0_ffn_Wg", [DM, 8]), ("L0_ffn_bg", [8]),
                    ("L0_ffn_W1", [8, DM, DM]), ("L0_ffn_b1", [8, DM]),
                    ("L0_ffn_W2", [8, DM, DM]), ("L0_ffn_b2", [8, DM])]:
        d[nm] = nc.dram_tensor(nm, shp, F32, kind="ExternalInput").ap()


def build_moe_program(debug=False):
    nc = bass.Bass("TRN2", target_bir_lowering=False)
    es = ExitStack()
    p = Prog(nc, es)
    cx = Ctx(nc, p, es)
    d = cx.dram
    d["x"] = nc.dram_tensor("x", [T, DM], F32, kind="ExternalInput").ap()
    moe_dram_decls(nc, d)
    for nm, arr in host_constants().items():
        d[nm] = nc.dram_tensor(nm, list(arr.shape),
                               BF16 if arr.dtype == BF else F32,
                               kind="ExternalInput").ap()
    d["x_out"] = nc.dram_tensor("x_out", [T, DM], F32, kind="ExternalOutput").ap()
    d["d_rows"] = nc.dram_tensor("d_rows", [2, T], F32).ap()
    if debug:
        d["dbg_lg"] = nc.dram_tensor("dbg_lg", [128, 128], F32, kind="ExternalOutput").ap()
        d["dbg_eq"] = nc.dram_tensor("dbg_eq", [128, 128], F32, kind="ExternalOutput").ap()
        d["dbg_S"] = nc.dram_tensor("dbg_S", [8, T], F32, kind="ExternalOutput").ap()
        d["dbg_red"] = nc.dram_tensor("dbg_red", [128, 16], F32, kind="ExternalOutput").ap()
        d["dbg_mr"] = nc.dram_tensor("dbg_mr", [256, 16], F32, kind="ExternalOutput").ap()
        d["dbg_cum"] = nc.dram_tensor("dbg_cum", [128, 128], F32, kind="ExternalOutput").ap()
    with es:
        pk0 = cx.psum("pk0", [128, 512])
        pk1 = cx.psum("pk1", [128, 512])
        pk2 = cx.psum("pk2", [128, 512])
        pk3 = cx.psum("pk3", [128, 512])
        pw0 = cx.psum("pw0", [128, 1024])
        pw1 = cx.psum("pw1", [128, 1024])
        cx.sb["c_trf"] = pk0
        xr = cx.sbuf("xr", [128, DMC * T], F32)
        xl = cx.sbuf("xl", [128, DMC * T], BF16)
        negm = cx.sbuf("negm", [128, 16], F32)
        rcol = cx.sbuf("rcol", [128, 16], F32)
        ctok = load_common_consts(cx, needed=[
            "c_id128f", "c_ones_1x128b", "c_onescol_b", "c_ones_1x128f",
            "c_onescol_f", "c_oneh8"])
        p.waits("tensor", [ctok])
        xtok = emit_load_x_fm(cx, d["x"], xr, pw0, pw1)
        p.barrier()
        lntok, coltok = emit_ln_rows(cx, xr, xl, pk0, pk1, pk2, pk3,
                                     negm, rcol, d["d_rows"])
        p.barrier()
        emit_moe(cx, xr, xl, negm, rcol, pk0, pk1, pk2, pk3, pw0, pw1,
                 lntok, coltok, debug=debug)
        p.barrier()
        stok = emit_store_x_fm(cx, xr, d["x_out"], pw0, pw1)
        p.waits("gpsimd", [stok])
        p.finalize()
    return nc


def make_b2sel(b2):
    out = np.zeros((8, 4, DM), dtype=np.float32)
    for e in range(8):
        out[e, e // 2] = b2[e]
    return out.reshape(8, 4 * DM)


def attn_dram_decls(nc, d):
    for nm, shp in [("L1_attn_Wq", [DM, DM]), ("L1_attn_Wk", [DM, DM]),
                    ("L1_attn_Wv", [DM, DM]), ("L1_attn_Wo", [DM, DM]),
                    ("L1_attn_bo", [DM])]:
        d[nm] = nc.dram_tensor(nm, shp, F32, kind="ExternalInput").ap()
    d["c_qrowsM"] = nc.dram_tensor("c_qrowsM", [16 * AH, T], BF16,
                                   kind="ExternalInput").ap()
    d["d_q"] = nc.dram_tensor("d_q", [DMC, 128, T], BF16).ap()
    d["d_kvK_in"] = nc.dram_tensor("d_kvK_in", [2048, DM], BF16).ap()
    d["d_kvK_out"] = nc.dram_tensor("d_kvK_out", [2 * 2048, DM], BF16).ap()
    d["d_kvV_in"] = nc.dram_tensor("d_kvV_in", [2048, DM], BF16).ap()
    d["d_kvV_out"] = nc.dram_tensor("d_kvV_out", [2 * 2048, DM], BF16).ap()
    d["d_k2_in"] = nc.dram_tensor("d_k2_in", [1, DM], BF16).ap()
    d["d_k2_out"] = nc.dram_tensor("d_k2_out", [2, DM], BF16).ap()
    d["c_selAB"] = nc.dram_tensor("c_selAB", [16, 4096], BF16,
                                  kind="ExternalInput").ap()
    d["c_biasrows"] = nc.dram_tensor("c_biasrows", [AH, 12, 4096], BF16,
                                     kind="ExternalInput").ap()
    d["c_corr"] = nc.dram_tensor("c_corr", [AH, 128, 8 * 512], BF16,
                                 kind="ExternalInput").ap()


def emit_attn(cx, xr, xl, qoff, pk0, pk1, pk2, pk3, pw0, pw1, xl_tok,
              debug=False):
    p = cx.p
    d = cx.dram
    idf = cx.sb["c_id128f"]
    ones1f = cx.sb["c_ones_1x128f"]
    ones2 = cx.sb["c_ones2"]
    ctxU = xl
    bocol = cx.sget("at_bocol", [128, 8], F32)
    q2m = cx.sget("at_q2m", [2, 8], F32)
    k2m = cx.sget("at_k2m", [2, 8], F32)
    k2m2 = cx.sget("at_k2m2", [2, 16], BF16)
    mrow = cx.sget("at_mrow", [1, 16], F32)
    negM = cx.sget("at_negM", [128, 16], F32)
    red2 = cx.sget("at_red2", [2, 1], F32)
    kvstg = [cx.sget(f"xio_stg{i}", [128, DM], F32).bitcast(BF16)
             for i in range(2)]
    sq = cx.sget("at_sq", [128, 512], BF16)
    rl = cx.sget("at_rl", [1, 512], F32)
    rlb = cx.sget("at_rlb", [64, 512], BF16)
    stage = cx.sget("at_wstg", [128, 256], F32)
    wT = [cx.sget("at_wT0", [128, 8 * DM], BF16),
          cx.sget("at_wT1", [128, 8 * DM], BF16)]
    kaux = [cx.sget(f"at_kaux{i}", [92, 2 * T], BF16) for i in range(2)]
    qt = [cx.sget(f"at_qt{i}", [92, T], BF16) for i in range(2)]
    vt_t = [cx.sget(f"at_vt{i}", [128, 32 * 65], BF16) for i in range(2)]
    corr = [cx.sget(f"at_corr{i}", [128, 8 * 512], BF16) for i in range(2)]
    ptile2 = [cx.sget(f"at_pt2{i}", [128, 1024], BF16) for i in range(2)]

    botok = p.dma("sync", bocol[:], d["L1_attn_bo"].rearrange("(c p) -> p c", p=128),
                  allow_slow_non_contiguous=True)
    z1 = p.vop("memset", q2m[:], -1e30)
    z2 = p.vop("memset", k2m[:], -1e30)
    p.vfence()

    def transpose_weight_into(dst, src_dram, scale=None, guard=(), dst_guard=()):
        ld_tok = [None, None]
        tr_done = [None, None]
        ev = [list(guard), list(guard)]
        dg = list(dst_guard)
        vt = None
        for b in range(8):
            i = b % 2
            pws = pw0 if i == 0 else pw1
            for a in range(8):
                p.waits("sync", [tr_done[0], tr_done[1]])
                ld_tok[i] = p.dma("sync", stage[:, i * 128:(i + 1) * 128],
                                  src_dram[a * 128:(a + 1) * 128,
                                           b * 128:(b + 1) * 128])
                p.waits("tensor", [ld_tok[i]] + ev[i])
                tr_done[i] = p.mm(pws[:, a * 128:(a + 1) * 128],
                                  stage[:, i * 128:(i + 1) * 128], idf[:])
            p.waits("vector", [tr_done[i]] + dg)
            dg = []
            if scale is None:
                vt = p.vop("tensor_copy", out=dst[:, b * DM:(b + 1) * DM],
                           in_=pws[:, 0:DM])
            else:
                vt = p.vop("tensor_scalar", out=dst[:, b * DM:(b + 1) * DM],
                           in0=pws[:, 0:DM], scalar1=float(scale),
                           scalar2=None, op0=ALU.mult)
            ev[i] = [vt]
        return vt

    wtokQ = transpose_weight_into(wT[0], d["L1_attn_Wq"], scale=0.125)
    ev_rot = [None, None]
    st_done = [None, None]
    sq_done = None
    q2g = None
    for oc in range(DMC):
        sl = oc % 2
        for tqb in range(T // 512):
            i = (oc * 4 + tqb) % 2
            pkx = pk0 if i == 0 else pk1
            p.waits("tensor", [ev_rot[i], sq_done, wtokQ, xl_tok])
            mt = None
            for kc in range(DMC):
                mt = p.mm(pkx[:, 0:512],
                          wT[0][:, kc * DM + oc * 128: kc * DM + (oc + 1) * 128],
                          xl[:, kc * T + tqb * 512: kc * T + (tqb + 1) * 512],
                          start=(kc == 0), stop=(kc == DMC - 1))
            p.waits("vector", [mt] + ([st_done[0], st_done[1]]
                                      if tqb == 0 else []))
            ev_rot[i] = p.vop("tensor_copy",
                              out=kvstg[sl][:, tqb * 512:(tqb + 1) * 512],
                              in_=pkx[:, 0:512])
            if tqb == 3:
                p.waits("gpsimd", [ev_rot[0], ev_rot[1]])
                st_done[sl] = p.dma("gpsimd", d["d_q"][oc],
                                    kvstg[sl][:, 0:2048])
            p.waits("scalar", [ev_rot[i]])
            sq_done = p.sop("activation", out=sq[:], in_=pkx[:, 0:512],
                            func=AF.Square)
            p.waits("tensor", [sq_done, q2g])
            m2 = p.mm(pk2[0:2, 0:512], ones2[:], sq[:])
            p.waits("vector", [m2])
            p.vop("tensor_reduce", out=red2[:], in_=pk2[0:2, 0:512],
                  axis=mybir.AxisListType.X, op=ALU.max)
            p.vfence()
            q2g = p.vop("tensor_tensor", out=q2m[:, oc:oc + 1],
                        in0=q2m[:, oc:oc + 1], in1=red2[:], op=ALU.max)
            p.vfence()
    w0_rd = mt
    qsp_last = st_done[i]
    wtokK = transpose_weight_into(wT[1], d["L1_attn_Wk"])
    kv_view = d["d_kvK_in"].rearrange("(f two) c -> f two c", two=2)
    for oc in range(DMC):
        for tqb in range(T // 512):
            i = (oc * 4 + tqb) % 2
            pkx = pk0 if i == 0 else pk1
            p.waits("tensor", [ev_rot[i], sq_done, wtokK])
            mt = None
            for kc in range(DMC):
                mt = p.mm(pkx[:, 0:512],
                          wT[1][:, kc * DM + oc * 128: kc * DM + (oc + 1) * 128],
                          xl[:, kc * T + tqb * 512: kc * T + (tqb + 1) * 512],
                          start=(kc == 0), stop=(kc == DMC - 1))
            p.waits("vector", [mt, st_done[0], st_done[1]])
            ev_rot[i] = p.vop("tensor_copy", out=kvstg[i][:, 0:512],
                              in_=pkx[:, 0:512])
            p.waits("gpsimd", [ev_rot[i]])
            tv = tqb * 512
            st_done[i] = p.dma(
                "gpsimd",
                kv_view[oc * 128:(oc + 1) * 128,
                        tv // 1024, tv % 1024: tv % 1024 + 512],
                kvstg[i][:, 0:512])
            p.waits("scalar", [ev_rot[i]])
            sq_done = p.sop("activation", out=sq[:], in_=pkx[:, 0:512],
                            func=AF.Square)
            p.waits("tensor", [sq_done, q2g])
            m2 = p.mm(pk2[0:2, 0:512], ones2[:], sq[:])
            p.waits("vector", [m2])
            p.vop("tensor_reduce", out=red2[:], in_=pk2[0:2, 0:512],
                  axis=mybir.AxisListType.X, op=ALU.max)
            p.vfence()
            q2g = p.vop("tensor_tensor", out=k2m[:, oc:oc + 1],
                        in0=k2m[:, oc:oc + 1], in1=red2[:], op=ALU.max)
            p.vfence()
    w1_rd = mt
    k2mb = cx.sget("at_k2mb", [2, 8], BF16)
    p.waits("vector", [q2g])
    cv = p.vop("tensor_copy", out=k2mb[:], in_=k2m[:])
    p.waits("gpsimd", [cv, st_done[0], st_done[1]])
    k2tok = p.dma("gpsimd", d["d_k2_in"][0:1, 0:16].rearrange(
        "x (a b) -> (x a) b", a=2), k2mb[:])
    p.waits("gpsimd", [k2tok])
    grp = [[0, 1], [2, 3], [4, 5], [6, 7]]
    cc_k2 = p.collective("AllGather", ALU.bypass, grp,
                         [d["d_k2_in"].opt()], [d["d_k2_out"].opt()])
    if ATTN_SERIAL_CC:
        p.waits("gpsimd", [cc_k2])
    cc_K = p.collective("AllGather", ALU.bypass, grp,
                        [d["d_kvK_in"].opt()], [d["d_kvK_out"].opt()])
    p.waits("sync", [cc_k2])
    t1 = p.dma("sync", k2m2[:, 0:8], d["d_k2_out"][0:1, 0:16].rearrange(
        "x (a b) -> (x a) b", a=2))
    t2 = p.dma("sync", k2m2[:, 8:16], d["d_k2_out"][1:2, 0:16].rearrange(
        "x (a b) -> (x a) b", a=2))
    p.waits("vector", [t1, t2])
    p.vop("tensor_tensor", out=k2m[:], in0=k2m2[:, 0:8], in1=k2m2[:, 8:16],
          op=ALU.max)
    p.vfence()
    p.vop("tensor_scalar", out=q2m[:], in0=q2m[:], scalar1=64.0, scalar2=None,
          op0=ALU.mult)
    p.vfence()
    p.vop("tensor_tensor", out=k2m[:], in0=k2m[:], in1=q2m[:], op=ALU.add)
    p.vfence()
    mtk2 = p.vop("tensor_scalar", out=k2m[:], in0=k2m[:], scalar1=1.0 / 16,
                 scalar2=None, op0=ALU.mult)
    p.waits("gpsimd", [mtk2])
    bt = p.dma("gpsimd", d["d_rows"][0:1, 0:16].rearrange("x (a b) -> (x a) b", a=2),
               k2m[:])
    p.waits("sync", [bt])
    bt2 = p.dma("sync", mrow[:], d["d_rows"][0:1, 0:16])
    p.waits("tensor", [bt2, q2g])
    mmt = p.mm(pk2[:, 0:16], ones1f[:], mrow[:])
    p.waits("vector", [mmt])
    negtok = p.vop("tensor_scalar", out=negM[:], in0=pk2[:, 0:16], scalar1=-1.0,
                   scalar2=None, op0=ALU.mult)
    p.vfence()
    wtokV = transpose_weight_into(wT[0], d["L1_attn_Wv"], guard=[wtokK],
                                  dst_guard=[w0_rd])
    evv = [None, None]
    for tc in range(T // 128):
        i = tc % 2
        for ohalf in range(2):
            pwx = pw0 if ohalf == 0 else pw1
            p.waits("tensor", [evv[ohalf], wtokV])
            mt = None
            for kc in range(DMC):
                mt = p.mm(pwx[:, 0:512],
                          xl[:, kc * T + tc * 128: kc * T + (tc + 1) * 128],
                          wT[0][:, kc * DM + ohalf * 512: kc * DM + (ohalf + 1) * 512],
                          start=(kc == 0), stop=(kc == DMC - 1))
            p.waits("vector", [mt, st_done[0], st_done[1]])
            evv[ohalf] = p.vop("tensor_copy",
                               out=kvstg[i][:, ohalf * 512:(ohalf + 1) * 512],
                               in_=pwx[:, 0:512])
        p.waits("gpsimd", [evv[0], evv[1]])
        st_done[i] = p.dma("gpsimd", d["d_kvV_in"][tc * 128:(tc + 1) * 128, :],
                           kvstg[i][:, 0:1024])
    p.waits("gpsimd", [st_done[0], st_done[1]])
    if ATTN_SERIAL_CC:
        p.waits("gpsimd", [cc_K])
    cc_V = p.collective("AllGather", ALU.bypass, grp,
                        [d["d_kvV_in"].opt()], [d["d_kvV_out"].opt()])
    wtokO = transpose_weight_into(wT[1], d["L1_attn_Wo"],
                                  guard=[evv[0], evv[1]], dst_guard=[w1_rd])

    def c_union(tqb):
        pos = []
        for qo in (0, 2048):
            for k2_ in range(4):
                tk0 = qo + tqb * 512 + k2_ * 128
                if 0 <= tk0 < 4096:
                    pos.append(tk0 // 128)
        return pos

    kv_blkK = [d["d_kvK_out"][r * 2048:(r + 1) * 2048, :]
               .rearrange("(f two) c -> f two c", two=2) for r in range(2)]
    vv_blk = [d["d_kvV_out"][r * 2048:(r + 1) * 2048, :] for r in range(2)]
    kaux_rd = [None, None]
    qt_rd = [None, None]
    vt_rd = [None, None]
    corr_rd = [None, None]
    ptile_rd = [None, None]
    pS_ev = [wtokO, wtokO]

    def pf_head(h):
        buf = h % 2
        t = {}
        p.waits("vector", [vt_rd[buf]])
        mz = p.vop("memset", vt_t[buf][:], 1.0)
        p.waits("sync", [kaux_rd[buf], cc_K])
        t["k0"] = p.dma("sync",
                        kaux[buf][0:64, 0:T].rearrange("p (a b) -> p a b", b=1024),
                        kv_blkK[0][h * 64:(h + 1) * 64, :, :])
        t["k1"] = p.dma("sync",
                        kaux[buf][0:64, T:2 * T].rearrange("p (a b) -> p a b", b=1024),
                        kv_blkK[1][h * 64:(h + 1) * 64, :, :])
        t["sel"] = p.dma("sync", kaux[buf][64:80, :], d["c_selAB"])
        t["bias"] = p.dma("sync", kaux[buf][80:92, :], d["c_biasrows"][h])
        p.waits("sync", [qt_rd[buf], qsp_last])
        t["q"] = p.dma("sync", qt[buf][0:64, :],
                       d["d_q"][h // 2, (h % 2) * 64:(h % 2) * 64 + 64, :])
        t["qr"] = p.dma("sync", qt[buf][64:80, :],
                        d["c_qrowsM"][16 * h:16 * (h + 1), :])
        t["on"] = p.dma("sync", qt[buf][80:92, :], d["c_onesq"])
        p.waits("sync", [mz, cc_V])
        t["v0"] = p.dma("sync",
                        vt_t[buf][:].rearrange("p (c e) -> p c e", e=65)[:, 0:16, 0:64],
                        vv_blk[0][:, h * 64:(h + 1) * 64]
                        .rearrange("(c pp2) dd -> pp2 c dd", pp2=128))
        t["v1"] = p.dma("sync",
                        vt_t[buf][:].rearrange("p (c e) -> p c e", e=65)[:, 16:32, 0:64],
                        vv_blk[1][:, h * 64:(h + 1) * 64]
                        .rearrange("(c pp2) dd -> pp2 c dd", pp2=128))
        p.waits("sync", [corr_rd[buf]])
        t["corr"] = p.dma("sync", corr[buf][:], d["c_corr"][h])
        return t

    pf = [pf_head(0), pf_head(1)]
    ctx_ev = [None, None]
    amt = None
    for h in range(AH):
        if ATTN_HEAD_BARRIER:
            p.barrier()
        buf = h % 2
        colh = (h % 2) * 8 + h // 2
        t = pf[buf]
        ktoks = [t["k0"], t["k1"], t["sel"], t["bias"], t["q"], t["qr"],
                 t["on"]]
        for tqb in range(T // 512):
            if tqb == 1 and h + 1 < AH:
                pf[(h + 1) % 2] = pf_head(h + 1)
            cpos = c_union(tqb)
            ic = tqb % 2
            pctx = pk2 if ic == 0 else pk3
            for j in range(16):
                ip = j % 2
                pS = pw0 if ip == 0 else pw1
                tka, tkb = 2 * j, 2 * j + 1
                waits_t = [pS_ev[ip]]
                if tqb == 0 and j == 0:
                    waits_t += ktoks
                p.waits("tensor", waits_t)
                p.mm(pS[:, 0:512], kaux[buf][:, tka * 128:(tka + 1) * 128],
                     qt[buf][:, tqb * 512:(tqb + 1) * 512], inc=False)
                smt = p.mm(pS[:, 512:1024],
                           kaux[buf][:, tkb * 128:(tkb + 1) * 128],
                           qt[buf][:, tqb * 512:(tqb + 1) * 512])
                kaux_rd[buf] = smt
                qt_rd[buf] = smt
                p.waits("scalar", [smt, negtok, ptile_rd[ip]])
                ex = p.sop("activation", out=ptile2[ip][:], in_=pS[:, 0:1024],
                           func=AF.Exp, bias=negM[:, colh:colh + 1])
                pS_ev[ip] = ex
                pa = pb = ex
                if tka in cpos:
                    di = cpos.index(tka)
                    p.waits("vector", [ex, t["corr"]])
                    pa = p.vop("tensor_tensor", out=ptile2[ip][:, 0:512],
                               in0=ptile2[ip][:, 0:512],
                               in1=corr[buf][:, di * 512:(di + 1) * 512],
                               op=ALU.mult)
                    corr_rd[buf] = pa
                if tkb in cpos:
                    di = cpos.index(tkb)
                    p.waits("vector", [ex, t["corr"]])
                    pb = p.vop("tensor_tensor", out=ptile2[ip][:, 512:1024],
                               in0=ptile2[ip][:, 512:1024],
                               in1=corr[buf][:, di * 512:(di + 1) * 512],
                               op=ALU.mult)
                    corr_rd[buf] = pb
                wt2 = [pa, pb]
                if j == 0:
                    wt2 += [ctx_ev[ic], t["v0"], t["v1"]]
                p.waits("tensor", wt2)
                p.mm(pctx[0:65, 0:512], vt_t[buf][:, tka * 65:(tka + 1) * 65],
                     ptile2[ip][:, 0:512], start=(j == 0), stop=False,
                     inc=False)
                amt = p.mm(pctx[0:65, 0:512],
                           vt_t[buf][:, tkb * 65:(tkb + 1) * 65],
                           ptile2[ip][:, 512:1024], start=False, stop=(j == 15))
                vt_rd[buf] = amt
                ptile_rd[ip] = amt
            p.waits("vector", [amt])
            p.vop("reciprocal", out=rl[:], in_=pctx[64:65, 0:512])
            p.vfence()
            rc = p.vop("tensor_copy", out=sq[0:1, 0:512], in_=rl[:])
            p.waits("tensor", [rc])
            bmt = p.mm(pk0[0:64, 0:512], cx.sb["c_ones_1x128b"][0:1, 0:64],
                       sq[0:1, 0:512])
            p.waits("vector", [bmt])
            cpt = p.vop("tensor_copy", out=rlb[:], in_=pk0[0:64, 0:512])
            p.vfence()
            ctx_ev[ic] = p.vop(
                "tensor_tensor",
                out=ctxU[(h % 2) * 64:(h % 2) * 64 + 64,
                         (h // 2) * T + tqb * 512: (h // 2) * T + (tqb + 1) * 512],
                in0=pctx[0:64, 0:512], in1=rlb[:], op=ALU.mult)
    p.barrier()
    ev_o = [None, None]
    for oc in range(DMC):
        for tqb in range(T // 512):
            i = (oc * 4 + tqb) % 2
            pkx = pk0 if i == 0 else pk1
            p.waits("tensor", [ev_o[i]])
            mt = None
            for kc in range(DMC):
                mt = p.mm(pkx[:, 0:512],
                          wT[1][:, kc * DM + oc * 128: kc * DM + (oc + 1) * 128],
                          ctxU[:, kc * T + tqb * 512: kc * T + (tqb + 1) * 512],
                          start=(kc == 0), stop=(kc == DMC - 1))
            p.waits("vector", [mt, botok])
            ev_o[i] = p.vop(
                "scalar_tensor_tensor",
                out=xr[:, oc * T + tqb * 512: oc * T + (tqb + 1) * 512],
                in0=pkx[:, 0:512], scalar=bocol[:, oc:oc + 1],
                in1=xr[:, oc * T + tqb * 512: oc * T + (tqb + 1) * 512],
                op0=ALU.add, op1=ALU.add)
    return ev_o[i]


def build_attn_program(debug=False):
    nc = bass.Bass("TRN2", target_bir_lowering=False)
    es = ExitStack()
    p = Prog(nc, es)
    cx = Ctx(nc, p, es)
    d = cx.dram
    d["x"] = nc.dram_tensor("x", [T, DM], F32, kind="ExternalInput").ap()
    attn_dram_decls(nc, d)
    for nm, arr in host_constants().items():
        d[nm] = nc.dram_tensor(nm, list(arr.shape),
                               BF16 if arr.dtype == BF else F32,
                               kind="ExternalInput").ap()
    d["x_out"] = nc.dram_tensor("x_out", [T, DM], F32, kind="ExternalOutput").ap()
    d["d_rows"] = nc.dram_tensor("d_rows", [2, T], F32).ap()
    if debug:
        d["dbg_negM"] = nc.dram_tensor("dbg_negM", [128, 16], F32,
                                       kind="ExternalOutput").ap()
        d["dbg_l"] = nc.dram_tensor("dbg_l", [64, 512], F32,
                                    kind="ExternalOutput").ap()
        d["dbg_p"] = nc.dram_tensor("dbg_p", [4, 128, 512], BF16,
                                    kind="ExternalOutput").ap()
        d["dbg_ctx"] = nc.dram_tensor("dbg_ctx", [128, DMC * T], BF16,
                                      kind="ExternalOutput").ap()
        d["dbg_wt1"] = nc.dram_tensor("dbg_wt1", [128, 8 * DM], BF16,
                                      kind="ExternalOutput").ap()
        d["dbg_wt0"] = nc.dram_tensor("dbg_wt0", [128, 8 * DM], BF16,
                                      kind="ExternalOutput").ap()
    with es:
        pk0 = cx.psum("pk0", [128, 512])
        pk1 = cx.psum("pk1", [128, 512])
        pk2 = cx.psum("pk2", [128, 512])
        pk3 = cx.psum("pk3", [128, 512])
        pw0 = cx.psum("pw0", [128, 1024])
        pw1 = cx.psum("pw1", [128, 1024])
        cx.sb["c_trf"] = pk0
        xr = cx.sbuf("xr", [128, DMC * T], F32)
        xl = cx.sbuf("xl", [128, DMC * T], BF16)
        ctok = load_common_consts(cx, needed=[
            "c_id128f", "c_ones_1x128b", "c_onescol_b", "c_ones_1x128f",
            "c_onescol_f", "c_ones2"])
        p.waits("tensor", [ctok])
        xtok = emit_load_x_fm(cx, d["x"], xr, pw0, pw1)
        p.barrier()
        cx.push_scope()
        lntok = emit_ln(cx, xr, xl, pk0, pk1, pk2, pk3)
        p.barrier()
        cx.pop_scope()
        emit_attn(cx, xr, xl, 0, pk0, pk1, pk2, pk3, pw0, pw1, lntok,
                  debug=debug)
        p.barrier()
        if debug:
            p.dma("gpsimd", d["dbg_ctx"], xl[:])
            p.dma("gpsimd", d["dbg_negM"], cx.sb["at_negM"][:])
            p.dma("gpsimd", d["dbg_wt1"], cx.sb["at_wT1"][:])
            p.dma("gpsimd", d["dbg_wt0"], cx.sb["at_wT0"][:])
            p.barrier()
        stok = emit_store_x_fm(cx, xr, d["x_out"], pw0, pw1)
        p.waits("gpsimd", [stok])
        p.finalize()
    return nc


def mamba_dram_decls(nc, d):
    for nm, shp in [("L0_attn_W_in", [DM, PROJ]), ("L0_attn_conv_w", [XBC, D_CONV]),
                    ("L0_attn_conv_b", [XBC]), ("L0_attn_A_log", [N_HEADS]),
                    ("L0_attn_D_param", [N_HEADS]), ("L0_attn_dt_bias", [N_HEADS]),
                    ("L0_attn_norm_w", [D_INNER]), ("L0_attn_W_out", [D_INNER, DM])]:
        d[nm] = nc.dram_tensor(nm, shp, F32, kind="ExternalInput").ap()
    d["d_z"] = nc.dram_tensor("d_z", [NCHUNK, 128, D_INNER], BF16).ap()
    d["d_y"] = nc.dram_tensor("d_y", [NCHUNK, 128, D_INNER], BF16).ap()
    d["d_misc"] = nc.dram_tensor("d_misc", [4, 32, 128], BF16).ap()
    d["d_halo_in"] = nc.dram_tensor("d_halo_in", [128, 51], F32).ap()
    d["d_halo_out"] = nc.dram_tensor("d_halo_out", [256, 51], F32).ap()
    d["d_state_in"] = nc.dram_tensor("d_state_in", [64, D_INNER], F32).ap()
    d["d_state_out"] = nc.dram_tensor("d_state_out", [128, D_INNER], F32).ap()
    d["d_xfm"] = nc.dram_tensor("d_xfm", [T, DM], F32).ap()


def emit_mamba(cx, pk0, pk1, pk2, pk3, pw0, pw1):
    p = cx.p
    d = cx.dram
    idf = cx.sb["c_id128f"]
    idb = cx.sb["c_id128b"]
    ones1b = cx.sb["c_ones_1x128b"]
    onescb = cx.sb["c_onescol_b"]
    selhd = cx.sb["c_selhd"]
    lt01 = cx.sb["c_lt01"]
    o3264 = cx.sb["c_ones32x64b"]
    ones32f = cx.sb["c_ones32x128f"]
    mask64 = cx.sb["c_mask64"]
    mask128 = cx.sb["c_mask128"]
    winb = cx.sget("mb_win", [128, 8 * PROJ], BF16)
    C_sb = cx.sget("mb_C", [64, T], BF16)
    cumg = cx.sget("mb_cumg", [32, T], F32)
    state = cx.sget("mb_state", [64, D_INNER], F32)
    state_b = cx.sget("mb_stateb", [64, D_INNER], BF16)
    lamr = cx.sget("mb_lamr", [32, D_INNER], BF16)
    Acol = cx.sget("mb_Acol", [32, 1], F32)
    Dcol = cx.sget("mb_Dcol", [32, 1], F32)
    dtbc = cx.sget("mb_dtbc", [32, 1], F32)
    normw = cx.sget("mb_normw", [128, 16], F32)
    wcol = cx.sget("mb_wcol", [128, 68], F32)
    bcol = cx.sget("mb_bcol", [128, 17], F32)
    halo = cx.sget("mb_halo", [128, 51], F32)
    hin = cx.sget("mb_hin", [64, D_INNER], F32)
    stage = cx.sget("mb_stage", [128, 2128], F32)
    xbc = cx.sget("mb_xbc", [128, 17 * 131], F32)
    cnv = cx.sget("mb_cnv", [128, 17 * 128], F32)
    xcv = cx.sget("mb_xcv", [128, 16 * 128], BF16)
    xcvT = cx.sget("mb_xcvT", [128, D_INNER], BF16)
    xtid = cx.sget("mb_xtid", [128, D_INNER], BF16)
    zstg = cx.sget("mb_zstg", [128, 16 * 128], BF16)
    ystg = [cx.sget(f"mb_ystg{i}", [128, 512], BF16) for i in range(2)]
    dt_c = cx.sget("mb_dt", [32, 128], F32)
    dtr = cx.sget("mb_dtr", [32, 128], F32)
    acl = cx.sget("mb_acl", [32, 128], F32)
    asc = [cx.sget(f"mb_asc{i}", [32, 128], F32) for i in range(3)]
    ahl = [cx.sget(f"mb_ahl{i}", [32, 128], BF16) for i in range(3)]
    sel32 = cx.sb["c_sel32"]
    colT = cx.sget("mb_colT", [128, 128], F32)
    B_c = cx.sget("mb_Bc", [64, 128], BF16)
    BT = cx.sget("mb_BT", [128, 64], BF16)
    M0LT = cx.sget("mb_M0LT", [128, 128], BF16)
    Dtmp = [cx.sget(f"mb_Dtmp{i}", [128, 128], F32) for i in range(2)]
    Mh = [cx.sget(f"mb_Mh{i}", [128, 128], BF16) for i in range(2)]
    Cdd = [cx.sget(f"mb_Cdd{i}", [64, 128], BF16) for i in range(2)]
    Bpt = [cx.sget(f"mb_Bpt{i}", [128, 64], BF16) for i in range(2)]
    sttmp = cx.sget("mb_sttmp", [64, 512], F32)
    ln3 = cx.sget("mb_ln3", [128, 8 * 128], BF16)
    x2c = cx.sget("mb_x2c", [128, 8 * 128], BF16)
    xc = cx.sget("mb_xc", [128, 8 * 128], F32)
    xtm_stg = cx.sget("mb_xtm", [128, DM], F32)
    cx.sb["c_trf2"] = pk3
    xfm = d["d_xfm"]
    srow = cx.sget("mb_srow", [1, 128], F32)
    srow2 = cx.sget("mb_srow2", [1, 128], F32)
    mrb = cx.sget("mb_mrb", [1, 128], BF16)
    rrb = cx.sget("mb_rrb", [1, 128], BF16)
    ytot = cx.sget("mb_ytot", [128, D_INNER], BF16)
    yfm = cx.sget("mb_yfm", [128, D_INNER], BF16)
    gf = cx.sget("mb_gf", [128, D_INNER], BF16)

    t1 = p.dma("sync", Acol[:], d["L0_attn_A_log"].rearrange("(a b) -> a b", b=1))
    t2 = p.dma("sync", Dcol[:], d["L0_attn_D_param"].rearrange("(a b) -> a b", b=1))
    t3 = p.dma("sync", dtbc[:], d["L0_attn_dt_bias"].rearrange("(a b) -> a b", b=1))
    t4 = p.dma("sync", normw[:], d["L0_attn_norm_w"].rearrange("(c p) -> p c", p=128),
               allow_slow_non_contiguous=True)
    t5 = p.dma("sync", wcol[:].rearrange("p (c k) -> p c k", k=4),
               d["L0_attn_conv_w"].rearrange("(c p) k -> p c k", p=128),
               allow_slow_non_contiguous=True)
    t6 = p.dma("sync", bcol[:], d["L0_attn_conv_b"].rearrange("(c p) -> p c", p=128),
               allow_slow_non_contiguous=True)
    p.waits("scalar", [t1])
    st = p.sop("activation", out=Acol[:], in_=Acol[:], func=AF.Exp)
    p.waits("vector", [st])
    atok = p.vop("tensor_scalar", out=Acol[:], in0=Acol[:], scalar1=-1.0,
                 scalar2=None, op0=ALU.mult)
    wtoks = []
    cast_tok = None
    for kc in range(DMC):
        for hh in range(2):
            lt = p.dma("sync", stage[:, 0:2128] if hh == 0 else stage[:, 0:2128],
                       d["L0_attn_W_in"][kc * 128:(kc + 1) * 128,
                                         hh * 2128:(hh + 1) * 2128])
            p.waits("vector", [lt, cast_tok])
            cast_tok = p.vop("tensor_copy",
                             out=winb[:, kc * PROJ + hh * 2128: kc * PROJ + (hh + 1) * 2128],
                             in_=stage[:, 0:2128])
            p.waits("sync", [cast_tok])
    win_tok = cast_tok

    def load_xc_fm(tc):
        p.waits("gpsimd", [p.tok("T"), p.tok("V"), p.tok("S")])
        ldx = p.dma("gpsimd", xtm_stg[:], xfm[tc * 128:(tc + 1) * 128, :])
        p.waits("tensor", [ldx])
        for c in range(DMC):
            p.tr(pk0[:, 0:128] if c % 2 == 0 else pk1[:, 0:128],
                 xtm_stg[:, c * 128:(c + 1) * 128], idf[:])
            tt = p.trfence(cx.sb["c_trf2"], cx.sb["c_ones_1x128b"])
            p.waits("vector", [tt])
            ev = p.vop("tensor_copy", out=xc[:, c * 128:(c + 1) * 128],
                       in_=(pk0 if c % 2 == 0 else pk1)[:, 0:128])
            p.waits("tensor", [ev])
        p.vfence()
        return p.tok("V")

    def ln_chunk(tc, out_tile, ncols=128, col0=0):
        xsl = xc[:, 0:DMC * 128].rearrange(
            "p (c t) -> p c t", t=128)[:, :, col0:col0 + ncols]
        ot = out_tile[:, 0:DMC * ncols].rearrange("p (c t) -> p c t", t=ncols)
        x2 = x2c[:, 0:DMC * ncols].rearrange("p (c t) -> p c t", t=ncols)
        p.waits("scalar", [p.tok("V")])
        ct = p.vop("tensor_copy", out=ot, in_=xsl)
        st = p.sop("activation", out=x2, in_=xsl, func=AF.Square)
        p.waits("tensor", [ct, st])
        for c in range(DMC):
            p.mm(pk0[0:1, 0:ncols], onescb[:],
                 out_tile[:, c * ncols:(c + 1) * ncols],
                 start=(c == 0), stop=(c == DMC - 1))
        for c in range(DMC):
            mt = p.mm(pk1[0:1, 0:ncols], onescb[:],
                      x2c[:, c * ncols:(c + 1) * ncols],
                      start=(c == 0), stop=(c == DMC - 1))
        p.waits("vector", [mt])
        p.vop("tensor_scalar", out=srow[0:1, 0:ncols], in0=pk0[0:1, 0:ncols],
              scalar1=1.0 / DM, scalar2=None, op0=ALU.mult)
        v1 = p.vop("tensor_scalar", out=srow2[0:1, 0:ncols], in0=pk1[0:1, 0:ncols],
                   scalar1=1.0 / DM, scalar2=EPS, op0=ALU.mult, op1=ALU.add)
        p.vfence()
        p.vop("tensor_copy", out=mrb[0:1, 0:ncols], in_=srow[0:1, 0:ncols])
        p.vop("tensor_tensor", out=srow[0:1, 0:ncols], in0=srow[0:1, 0:ncols],
              in1=srow[0:1, 0:ncols], op=ALU.mult)
        p.vfence()
        p.vop("tensor_tensor", out=srow2[0:1, 0:ncols], in0=srow2[0:1, 0:ncols],
              in1=srow[0:1, 0:ncols], op=ALU.subtract)
        p.vfence()
        v2 = p.vop("reciprocal", out=srow2[0:1, 0:ncols], in_=srow2[0:1, 0:ncols])
        p.vfence()
        p.waits("scalar", [v2])
        s2 = p.sop("activation", out=rrb[0:1, 0:ncols], in_=srow2[0:1, 0:ncols],
                   func=AF.Sqrt)
        p.waits("tensor", [s2, p.tok("V")])
        p.mm(pk0[:, 0:ncols], ones1b[:], mrb[0:1, 0:ncols])
        bm = p.mm(pk1[:, 0:ncols], ones1b[:], rrb[0:1, 0:ncols])
        p.waits("vector", [bm])
        mb = pk0[:, 0:ncols].rearrange("p (o t) -> p o t", o=1).broadcast_to(
            [128, DMC, ncols])
        rb = pk1[:, 0:ncols].rearrange("p (o t) -> p o t", o=1).broadcast_to(
            [128, DMC, ncols])
        p.vop("tensor_tensor", out=ot, in0=xsl, in1=mb, op=ALU.subtract)
        p.vfence()
        vt = p.vop("tensor_tensor", out=ot, in0=ot, in1=rb, op=ALU.mult)
        p.vfence()
        return vt

    load_xc_fm(15)
    lt3 = ln_chunk(15, ln3, ncols=3, col0=125)
    p.waits("tensor", [lt3, win_tok])
    hm = None
    for s in range(17):
        for kc in range(DMC):
            hm = p.mm(pk2[:, 0:3],
                      winb[:, kc * PROJ + D_INNER + s * 128:
                           kc * PROJ + D_INNER + (s + 1) * 128],
                      ln3[:, kc * 3:(kc + 1) * 3],
                      start=(kc == 0), stop=(kc == DMC - 1))
        p.waits("vector", [hm])
        hv = p.vop("tensor_copy", out=halo[:, s * 3:(s + 1) * 3], in_=pk2[:, 0:3])
        p.waits("tensor", [hv])
    p.waits("gpsimd", [hv])
    hst = p.dma("gpsimd", d["d_halo_in"], halo[:])
    p.waits("gpsimd", [hst])
    cc1 = p.collective("AllGather", ALU.bypass, [[0, 1], [2, 3], [4, 5], [6, 7]],
                       [d["d_halo_in"].opt()], [d["d_halo_out"].opt()])
    p.waits("sync", [cc1])
    hld = p.dma("sync", halo[:], d["d_halo_out"][0:128, :])
    p.waits("vector", [hld])
    htok = p.vop("tensor_scalar", out=halo[:], in0=halo[:], scalar1=mask128[:],
                 scalar2=None, op0=ALU.mult)
    p.vop("memset", state[:], 0.0)
    sz = p.vop("memset", state_b[:], 0.0)
    p.vfence()

    prev_z_st = [None, None]
    y_st = [None, None]
    for k in range(NCHUNK):
        p.barrier()
        load_xc_fm(k)
        lnt = ln_chunk(k, ln3, ncols=128)
        p.waits("tensor", [lnt])
        for oz in range(16):
            pkx = pk0 if oz % 2 == 0 else pk1
            mt = None
            for kc in range(DMC):
                mt = p.mm(pkx[:, 0:128],
                          winb[:, kc * PROJ + oz * 128: kc * PROJ + (oz + 1) * 128],
                          ln3[:, kc * 128:(kc + 1) * 128],
                          start=(kc == 0), stop=(kc == DMC - 1))
            p.waits("vector", [mt] + ([prev_z_st[0]] if oz == 0 else []))
            zv = p.vop("tensor_copy", out=zstg[:, oz * 128:(oz + 1) * 128],
                       in_=pkx[:, 0:128])
            p.waits("tensor", [zv])
        p.waits("gpsimd", [zv])
        zst = p.dma("gpsimd", d["d_z"][k], zstg[:])
        prev_z_st[0] = zst
        xv = p.vop("tensor_copy",
                   out=xbc[:].rearrange("p (s w) -> p s w", w=131)[:, :, 0:3],
                   in_=(halo[:].rearrange("p (s w) -> p s w", w=3) if k == 0 else
                        xbc[:].rearrange("p (s w) -> p s w", w=131)[:, :, 128:131]))
        p.vfence()
        p.waits("tensor", [xv, htok])
        for s in range(17):
            pkx = pk0 if s % 2 == 0 else pk1
            mt = None
            for kc in range(DMC):
                mt = p.mm(pkx[:, 0:128],
                          winb[:, kc * PROJ + D_INNER + s * 128:
                               kc * PROJ + D_INNER + (s + 1) * 128],
                          ln3[:, kc * 128:(kc + 1) * 128],
                          start=(kc == 0), stop=(kc == DMC - 1))
            p.waits("vector", [mt])
            xv = p.vop("tensor_copy", out=xbc[:, s * 131 + 3:(s + 1) * 131],
                       in_=pkx[:, 0:128])
            p.waits("tensor", [xv])
        mt = None
        for kc in range(DMC):
            mt = p.mm(pk2[0:32, 0:128],
                      winb[:, kc * PROJ + D_INNER + XBC: kc * PROJ + PROJ],
                      ln3[:, kc * 128:(kc + 1) * 128],
                      start=(kc == 0), stop=(kc == DMC - 1))
        p.waits("vector", [xv, mt])
        for s in range(17):
            p.vop("tensor_scalar", out=cnv[:, s * 128:(s + 1) * 128],
                  in0=xbc[:, s * 131 + 0: s * 131 + 128],
                  scalar1=wcol[:, s * 4: s * 4 + 1], scalar2=None, op0=ALU.mult)
        p.vfence()
        for tap in range(1, 4):
            for s in range(17):
                p.vop("scalar_tensor_tensor", out=cnv[:, s * 128:(s + 1) * 128],
                      in0=xbc[:, s * 131 + tap: s * 131 + tap + 128],
                      scalar=wcol[:, s * 4 + tap: s * 4 + tap + 1],
                      in1=cnv[:, s * 128:(s + 1) * 128],
                      op0=ALU.mult, op1=ALU.add)
            p.vfence()
        cvt = p.tok("V")
        dv = p.vop("tensor_scalar", out=dtr[:], in0=pk2[0:32, 0:128],
                   scalar1=dtbc[:], scalar2=None, op0=ALU.add)
        p.vfence()
        p.waits("scalar", [p.tok("V")])
        p.sop("activation", out=dt_c[:], in_=dtr[:], func=AF.Abs)
        p.sop("activation", out=acl[:], in_=dt_c[:], func=AF.Exp, scale=-1.0)
        sv = p.sop("activation", out=acl[:], in_=acl[:], func=AF.Ln, bias=1.0)
        p.waits("vector", [sv])
        p.vop("tensor_scalar_max", out=dtr[:], in0=dtr[:], scalar1=0.0)
        p.vfence()
        dtk = p.vop("tensor_tensor", out=dt_c[:], in0=dtr[:], in1=acl[:],
                    op=ALU.add)
        p.vfence()
        sc = p.vop("tensor_tensor_scan", out=cumg[:, k * 128:(k + 1) * 128],
                   data0=ones32f[:], data1=dt_c[:],
                   initial=(0.0 if k == 0 else cumg[:, k * 128 - 1:k * 128]),
                   op0=ALU.mult, op1=ALU.add)
        p.vfence()
        if k == 0:
            av = p.vop("tensor_scalar", out=acl[:], in0=cumg[:, 0:128],
                       scalar1=Acol[:], scalar2=None, op0=ALU.mult)
        else:
            av = p.vop("tensor_scalar", out=acl[:], in0=cumg[:, k * 128:(k + 1) * 128],
                       scalar1=cumg[:, k * 128 - 1:k * 128], scalar2=Acol[:],
                       op0=ALU.subtract, op1=ALU.mult)
        p.vfence()
        p.waits("scalar", [cvt, p.tok("V")])
        svt = None
        for s in range(17):
            if s < 16:
                svt = p.sop("activation", out=xcv[:, s * 128:(s + 1) * 128],
                            in_=cnv[:, s * 128:(s + 1) * 128], func=AF.Silu,
                            bias=bcol[:, s:s + 1])
            else:
                svt = p.sop("activation", out=cnv[:, s * 128:(s + 1) * 128],
                            in_=cnv[:, s * 128:(s + 1) * 128], func=AF.Silu,
                            bias=bcol[:, s:s + 1])
        p.waits("vector", [svt])
        bv = p.vop("tensor_copy", out=B_c[:], in_=cnv[0:64, 16 * 128:17 * 128])
        cv = p.vop("tensor_copy", out=C_sb[:, k * 128:(k + 1) * 128],
                   in_=cnv[64:128, 16 * 128:17 * 128])
        p.vop("tensor_copy", out=ahl[0][:], in_=acl[:])
        p.vfence()
        p.vop("tensor_tensor", out=asc[0][:], in0=acl[:], in1=ahl[0][:],
              op=ALU.subtract)
        p.vfence()
        p.vop("tensor_copy", out=ahl[1][:], in_=asc[0][:])
        p.vop("tensor_scalar", out=asc[1][:], in0=acl[:],
              scalar1=acl[:, 127:128], scalar2=-1.0,
              op0=ALU.subtract, op1=ALU.mult)
        p.vfence()
        p.waits("scalar", [p.tok("V")])
        se = p.sop("activation", out=asc[1][:], in_=asc[1][:], func=AF.Exp)
        se2 = p.sop("activation", out=asc[2][:], in_=acl[:], func=AF.Exp)
        p.waits("vector", [se, se2])
        fcb = p.vop("tensor_copy", out=ahl[2][:], in_=asc[2][:])
        lv = p.vop("tensor_scalar", out=lamr[:], in0=selhd[:],
                   scalar1=asc[2][:, 127:128], scalar2=None, op0=ALU.mult)
        rv = p.vop("reciprocal", out=dtr[:], in_=dt_c[:])
        p.vfence()
        rv2 = p.vop("tensor_scalar", out=dtr[:], in0=dtr[:], scalar1=Dcol[:],
                    scalar2=None, op0=ALU.mult)
        p.vfence()
        p.waits("tensor", [p.tok("V"), p.tok("S")])
        tt1 = p.tr(pk0[:, 0:32], acl[:], idf[0:32, 0:32])
        tt2 = p.tr(pk1[:, 0:32], dtr[:], idf[0:32, 0:32])
        p.waits("vector", [tt2])
        p.vop("tensor_copy", out=colT[:, 0:32], in_=pk0[:, 0:32])
        e1 = p.vop("tensor_copy", out=colT[:, 32:64], in_=pk1[:, 0:32])
        p.waits("tensor", [e1])
        tt3 = p.tr(pk0[:, 0:32], asc[1][:], idf[0:32, 0:32])
        tt4 = p.tr(pk1[:].bitcast(BF16)[:, 0:64], B_c[:], idb[0:64, 0:64])
        p.waits("vector", [tt4])
        p.vop("tensor_copy", out=colT[:, 64:96], in_=pk0[:, 0:32])
        e2 = p.vop("tensor_copy", out=BT[:], in_=pk1[:].bitcast(BF16)[:, 0:64])
        p.waits("tensor", [e2])
        tt5 = p.tr(pk0[:, 0:32], dt_c[:], idf[0:32, 0:32])
        p.waits("vector", [tt5])
        e2b = p.vop("tensor_copy", out=colT[:, 96:128], in_=pk0[:, 0:32])
        p.waits("tensor", [e2b])
        for bb in range(2):
            pwx = pw0 if bb == 0 else pw1
            pwb = pwx[:].bitcast(BF16)
            ttl = None
            for cc2 in range(8):
                ttl = p.tr(pwb[:, cc2 * 128:(cc2 + 1) * 128],
                           xcv[:, (bb * 8 + cc2) * 128:(bb * 8 + cc2 + 1) * 128],
                           idb[:])
            p.waits("vector", [ttl])
            e3 = p.vop("tensor_copy", out=xcvT[:, bb * 1024:(bb + 1) * 1024],
                       in_=pwb[:, 0:1024])
        p.vfence()
        xt = p.vop("tensor_tensor",
                   out=xtid[:].rearrange("p (h dd) -> p h dd", dd=64),
                   in0=xcvT[:].rearrange("p (h dd) -> p h dd", dd=64),
                   in1=colT[:, 96:128]
                   .rearrange("p (h o) -> p h o", o=1).broadcast_to([128, 32, 64]),
                   op=ALU.mult)
        p.vfence()
        p.waits("tensor", [bv, cv, p.tok("V")])
        m0 = p.mm(pk2[:, 0:128], B_c[:], C_sb[:, k * 128:(k + 1) * 128])
        p.waits("vector", [m0])
        m0e = p.vop("tensor_tensor", out=M0LT[:], in0=pk2[:, 0:128], in1=lt01[:],
                    op=ALU.mult)
        p.vfence()
        f1 = f2 = f3 = p.tok("V")
        ps_d = [pk0, pk1]
        dex_tok = [None, None]
        mh_tok = [None, None]
        cdd_tok = [None, None]
        bpt_tok = [None, None]
        yev = None
        dlast = None
        for hg in range(4):
            for hh in range(8):
                h = hg * 8 + hh
                i2 = h % 2
                pD = ps_d[i2]
                p.waits("tensor", [f1, f2, dex_tok[i2]])
                p.mm(pD[:, 0:128], sel32[:, h * 128:(h + 1) * 128], ahl[0][:],
                     start=True, stop=False)
                bm2 = p.mm(pD[:, 0:128], sel32[:, h * 128:(h + 1) * 128],
                           ahl[1][:], start=False, stop=True)
                p.waits("vector", [bm2, mh_tok[i2]])
                p.vop("tensor_scalar", out=Dtmp[i2][:], in0=pD[:, 0:128],
                      scalar1=colT[:, h:h + 1], scalar2=None, op0=ALU.subtract)
                p.vfence()
                dv2 = p.vop("tensor_scalar_min", out=Dtmp[i2][:],
                            in0=Dtmp[i2][:], scalar1=0.0)
                p.vfence()
                dex_tok[i2] = dv2
                p.waits("scalar", [dv2])
                sx = p.sop("activation", out=Dtmp[i2][:], in_=Dtmp[i2][:],
                           func=AF.Exp)
                p.waits("vector", [sx, m0e])
                p.vop("tensor_tensor", out=Mh[i2][:], in0=Dtmp[i2][:],
                      in1=M0LT[:], op=ALU.mult)
                p.vfence()
                mh_tok[i2] = p.vop("scalar_tensor_tensor", out=Mh[i2][:],
                                   in0=idb[:], scalar=colT[:, 32 + h:33 + h],
                                   in1=Mh[i2][:], op0=ALU.mult, op1=ALU.add)
                p.vfence()
                p.waits("tensor", [mh_tok[i2], xt, yev])
                p.mm(pw0[:, hh * 64:(hh + 1) * 64], Mh[i2][:],
                     xtid[:, h * 64:(h + 1) * 64], start=True, stop=False)
                p.waits("tensor", [f3, cdd_tok[0], cdd_tok[1], p.tok("V")])
                cb = p.mm(pk3[0:64, 0:128], sel32[:, h * 128:h * 128 + 64],
                          ahl[2][:])
                p.waits("vector", [cb])
                cdd_tok[i2] = p.vop("tensor_tensor", out=Cdd[i2][:],
                                    in0=pk3[0:64, 0:128],
                                    in1=C_sb[:, k * 128:(k + 1) * 128],
                                    op=ALU.mult)
                p.vfence()
                p.waits("tensor", [cdd_tok[i2], sz])
                ym = p.mm(pw0[:, hh * 64:(hh + 1) * 64], Cdd[i2][:],
                          state_b[:, h * 64:(h + 1) * 64], start=False, stop=True)
                p.waits("vector", [e2, bpt_tok[i2]])
                bp = p.vop("tensor_scalar", out=Bpt[i2][:], in0=BT[:],
                           scalar1=colT[:, 64 + h:65 + h], scalar2=None,
                           op0=ALU.mult)
                p.vfence()
                p.waits("tensor", [bp, dlast])
                bpt_tok[i2] = dm = p.mm(pw1[0:64, hh * 64:(hh + 1) * 64],
                                        Bpt[i2][:], xtid[:, h * 64:(h + 1) * 64])
            p.waits("vector", [ym, y_st[hg % 2]])
            yev = p.vop("tensor_copy", out=ystg[hg % 2][:], in_=pw0[:, 0:512])
            p.waits("gpsimd", [yev])
            y_st[hg % 2] = p.dma("gpsimd", d["d_y"][k][:, hg * 512:(hg + 1) * 512],
                                 ystg[hg % 2][:])
            p.waits("tensor", [yev, lv])
            lb = p.mm(pk3[0:64, 0:512], o3264[:],
                      lamr[:, hg * 512:(hg + 1) * 512])
            p.waits("vector", [lb, dm])
            p.vop("tensor_tensor", out=sttmp[:], in0=state[:, hg * 512:(hg + 1) * 512],
                  in1=pk3[0:64, 0:512], op=ALU.mult)
            p.vfence()
            dlast = p.vop("tensor_tensor", out=state[:, hg * 512:(hg + 1) * 512],
                          in0=sttmp[:], in1=pw1[0:64, 0:512], op=ALU.add)
            p.vfence()
            sz = p.vop("tensor_copy", out=state_b[:, hg * 512:(hg + 1) * 512],
                       in_=state[:, hg * 512:(hg + 1) * 512])
            p.vfence()
    p.barrier()
    sst = p.dma("gpsimd", d["d_state_in"], state[:])
    p.waits("gpsimd", [sst])
    cc2 = p.collective("AllGather", ALU.bypass, [[0, 1], [2, 3], [4, 5], [6, 7]],
                       [d["d_state_in"].opt()], [d["d_state_out"].opt()])
    p.waits("sync", [cc2])
    hld2 = p.dma("sync", hin[:], d["d_state_out"][0:64, :])
    p.waits("vector", [hld2])
    p.vop("tensor_scalar", out=hin[:], in0=hin[:], scalar1=mask64[:],
          scalar2=None, op0=ALU.mult)
    p.vfence()
    hintok = p.vop("tensor_copy", out=state_b[:], in_=hin[:])
    p.vfence()
    cast_tok = None
    for kc in range(16):
        lt = p.dma("sync", stage[:, 0:1024],
                   d["L0_attn_W_out"][kc * 128:(kc + 1) * 128, :])
        p.waits("vector", [lt, cast_tok])
        cast_tok = p.vop("tensor_copy", out=winb[:, kc * 1024:(kc + 1) * 1024],
                         in_=stage[:, 0:1024])
        p.waits("sync", [cast_tok])
    wout_tok = cast_tok
    p.barrier()
    for k in range(NCHUNK):
        p.barrier()
        av = p.vop("tensor_scalar", out=acl[:], in0=cumg[:, k * 128:(k + 1) * 128],
                   scalar1=Acol[:], scalar2=None, op0=ALU.mult)
        p.vfence()
        p.waits("scalar", [av])
        se = p.sop("activation", out=asc[2][:], in_=acl[:], func=AF.Exp)
        p.waits("vector", [se])
        f3 = fcb = p.vop("tensor_copy", out=ahl[2][:], in_=asc[2][:])
        yld = p.dma("gpsimd", ytot[:], d["d_y"][k])
        zld = p.dma("gpsimd", zstg[:], d["d_z"][k])
        cdd_tok = [None, None]
        for h in range(N_HEADS):
            i2 = h % 2
            p.waits("tensor", [f3, cdd_tok[0], cdd_tok[1], p.tok("V")])
            cb = p.mm(pk3[0:64, 0:128], sel32[:, h * 128:h * 128 + 64],
                      ahl[2][:])
            p.waits("vector", [cb])
            cdd_tok[i2] = p.vop("tensor_tensor", out=Cdd[i2][:],
                                in0=pk3[0:64, 0:128],
                                in1=C_sb[:, k * 128:(k + 1) * 128], op=ALU.mult)
            p.vfence()
            p.waits("tensor", [cdd_tok[i2], hintok])
            pwx = pw0 if h < 16 else pw1
            cm = p.mm(pwx[:, (h % 16) * 64:((h % 16) + 1) * 64], Cdd[i2][:],
                      state_b[:, h * 64:(h + 1) * 64])
        p.waits("vector", [cm, yld])
        for half in range(2):
            pwx = pw0 if half == 0 else pw1
            p.vop("tensor_tensor", out=ytot[:, half * 1024:(half + 1) * 1024],
                  in0=ytot[:, half * 1024:(half + 1) * 1024],
                  in1=pwx[:, 0:1024], op=ALU.add)
        p.vfence()
        yt2 = p.tok("V")
        p.waits("tensor", [yt2])
        for bb in range(2):
            pwx = pw0 if bb == 0 else pw1
            pwb = pwx[:].bitcast(BF16)
            ttl = None
            for cc3 in range(8):
                ttl = p.tr(pwb[:, cc3 * 128:(cc3 + 1) * 128],
                           ytot[:, (bb * 8 + cc3) * 128:(bb * 8 + cc3 + 1) * 128],
                           idb[:])
            p.waits("vector", [ttl])
            p.vop("tensor_copy", out=yfm[:, bb * 1024:(bb + 1) * 1024],
                  in_=pwb[:, 0:1024])
        p.vfence()
        p.waits("scalar", [zld, p.tok("V")])
        szt = p.sop("activation", out=zstg[:], in_=zstg[:], func=AF.Silu)
        p.waits("vector", [szt])
        gt = p.vop("tensor_tensor", out=gf[:], in0=yfm[:], in1=zstg[:],
                   op=ALU.mult)
        p.vfence()
        p.waits("scalar", [gt])
        g2t = p.sop("activation", out=ytot[:], in_=gf[:], func=AF.Square)
        p.waits("tensor", [g2t])
        mt = None
        for c in range(16):
            mt = p.mm(pk2[0:1, 0:128], onescb[:], ytot[:, c * 128:(c + 1) * 128],
                      start=(c == 0), stop=(c == 15))
        p.waits("vector", [mt])
        p.vop("tensor_scalar", out=srow[:], in0=pk2[0:1, 0:128],
              scalar1=1.0 / D_INNER, scalar2=EPS, op0=ALU.mult, op1=ALU.add)
        p.vfence()
        rv = p.vop("reciprocal", out=srow2[:], in_=srow[:])
        p.vfence()
        p.waits("scalar", [rv])
        s2 = p.sop("activation", out=rrb[:], in_=srow2[:], func=AF.Sqrt)
        p.waits("tensor", [s2])
        bm = p.mm(pk3[:, 0:128], ones1b[:], rrb[:])
        p.waits("vector", [bm])
        yn = None
        for c in range(16):
            yn = p.vop("scalar_tensor_tensor", out=ytot[:, c * 128:(c + 1) * 128],
                       in0=gf[:, c * 128:(c + 1) * 128],
                       scalar=normw[:, c:c + 1], in1=pk3[:, 0:128],
                       op0=ALU.mult, op1=ALU.mult)
        p.vfence()
        xctok = load_xc_fm(k)
        p.waits("tensor", [yn, wout_tok])
        ev_rot = [None, None]
        for oc in range(DMC):
            i2 = oc % 2
            pkx = pk0 if i2 == 0 else pk1
            p.waits("tensor", [ev_rot[i2], xctok])
            mt = None
            for kc in range(16):
                mt = p.mm(pkx[:, 0:128],
                          winb[:, kc * 1024 + oc * 128: kc * 1024 + (oc + 1) * 128],
                          ytot[:, kc * 128:(kc + 1) * 128],
                          start=(kc == 0), stop=(kc == 15))
            p.waits("vector", [mt])
            ev_rot[i2] = p.vop("tensor_tensor",
                               out=xc[:, oc * 128:(oc + 1) * 128],
                               in0=xc[:, oc * 128:(oc + 1) * 128],
                               in1=pkx[:, 0:128], op=ALU.add)
        p.waits("tensor", [ev_rot[0], ev_rot[1]])
        for c in range(DMC):
            p.tr(pk0[:, 0:128] if c % 2 == 0 else pk1[:, 0:128],
                 xc[:, c * 128:(c + 1) * 128], idf[:])
            tt = p.trfence(cx.sb["c_trf2"], cx.sb["c_ones_1x128b"])
            p.waits("vector", [tt])
            ev = p.vop("tensor_copy", out=xtm_stg[:, c * 128:(c + 1) * 128],
                       in_=(pk0 if c % 2 == 0 else pk1)[:, 0:128])
            p.waits("tensor", [ev])
        nv = p.vop("engine_nop")
        p.waits("gpsimd", [nv])
        xst = p.dma("gpsimd", xfm[k * 128:(k + 1) * 128, :], xtm_stg[:])
    p.waits("gpsimd", [xst])
    return xst


def build_mamba_program():
    nc = bass.Bass("TRN2", target_bir_lowering=False)
    es = ExitStack()
    p = Prog(nc, es)
    cx = Ctx(nc, p, es)
    d = cx.dram
    d["x"] = nc.dram_tensor("x", [T, DM], F32, kind="ExternalInput").ap()
    mamba_dram_decls(nc, d)
    for nm, arr in host_constants().items():
        d[nm] = nc.dram_tensor(nm, list(arr.shape),
                               BF16 if arr.dtype == BF else F32,
                               kind="ExternalInput").ap()
    d["c_mask64"] = nc.dram_tensor("c_mask64", [64, 1], F32,
                                   kind="ExternalInput").ap()
    d["c_mask128"] = nc.dram_tensor("c_mask128", [128, 1], F32,
                                    kind="ExternalInput").ap()
    d["x_out"] = nc.dram_tensor("x_out", [T, DM], F32, kind="ExternalOutput").ap()
    with es:
        pk0 = cx.psum("pk0", [128, 512])
        pk1 = cx.psum("pk1", [128, 512])
        pk2 = cx.psum("pk2", [128, 512])
        pk3 = cx.psum("pk3", [128, 512])
        pw0 = cx.psum("pw0", [128, 1024])
        pw1 = cx.psum("pw1", [128, 1024])
        cx.sb["c_trf"] = pk0
        ctok = load_common_consts(cx, needed=[
            "c_id128f", "c_id128b", "c_ones_1x128b", "c_onescol_b",
            "c_selhd", "c_lt01", "c_ones32x64b", "c_ones32x128f",
            "c_sel32", "c_mask64", "c_mask128"])
        p.waits("tensor", [ctok])
        cx.sb["c_trf"] = pk0
        emit_tm_copy(cx, d["x"], d["d_xfm"])
        p.barrier()
        emit_mamba(cx, pk0, pk1, pk2, pk3, pw0, pw1)
        p.barrier()
        stok = emit_tm_copy(cx, d["d_xfm"], d["x_out"])
        p.waits("gpsimd", [stok])
        p.finalize()
    return nc


def emit_tm_copy(cx, src_tm, dst_tm, qname="gpsimd"):
    p = cx.p
    stg = [cx.sget(f"xio_stg{i}", [128, DM], F32) for i in range(2)]
    ld = [None, None]
    st = [None, None]
    gtok = None
    for tc in range(T // 128):
        i = tc % 2
        p.waits(qname, [st[i]])
        ld[i] = p.dma(qname, stg[i][:], src_tm[tc * 128:(tc + 1) * 128, :])
        p.waits(qname, [ld[i]])
        st[i] = gtok = p.dma(qname, dst_tm[tc * 128:(tc + 1) * 128, :], stg[i][:])
    return gtok


def emit_x_tm_to_dram_fm(cx, x_tm_dram, d_xfm, pw0, pw1):
    p = cx.p
    stg = [cx.sget(f"xio_stg{i}", [128, DM], F32) for i in range(2)]
    stg2 = [cx.sget(f"xio2_stg{i}", [128, DM], F32) for i in range(2)]
    idf = cx.sb["c_id128f"]
    pws = [pw0, pw1]
    ev_tok = [None, None]
    ld_tok = [None, None]
    tr_tok = [None, None]
    st_tok = [None, None]
    gtok = None
    for tc in range(T // 128):
        i = tc % 2
        p.waits("sync", [tr_tok[i]])
        ld_tok[i] = p.dma("sync", stg[i][:], x_tm_dram[tc * 128:(tc + 1) * 128, :])
        p.waits("tensor", [ld_tok[i], ev_tok[i]])
        for c in range(DMC):
            tr_tok[i] = p.tr(pws[i][:, c * 128:(c + 1) * 128],
                             stg[i][:, c * 128:(c + 1) * 128], idf[:])
        tr_tok[i] = p.trfence(cx.sb["c_trf"], cx.sb["c_ones_1x128b"])
        p.waits("vector", [tr_tok[i], st_tok[i]])
        p.vop("tensor_copy", out=stg2[i][:], in_=pws[i][:])
        ev_tok[i] = p.vop("engine_nop")
        p.waits("gpsimd", [ev_tok[i]])
        st_tok[i] = gtok = p.dma(
            "gpsimd", d_xfm[:, :, tc * 128:(tc + 1) * 128],
            stg2[i][:].rearrange("p (c t) -> p c t", t=128))
    return gtok


def emit_dram_fm_to_x_tm(cx, d_xfm, x_tm_dram, pw0, pw1, paranoid=False):
    p = cx.p
    stg = [cx.sget(f"xio_stg{i}", [128, DM], F32) for i in range(2)]
    stg2 = [cx.sget(f"xio2_stg{i}", [128, DM], F32) for i in range(2)]
    idf = cx.sb["c_id128f"]
    pws = [pw0, pw1]
    ev_tok = [None, None]
    ld_tok = [None, None]
    tr_tok = [None, None]
    st_tok = [None, None]
    gtok = None
    for tc in range(T // 128):
        i = tc % 2
        if paranoid:
            p.barrier()
        p.waits("gpsimd", [tr_tok[i]])
        ld_tok[i] = p.dma("gpsimd",
                          stg[i][:].rearrange("p (c t) -> p c t", t=128),
                          d_xfm[:, :, tc * 128:(tc + 1) * 128])
        p.waits("tensor", [ld_tok[i], ev_tok[i]])
        for c in range(DMC):
            tr_tok[i] = p.tr(pws[i][:, c * 128:(c + 1) * 128],
                             stg[i][:, c * 128:(c + 1) * 128], idf[:])
        tr_tok[i] = p.trfence(cx.sb["c_trf"], cx.sb["c_ones_1x128b"])
        p.waits("vector", [tr_tok[i], st_tok[i]])
        p.vop("tensor_copy", out=stg2[i][:], in_=pws[i][:])
        ev_tok[i] = p.vop("engine_nop")
        p.waits("gpsimd", [ev_tok[i]])
        st_tok[i] = gtok = p.dma("gpsimd", x_tm_dram[tc * 128:(tc + 1) * 128, :],
                                 stg2[i][:])
    return gtok



_PROGRAMS = {}


def _program_input_names(nc):
    names = set()
    for alloc in nc.m.functions[0].allocations:
        if (isinstance(alloc, mybir.MemoryLocationSet)
                and alloc.kind == "ExternalInput"):
            nm = alloc.name
            if nm.endswith("_set"):
                nm = nm[:-4]
            names.add(nm)
    return names


def _get_program(name):
    if name not in _PROGRAMS:
        nc = {
            "moe": build_moe_program,
            "mamba": build_mamba_program,
            "l1ffn": build_l1ffn_program,
            "attn": build_attn_program,
            "fused": build_fused_program,
        }[name]()
        _PROGRAMS[name] = (nc, _program_input_names(nc))
    return _PROGRAMS[name]


FUSED = True


def kernel(**inputs):
    consts = host_constants()
    core_c = [core_constants(c) for c in range(NCORES)]
    x = np.ascontiguousarray(inputs["L0_attn_x"].astype(np.float32, copy=False))
    shards = [np.ascontiguousarray(x[c // 2, (c % 2) * T:(c % 2 + 1) * T])
              for c in range(NCORES)]

    moe_w = {k: np.ascontiguousarray(np.asarray(inputs[k], dtype=np.float32))
             for k in ("L0_ffn_Wg", "L0_ffn_bg", "L0_ffn_W1", "L0_ffn_b1",
                       "L0_ffn_W2", "L0_ffn_b2")}
    mamba_w = {k: np.ascontiguousarray(np.asarray(inputs[k], dtype=np.float32))
               for k in ("L0_attn_W_in", "L0_attn_conv_b", "L0_attn_A_log",
                         "L0_attn_D_param", "L0_attn_dt_bias", "L0_attn_norm_w",
                         "L0_attn_W_out")}
    mamba_w["L0_attn_conv_w"] = np.ascontiguousarray(
        np.asarray(inputs["L0_attn_conv_w"], dtype=np.float32).reshape(XBC, 4))
    attn_w = {k: np.ascontiguousarray(np.asarray(inputs[k], dtype=np.float32))
              for k in ("L1_attn_Wq", "L1_attn_Wk", "L1_attn_Wv", "L1_attn_Wo",
                        "L1_attn_bo")}
    l1_w = {k: np.ascontiguousarray(np.asarray(inputs[k], dtype=np.float32))
            for k in ("L1_ffn_W", "L1_ffn_b", "L1_ffn_lora_down",
                      "L1_ffn_lora_up", "L1_ffn_down_aux", "L1_ffn_up_aux")}

    def launch(name, wdict):
        nonlocal shards
        nc, innames = _get_program(name)
        in_maps = []
        for c in range(NCORES):
            m = {"x": shards[c]}
            m.update(wdict)
            m.update(consts)
            m.update(core_c[c])
            in_maps.append({k: v for k, v in m.items() if k in innames})
        res = run_bass_kernel_spmd(nc, in_maps, core_ids=list(range(NCORES)))
        shards = [np.asarray(res.results[c]["x_out"]) for c in range(NCORES)]

    if FUSED:
        allw = {}
        for wd in (moe_w, mamba_w, attn_w, l1_w):
            allw.update(wd)
        launch("fused", allw)
    else:
        launch("moe", moe_w)
        launch("mamba", mamba_w)
        launch("moe", moe_w)
        launch("l1ffn", l1_w)
        launch("attn", attn_w)
        launch("l1ffn", l1_w)

    out = np.empty((4, L_SEQ, DM), dtype=np.float32)
    for c in range(NCORES):
        out[c // 2, (c % 2) * T:(c % 2 + 1) * T] = shards[c]
    return out


def build_fused_program(phases=("moe", "mamba", "moe2", "l1ffn", "attn",
                                "l1ffn2"), debug_xfm=False):
    nc = bass.Bass("TRN2", target_bir_lowering=False)
    es = ExitStack()
    p = Prog(nc, es)
    cx = Ctx(nc, p, es)
    d = cx.dram
    d["x"] = nc.dram_tensor("x", [T, DM], F32, kind="ExternalInput").ap()
    moe_dram_decls(nc, d)
    mamba_dram_decls(nc, d)
    attn_dram_decls(nc, d)
    for nm, shp in [("L1_ffn_W", [DM, DM]), ("L1_ffn_b", [DM]),
                    ("L1_ffn_lora_down", [4, 100]), ("L1_ffn_lora_up", [50, 4]),
                    ("L1_ffn_down_aux", [100, DM]), ("L1_ffn_up_aux", [DM, 50])]:
        d[nm] = nc.dram_tensor(nm, shp, F32, kind="ExternalInput").ap()
    for nm, arr in host_constants().items():
        d[nm] = nc.dram_tensor(nm, list(arr.shape),
                               BF16 if arr.dtype == BF else F32,
                               kind="ExternalInput").ap()
    d["c_mask64"] = nc.dram_tensor("c_mask64", [64, 1], F32,
                                   kind="ExternalInput").ap()
    d["c_mask128"] = nc.dram_tensor("c_mask128", [128, 1], F32,
                                    kind="ExternalInput").ap()
    d["c_regsel2"] = d["c_regsel"] if "c_regsel" in d else None
    d["x_out"] = nc.dram_tensor("x_out", [T, DM], F32, kind="ExternalOutput").ap()
    if debug_xfm:
        d["x_fm_out"] = nc.dram_tensor("x_fm_out", [DMC, 128, T], F32,
                                       kind="ExternalOutput").ap()
        d["x_dbg1"] = nc.dram_tensor("x_dbg1", [DMC, 128, T], F32,
                                     kind="ExternalOutput").ap()
    d["d_rows"] = nc.dram_tensor("d_rows", [2, T], F32).ap()
    xfm = d["d_xfm"]
    with es:
        pk0 = cx.psum("pk0", [128, 512])
        pk1 = cx.psum("pk1", [128, 512])
        pk2 = cx.psum("pk2", [128, 512])
        pk3 = cx.psum("pk3", [128, 512])
        pw0 = cx.psum("pw0", [128, 1024])
        pw1 = cx.psum("pw1", [128, 1024])
        cx.sb["c_trf"] = pk0
        ctok = load_common_consts(cx, needed=[
            "c_id128f", "c_id128b", "c_ones_1x128b", "c_onescol_b",
            "c_ones_1x128f", "c_onescol_f"])
        p.waits("tensor", [ctok])
        p.waits("vector", [ctok])
        p.waits("scalar", [ctok])
        p.waits("gpsimd", [ctok])
        xr_hold = [None]
        final_store = [None]

        def load_xr_from(src):
            return emit_load_x_fm(cx, src, xr_hold[0], pw0, pw1)

        def store_xr(xr_, pre, dst):
            p.waits("tensor", list(pre))
            p.waits("vector", list(pre))
            return emit_store_x_fm(cx, xr_, dst, pw0, pw1)

        def phase_moe(load=None, store=None):
            cx.push_scope()
            ct = load_common_consts(cx, needed=["c_oneh8"])
            p.waits("tensor", [ct])
            xl = cx.sget("ph_xl", [128, DMC * T], BF16)
            negm = cx.sget("ph_negm", [128, 16], F32)
            rcol = cx.sget("ph_rcol", [128, 16], F32)
            if xr_hold[0] is None:
                xr_hold[0] = cx.sget("ph_xr", [128, DMC * T], F32)
            xr = xr_hold[0]
            if load is not None:
                load_xr_from(load)
                p.barrier()
            lntok, coltok = emit_ln_rows(cx, xr, xl, pk0, pk1, pk2, pk3,
                                         negm, rcol, d["d_rows"])
            p.barrier()
            ftok = emit_moe(cx, xr, xl, negm, rcol, pk0, pk1, pk2, pk3,
                            pw0, pw1, lntok, coltok)
            if store is not None:
                p.barrier()
                store_xr(xr, [ftok], store)
            p.barrier()
            cx.pop_scope()

        def phase_mamba():
            cx.push_scope()
            ct = load_common_consts(cx, needed=[
                "c_selhd", "c_lt01", "c_ones32x64b", "c_ones32x128f",
                "c_sel32", "c_mask64", "c_mask128"])
            p.waits("tensor", [ct])
            p.waits("vector", [ct])
            p.waits("scalar", [ct])
            emit_mamba(cx, pk0, pk1, pk2, pk3, pw0, pw1)
            p.barrier()
            cx.pop_scope()

        def phase_l1ffn(load=None, store=None):
            cx.push_scope()
            xr = xr_hold[0]
            xl = cx.sget("ph_xl", [128, DMC * T], BF16)
            weT = cx.sget("ph_weT", [128, 8 * DM], BF16)
            stage32 = cx.sget("ph_stage32", [128, 8 * DM], F32)
            halfb = cx.sget("ph_halfb", [128, 8], F32)
            btok = p.dma("sync", halfb[:],
                         d["L1_ffn_b"].rearrange("(c p) -> p c", p=128),
                         allow_slow_non_contiguous=True)
            p.waits("vector", [btok])
            p.vop("tensor_scalar", out=halfb[:], in0=halfb[:],
                  scalar1=0.5, scalar2=None, op0=ALU.mult)
            if load is not None:
                load_xr_from(load)
                p.barrier()
            wtok = emit_weff(cx, weT, stage32, pw0, pw1, pk0)
            p.barrier()
            lntok = emit_ln(cx, xr, xl, pk0, pk1, pk2, pk3)
            p.barrier()
            ftok = emit_l1ffn(cx, xr, xl, weT, halfb, pk0, pk1, lntok, wtok)
            if store is not None:
                p.barrier()
                final_store[0] = store_xr(xr, [ftok], store)
            p.barrier()
            cx.pop_scope()

        def phase_attn(load=None, store=None):
            cx.push_scope()
            ct = load_common_consts(cx, needed=["c_ones2"])
            p.waits("tensor", [ct])
            p.waits("vector", [ct])
            xr = xr_hold[0]
            xl = cx.sget("ph_xl", [128, DMC * T], BF16)
            if load is not None:
                load_xr_from(load)
                p.barrier()
            cx.push_scope()
            lntok = emit_ln(cx, xr, xl, pk0, pk1, pk2, pk3)
            p.barrier()
            cx.pop_scope()
            ftok = emit_attn(cx, xr, xl, 0, pk0, pk1, pk2, pk3, pw0, pw1,
                             lntok)
            if store is not None:
                p.barrier()
                store_xr(xr, [ftok], store)
            p.barrier()
            cx.pop_scope()

        def phase_xronly():
            cx.push_scope()
            if xr_hold[0] is None:
                xr_hold[0] = cx.sget("ph_xr", [128, DMC * T], F32)
            xr = xr_hold[0]
            load_xr_from(xfm)
            p.barrier()
            if "x_dbg1" in d:
                for c in range(DMC):
                    p.dma("gpsimd", d["x_dbg1"][c], xr[:, c * T:(c + 1) * T])
                p.barrier()
            store_xr(xr, [], xfm)
            p.barrier()
            cx.pop_scope()

        def phase_lnonly():
            cx.push_scope()
            if xr_hold[0] is None:
                xr_hold[0] = cx.sget("ph_xr", [128, DMC * T], F32)
            xr = xr_hold[0]
            xl = cx.sget("ph_xl", [128, DMC * T], BF16)
            negm = cx.sget("ph_negm", [128, 16], F32)
            rcol = cx.sget("ph_rcol", [128, 16], F32)
            load_xr_from(xfm)
            p.barrier()
            emit_ln_rows(cx, xr, xl, pk0, pk1, pk2, pk3, negm, rcol,
                         d["d_rows"])
            p.barrier()
            store_xr(xr, [], xfm)
            p.barrier()
            cx.pop_scope()

        chain_scope = [False]

        def open_chain():
            cx.push_scope()
            xr_hold[0] = cx.sget("chain_xr", [128, DMC * T], F32)
            chain_scope[0] = True

        def run_moe1():
            phase_moe(load=d["x"], store=xfm)
            xr_hold[0] = None

        def run_moe2():
            if not chain_scope[0]:
                open_chain()
            phase_moe(load=xfm, store=None)

        fns = {"moe": run_moe1,
               "moe2": run_moe2,
               "mamba": phase_mamba,
               "l1ffn": lambda: phase_l1ffn(load=None, store=None),
               "l1ffn2": lambda: phase_l1ffn(load=None, store=d["x_out"]),
               "attn": lambda: phase_attn(load=None, store=None),
               "xronly": phase_xronly, "lnonly": phase_lnonly}
        for ph in phases:
            fns[ph]()
            p.barrier()
        if chain_scope[0]:
            cx.pop_scope()
        if final_store[0] is None:
            cx.push_scope()
            stok = emit_tm_copy(cx, xfm, d["x_out"])
            p.waits("gpsimd", [stok])
            cx.pop_scope()
        else:
            p.waits("gpsimd", [final_store[0]])
        p.finalize()
    return nc
